# version 11
# baseline (speedup 1.0000x reference)
import contextlib
import numpy as np
import ml_dtypes
import concourse.bass as bass
import concourse.mybir as mybir
from concourse.bass_utils import run_bass_kernel_spmd

F32 = mybir.dt.float32
BF16 = mybir.dt.bfloat16
I32 = mybir.dt.int32
AF = mybir.ActivationFunctionType
ALU = mybir.AluOpType

C = 2048
T = 2048
NH = 16
NCH = 16
NQ = 1024
HC = 64
RMS_EPS = 1.1920929e-07
LN_EPS = 1e-5
FFN_EPS = 1e-8
SCALE = float(np.float32(1.0) / np.float32(np.sqrt(np.float32(128.0))))
SHUF = [i ^ 1 for i in range(32)]

OWN = {0: [0, 3, 4, 7, 8, 11, 12, 15], 1: [1, 2, 5, 6, 9, 10, 13, 14]}
PERM = {p: OWN[p] + sorted(set(range(16)) - set(OWN[p])) for p in (0, 1)}

def _profiles():
    na, nb = [0] * 16, [0] * 16
    for ks in range(16):
        for p in (0, 1):
            g = PERM[p][ks]
            ca = sum(1 for j in range(0, 4) if PERM[p][j] >= g)
            cb = sum(1 for j in range(4, 8) if PERM[p][j] >= g)
            na[ks] = max(na[ks], ca)
            nb[ks] = max(nb[ks], cb)
    return na, nb

NA_U, NB_U = _profiles()
PROFS = (NA_U, NB_U)


def _mask_layout():
    off, o = {}, 0
    for b_ in (0, 1):
        for ks in range(16):
            nU = PROFS[b_][ks]
            if nU == 0:
                continue
            worst = 0
            for p in (0, 1):
                g = PERM[p][ks]
                qs = [PERM[p][j] for j in range(b_ * 4, b_ * 4 + 4)]
                n_p = sum(1 for t in qs if t >= g)
                worst = max(worst, (nU - n_p) + (1 if g in qs else 0))
            if worst > 0:
                off[(ks, b_)] = (o, worst)
                o += worst
    return off, o


MASK_OFF, MASK_SLOTS = _mask_layout()
for _p in (0, 1):
    for _b in (0, 1):
        _qs = [PERM[_p][j] for j in range(_b * 4, _b * 4 + 4)]
        assert all(_qs[i] < _qs[i + 1] for i in range(3))

ENGINES = ("sync", "gpsimd", "vector", "scalar", "tensor")
SEMS = ("t", "v", "s", "din", "xb0", "xb1", "spl0", "spl1", "sx2", "sout", "g0", "g1", "gin", "gin0", "gin1", "gw", "sg0", "sg1", "sw0", "sw1")


class Prog:
    def __init__(self):
        self.ops = {e: [] for e in ENGINES}
        self.cnt = {n: 0 for n in SEMS}

    def emit(self, eng, fn):
        self.ops[eng].append(fn)

    def wait(self, eng, sem, val=None):
        v = self.cnt[sem] if val is None else val
        if v > 0:
            self.emit(eng, lambda E, S, _s=sem, _v=v: E.wait_ge(S[_s], _v))

    def waits(self, eng, rel):
        for sem, v in rel:
            self.wait(eng, sem, v)

    def do(self, eng, fn, inc=None, n=1):
        if eng == "vector" or eng == "scalar":
            sem = "v" if eng == "vector" else "s"
            prev = self.cnt[sem]
            self.cnt[sem] = prev + 1
            def wrapped(E, S, _f=fn, _s=sem, _p=prev):
                if _p > 0:
                    E.wait_ge(S[_s], _p)
                return _f(E, S).then_inc(S[_s], 1)
            self.emit(eng, wrapped)
            return
        if inc is not None:
            self.cnt[inc] += n
            self.emit(eng, lambda E, S, _f=fn, _i=inc, _n=n: _f(E, S).then_inc(S[_i], _n))
        else:
            self.emit(eng, lambda E, S, _f=fn: _f(E, S))

    def snap(self):
        return [(s, v) for s, v in self.cnt.items() if v > 0]

    def barrier(self, engs=ENGINES):
        snap = self.snap()
        for e in engs:
            self.waits(e, snap)


def build_nc(phase=6):
    nc = bass.Bass("TRN2", target_bir_lowering=False)
    dtn = nc.dram_tensor
    xT = dtn("xT", [C, T], F32, kind="ExternalInput").ap()
    xown = dtn("xown", [C, NQ], F32, kind="ExternalInput").ap()
    wqT = dtn("wqT", [NH, 128, C], F32, kind="ExternalInput").ap()
    wkT = dtn("wkT", [NH, 128, C], F32, kind="ExternalInput").ap()
    wvT = dtn("wvT", [NCH, 128, C], F32, kind="ExternalInput").ap()
    woT = dtn("woT", [NCH, 128, C], F32, kind="ExternalInput").ap()
    wgT = dtn("wgT", [HC, 128, C], F32, kind="ExternalInput").ap()
    wuT = dtn("wuT", [HC, 128, C], F32, kind="ExternalInput").ap()
    wdT = dtn("wdT", [NCH, 128, 8192], F32, kind="ExternalInput").ap()
    rkc = dtn("rkc", [128, T], F32, kind="ExternalInput").ap()
    rks = dtn("rks", [128, T], F32, kind="ExternalInput").ap()
    emaskD = dtn("emask", [128, max(MASK_SLOTS, 1) * 128], BF16, kind="ExternalInput").ap()
    identD = dtn("ident", [128, 128], F32, kind="ExternalInput").ap()
    out = dtn("out", [C, NQ], F32, kind="ExternalOutput").ap()
    aoscr = dtn("aoscr", [C, NQ], BF16, kind="Internal").ap()
    sgg = dtn("sgg", [HC, 128, C], BF16, kind="Internal").ap()
    sgu = dtn("sgu", [HC, 128, C], BF16, kind="Internal").ap()
    sgd = dtn("sgd", [NCH, 128, 8192], BF16, kind="Internal").ap()
    x2scr = dtn("x2scr", [C, NQ], F32, kind="Internal").ap()
    dbg = dtn("dbg", [C, T], F32, kind="ExternalOutput").ap() if phase < 6 else None

    P = Prog()
    ctx = contextlib.ExitStack()
    stacks = []
    with ctx:
        def sbt(st, nm, sh, d=F32):
            return st.enter_context(nc.sbuf_tensor(nm, sh, d))
        onescol = sbt(ctx, "onescol", [128, 1], BF16)
        onesrow = sbt(ctx, "onesrow", [1, 128], F32)
        ident = sbt(ctx, "ident_s", [128, 128], F32)
        rowA = sbt(ctx, "rowA", [1, T])
        rowB = sbt(ctx, "rowB", [1, T])
        invsbc = sbt(ctx, "invsbc", [128, NQ])
        epsT = sbt(ctx, "epsT", [128, 3])
        psA = ctx.enter_context(nc.psum_tensor("psA", [128, 512], F32))
        psB = ctx.enter_context(nc.psum_tensor("psB", [128, 512], F32))
        psC = ctx.enter_context(nc.psum_tensor("psC", [128, 512], F32))
        psD = ctx.enter_context(nc.psum_tensor("psD", [128, 512], F32))
        psE = ctx.enter_context(nc.psum_tensor("psE", [128, 512], F32))
        psF = ctx.enter_context(nc.psum_tensor("psF", [128, 512], F32))
        psS = ctx.enter_context(nc.psum_tensor("psS", [1, 1024], F32))
        PS4 = [psA, psB, psC, psD]
        sems = {n: ctx.enter_context(nc.semaphore("sem_" + n)) for n in SEMS}
        blk = ctx.enter_context(nc.Block())

        P.do("sync", lambda E, S: E.dma_start(out=ident[:], in_=identD), inc="din", n=16)
        P.do("vector", lambda E, S: E.memset(onescol[:], 1.0))
        P.do("vector", lambda E, S: E.memset(onesrow[:], 1.0))
        P.do("vector", lambda E, S: E.memset(epsT[:, 0:1], LN_EPS))
        P.do("vector", lambda E, S: E.memset(epsT[:, 1:2], RMS_EPS))
        P.do("vector", lambda E, S: E.memset(epsT[:, 2:3], FFN_EPS), inc="v")

        def ln_rows(ntok, epscol, s1ap, s2ap, rR, rB):
            for a_, c0, n in s1ap:
                P.do("vector", lambda E, S, a=a_, c=c0, n=n: E.tensor_scalar_mul(rB[0:1, c:c + n], a, 1.0 / C))
            for a_, c0, n in s2ap:
                P.do("vector", lambda E, S, a=a_, c=c0, n=n: E.tensor_scalar_mul(rR[0:1, c:c + n], a, 1.0 / C))
            for a_, c0, n in s1ap:
                P.do("vector", lambda E, S, a=a_, c=c0, n=n: E.tensor_mul(a, rB[0:1, c:c + n], rB[0:1, c:c + n]))
            for a_, c0, n in s1ap:
                P.do("vector", lambda E, S, a=a_, c=c0, n=n: E.tensor_sub(rR[0:1, c:c + n], rR[0:1, c:c + n], a), inc="v")
            P.wait("scalar", "v")
            P.do("scalar", lambda E, S: E.activation(rR[0:1, 0:ntok], rR[0:1, 0:ntok], AF.Sqrt, bias=epsT[0:1, epscol:epscol + 1]), inc="s")
            P.wait("vector", "s")
            P.do("vector", lambda E, S: E.reciprocal(rR[0:1, 0:ntok], rR[0:1, 0:ntok]))
            P.do("vector", lambda E, S: E.scalar_tensor_tensor(rB[0:1, 0:ntok], rB[0:1, 0:ntok], -1.0, rR[0:1, 0:ntok], ALU.mult, ALU.mult), inc="v")

        def bcast(rowap, ncols, pslist):
            nb = (ncols + 511) // 512
            for i in range(nb):
                n = min(512, ncols - i * 512)
                P.do("tensor", lambda E, S, p=pslist[i], c=i * 512, n=n, r=rowap:
                     E.matmul(p[:, 0:n], onesrow[:], r[0:1, c:c + n], start=True, stop=True),
                     inc="t" if i == nb - 1 else None)

        a1 = contextlib.ExitStack(); stacks.append(a1)
        LNx = sbt(a1, "LNx", [128, NCH * T], BF16)
        V = sbt(a1, "V", [128, NCH * T], BF16)
        a1b = contextlib.ExitStack(); stacks.append(a1b)
        xbuf = [sbt(a1b, f"xbuf{i}", [128, T]) for i in range(2)]
        xbf = [sbt(a1b, f"xbf{i}", [128, T], BF16) for i in range(1)]
        xsq = [sbt(a1b, f"xsq{i}", [128, T], BF16) for i in range(1)]
        RBs = sbt(a1b, "RBs", [128, 2 * T])

        s1rows = [(psA[0:1, :], 0, 512), (psB[0:1, :], 512, 512), (psC[0:1, :], 1024, 512), (psD[0:1, :], 1536, 512)]
        s2rows = [(psE[0:1, :], 0, 512), (psF[0:1, :], 512, 512), (psS[0:1, 0:512], 1024, 512), (psS[0:1, 512:1024], 1536, 512)]
        rel = [[], []]
        for j in range(NCH):
            bi = j % 2
            P.waits("sync", rel[bi])
            P.do("sync", lambda E, S, j=j, b=bi: E.dma_start(out=xbuf[b][:], in_=xT[j * 128:(j + 1) * 128, :]), inc=f"xb{bi}", n=16)
            P.wait("vector", f"xb{bi}")
            P.wait("vector", "t", P.cnt["t"])
            P.do("vector", lambda E, S, b=bi: E.tensor_copy(xbf[0][:], xbuf[b][:]), inc="v")
            P.wait("scalar", f"xb{bi}")
            P.wait("scalar", "t", P.cnt["t"])
            P.do("scalar", lambda E, S, b=bi: E.activation(xsq[0][:], xbuf[b][:], AF.Square), inc="s")
            P.wait("tensor", "v")
            P.wait("tensor", "s")
            for tb in range(4):
                P.do("tensor", lambda E, S, j=j, tb=tb, p=PS4[tb]:
                     E.matmul(p[0:1, :], onescol[:], xbf[0][:, tb * 512:(tb + 1) * 512], start=(j == 0), stop=(j == NCH - 1)))
            for tb in range(4):
                tgt = [psE[0:1, :], psF[0:1, :], psS[0:1, 0:512], psS[0:1, 512:1024]][tb]
                P.do("tensor", lambda E, S, j=j, tb=tb, tg=tgt:
                     E.matmul(tg, onescol[:], xsq[0][:, tb * 512:(tb + 1) * 512], start=(j == 0), stop=(j == NCH - 1)),
                     inc="t" if tb == 3 else None)
            rel[bi] = [("v", P.cnt["v"]), ("s", P.cnt["s"])]
        P.wait("vector", "t")
        ln_rows(T, 0, s1rows, s2rows, rowA, rowB)
        P.wait("tensor", "v")
        for which, row in ((0, rowA), (1, rowB)):
            bcast(row, T, PS4)
            P.wait("vector", "t")
            for tb in range(4):
                P.do("vector", lambda E, S, w=which, tb=tb, p=PS4[tb]:
                     E.tensor_copy(RBs[:, w * T + tb * 512:w * T + (tb + 1) * 512], p[:, :]), inc="v" if tb == 3 else None)
            P.wait("tensor", "v")
        rel = [P.snap(), P.snap()]
        for j in range(NCH):
            bi = j % 2
            P.waits("sync", rel[bi])
            P.do("sync", lambda E, S, j=j, b=bi: E.dma_start(out=xbuf[b][:], in_=xT[j * 128:(j + 1) * 128, :]), inc=f"xb{bi}", n=16)
            P.wait("vector", f"xb{bi}")
            P.do("vector", lambda E, S, b=bi: E.tensor_mul(xbuf[b][:], xbuf[b][:], RBs[:, 0:T]))
            P.do("vector", lambda E, S, j=j, b=bi: E.tensor_add(LNx[:, j * T:(j + 1) * T], xbuf[b][:], RBs[:, T:2 * T]), inc="v")
            rel[bi] = [("v", P.cnt["v"])]
        P.barrier()
        a1b.close()
        if phase == 1:
            for j in range(NCH):
                P.do("gpsimd", lambda E, S, j=j: E.dma_start(out=dbg[j * 128:(j + 1) * 128, :], in_=LNx[:, j * T:(j + 1) * T]), inc="gin", n=16)

        if phase >= 2:
            a2 = contextlib.ExitStack(); stacks.append(a2); stacks.append(a2)
            wvb = [sbt(a2, f"wvb{i}", [128, NCH * 512], BF16) for i in range(2)]
            relw = [[], []]
            relv4 = [[("v", P.cnt["v"])] for _ in range(4)]
            for fb in range(4):
                bi = fb % 2
                P.waits("gpsimd", relw[bi])
                for ci in range(NCH):
                    P.do("gpsimd", lambda E, S, fb=fb, ci=ci, b=bi:
                         E.dma_start(out=wvb[b][:, ci * 512:(ci + 1) * 512], in_=wvT[ci, :, fb * 512:(fb + 1) * 512]), inc=f"g{bi}", n=16)
                P.wait("tensor", f"g{bi}")
                for ti in range(NCH):
                    pst = PS4[ti % 4]
                    P.waits("tensor", relv4[ti % 4])
                    for ci in range(NCH):
                        P.do("tensor", lambda E, S, ti=ti, ci=ci, b=bi, p=pst:
                             E.matmul(p[:, :], LNx[:, ci * T + ti * 128:ci * T + (ti + 1) * 128], wvb[b][:, ci * 512:(ci + 1) * 512],
                                      start=(ci == 0), stop=(ci == NCH - 1)), inc="t" if ci == NCH - 1 else None)
                    P.wait("vector", "t")
                    P.do("vector", lambda E, S, ti=ti, fb=fb, p=pst:
                         E.tensor_copy(V[:, ti * T + fb * 512:ti * T + (fb + 1) * 512], p[:, :]), inc="v")
                    relv4[ti % 4] = [("v", P.cnt["v"])]
                relw[bi] = [("t", P.cnt["t"])]
            a2.close()
            P.barrier()
            if phase == 2:
                for ti in range(NCH):
                    P.do("gpsimd", lambda E, S, ti=ti: E.dma_start(out=dbg[ti * 128:(ti + 1) * 128, :], in_=V[:, ti * T:(ti + 1) * T]), inc="gin", n=16)

        if phase >= 3:
            a3 = contextlib.ExitStack(); stacks.append(a3)
            rkcS = sbt(a3, "rkcS", [128, T])
            rksS = sbt(a3, "rksS", [128, T])
            emask = sbt(a3, "emask_s", [128, max(MASK_SLOTS, 1) * 128], BF16)
            wkq = [sbt(a3, f"wkq{i}", [128, C], BF16) for i in range(2)]
            krot2 = [sbt(a3, f"krot{i}", [128, T], BF16) for i in range(2)]
            qhat2 = [sbt(a3, f"qhat{i}", [128, NQ], BF16) for i in range(2)]
            ksc2 = [sbt(a3, f"ksc{i}", [128, 16]) for i in range(2)]
            qrot = sbt(a3, "qrot", [128, NQ], BF16)
            sqt = sbt(a3, "sqt", [128, T], BF16)
            tmpf = sbt(a3, "tmpf", [128, 512], BF16)
            rSb = sbt(a3, "rSb", [128, NQ])
            tshuf = rSb
            ebuf = [sbt(a3, f"ebuf{i}", [128, 512], BF16) for i in range(2)]
            aout1 = [sbt(a3, "aout1_0", [128, NQ], BF16)] * 2
            wsg = [sbt(a3, f"wsg{i}", [128, 512], BF16) for i in range(2)]
            for src, dst in ((rkc, rkcS), (rks, rksS)):
                P.do("sync", lambda E, S, s_=src, d_=dst: E.dma_start(out=d_[:], in_=s_), inc="din", n=16)
            if MASK_SLOTS > 0:
                P.do("sync", lambda E, S: E.dma_start(out=emask[:], in_=emaskD), inc="din", n=16)
            P.wait("vector", "din")
            P.wait("scalar", "din")
            rel_ao = [[], []]
            rel_wkq = [[], []]
            LAST_KT = {0: max(kt for kt in range(16) if NA_U[kt] > 0),
                       1: max(kt for kt in range(16) if NB_U[kt] > 0)}
            ITERS = [(kt, b_, PROFS[b_][kt]) for kt in range(16) for b_ in (0, 1) if PROFS[b_][kt] > 0]

            def make_prep(h):
                par = h % 2
                krot, ksc, qhat = krot2[par], ksc2[par], qhat2[par]
                steps = []

                def s_loadk():
                    P.waits("gpsimd", rel_wkq[par])
                    P.do("gpsimd", lambda E, S, h=h, b=par: E.dma_start(out=wkq[b][:], in_=wkT[h]), inc=f"g{par}", n=16)
                steps.append(s_loadk)

                def mk_kprod(half):
                    def f():
                        P.wait("tensor", f"g{par}")
                        P.wait("tensor", "v", P.cnt["v"])
                        for ci in range(NCH):
                            for q_, p in ((0, psC), (1, psD)):
                                tb = 2 * half + q_
                                P.do("tensor", lambda E, S, ci=ci, tb=tb, b=par, p=p:
                                     E.matmul(p[:, :], wkq[b][:, ci * 128:(ci + 1) * 128],
                                              LNx[:, ci * T + tb * 512:ci * T + (tb + 1) * 512],
                                              start=(ci == 0), stop=(ci == NCH - 1)),
                                     inc="t" if (ci == NCH - 1 and q_ == 1) else None)
                    return f

                def mk_krope(half):
                    def f():
                        P.wait("vector", "t")
                        for q_, p in ((0, psC), (1, psD)):
                            tb = 2 * half + q_
                            sl = slice(tb * 512, (tb + 1) * 512)
                            P.do("vector", lambda E, S, p=p, sl=sl, kr=krot: E.tensor_mul(kr[:, sl], p[:, :], rkcS[:, sl]))
                            P.do("vector", lambda E, S, p=p: E.stream_shuffle(tshuf[:, 0:512], p[:, :], SHUF))
                            P.do("vector", lambda E, S, sl=sl: E.tensor_mul(tmpf[:, 0:512], tshuf[:, 0:512], rksS[:, sl]))
                            P.do("vector", lambda E, S, sl=sl, kr=krot: E.tensor_add(kr[:, sl], kr[:, sl], tmpf[:, 0:512]), inc="v")
                    return f

                for half in (0, 1):
                    steps.append(mk_kprod(half))
                    steps.append(mk_krope(half))

                def s_ksq():
                    P.wait("scalar", "v")
                    P.do("scalar", lambda E, S, kr=krot: E.activation(sqt[:, 0:T], kr[:, 0:T], AF.Square), inc="s")
                steps.append(s_ksq)

                def mk_ksum(r):
                    def f():
                        P.wait("tensor", "s")
                        P.wait("tensor", "v", P.cnt["v"])
                        for q_, p in ((0, psC), (1, psD)):
                            kt = 2 * r + q_
                            P.do("tensor", lambda E, S, kt=kt, p=p:
                                 E.matmul(p[:, 0:1], sqt[:, kt * 128:(kt + 1) * 128], onescol[:], start=True, stop=True),
                                 inc="t" if q_ == 1 else None)
                        P.wait("vector", "t")
                        for q_, p in ((0, psC), (1, psD)):
                            kt = 2 * r + q_
                            P.do("vector", lambda E, S, kt=kt, p=p, k=ksc: E.tensor_copy(k[:, kt:kt + 1], p[:, 0:1]), inc="v")
                    return f

                for r in range(8):
                    steps.append(mk_ksum(r))

                def s_kscrow():
                    P.wait("scalar", "v")
                    P.do("scalar", lambda E, S, k=ksc: E.activation(k[:, 0:16], k[:, 0:16], AF.Sqrt, bias=epsT[:, 1:2], scale=1.0 / 128.0), inc="s")
                    P.wait("vector", "s")
                    P.do("vector", lambda E, S, k=ksc: E.reciprocal(k[:, 0:16], k[:, 0:16]))
                    P.do("vector", lambda E, S, k=ksc: E.tensor_scalar_mul(k[:, 0:16], k[:, 0:16], SCALE), inc="v")
                    P.wait("gpsimd", "t", P.cnt["t"])
                    P.do("gpsimd", lambda E, S, h=h, b=1 - par: E.dma_start(out=wkq[b][:], in_=wqT[h]), inc=f"g{1 - par}", n=16)
                steps.append(s_kscrow)

                def s_qprod():
                    P.wait("tensor", f"g{1 - par}")
                    P.wait("tensor", "v", P.cnt["v"])
                    for ci in range(NCH):
                        for q_, p in ((0, psC), (1, psD)):
                            P.do("tensor", lambda E, S, ci=ci, tb=q_, b=1 - par, p=p:
                                 E.matmul(p[:, :], wkq[b][:, ci * 128:(ci + 1) * 128],
                                          LNx[:, ci * T + tb * 512:ci * T + (tb + 1) * 512],
                                          start=(ci == 0), stop=(ci == NCH - 1)),
                                 inc="t" if (ci == NCH - 1 and q_ == 1) else None)
                    rel_wkq[1 - par] = [("t", P.cnt["t"])]
                steps.append(s_qprod)

                def s_qrope():
                    P.wait("vector", "t")
                    for q_, p in ((0, psC), (1, psD)):
                        sl = slice(q_ * 512, (q_ + 1) * 512)
                        P.do("vector", lambda E, S, p=p, sl=sl: E.tensor_mul(qrot[:, sl], p[:, :], rkcS[:, sl]))
                        P.do("vector", lambda E, S, p=p: E.stream_shuffle(tshuf[:, 0:512], p[:, :], SHUF))
                        P.do("vector", lambda E, S, sl=sl: E.tensor_mul(tmpf[:, 0:512], tshuf[:, 0:512], rksS[:, sl]))
                        P.do("vector", lambda E, S, sl=sl: E.tensor_add(qrot[:, sl], qrot[:, sl], tmpf[:, 0:512]), inc="v")
                    P.wait("scalar", "v")
                    P.do("scalar", lambda E, S: E.activation(sqt[:, 0:NQ], qrot[:, 0:NQ], AF.Square), inc="s")
                steps.append(s_qrope)

                def s_qstats():
                    P.wait("tensor", "s")
                    P.wait("tensor", "v", P.cnt["v"])
                    for q_, p in ((0, psC), (1, psD)):
                        P.do("tensor", lambda E, S, tb=q_, p=p:
                             E.matmul(p[0:1, :], onescol[:], sqt[:, tb * 512:(tb + 1) * 512], start=True, stop=True),
                             inc="t" if q_ == 1 else None)
                    P.wait("vector", "t")
                    P.do("vector", lambda E, S: E.tensor_copy(rowB[0:1, 0:512], psC[0:1, :]))
                    P.do("vector", lambda E, S: E.tensor_copy(rowB[0:1, 512:1024], psD[0:1, :]), inc="v")
                    P.wait("scalar", "v")
                    P.do("scalar", lambda E, S: E.activation(rowB[0:1, 0:NQ], rowB[0:1, 0:NQ], AF.Sqrt, bias=epsT[0:1, 1:2], scale=1.0 / 128.0), inc="s")
                    P.wait("vector", "s")
                    P.do("vector", lambda E, S: E.reciprocal(rowB[0:1, 0:NQ], rowB[0:1, 0:NQ]), inc="v")
                steps.append(s_qstats)

                def s_qhat():
                    P.wait("tensor", "v")
                    P.do("tensor", lambda E, S: E.matmul(psC[:, :], onesrow[:], rowB[0:1, 0:512], start=True, stop=True))
                    P.do("tensor", lambda E, S: E.matmul(psD[:, :], onesrow[:], rowB[0:1, 512:1024], start=True, stop=True), inc="t")
                    P.wait("vector", "t")
                    for q_, p in ((0, psC), (1, psD)):
                        P.do("vector", lambda E, S, tb=q_, p=p, q=qhat:
                             E.tensor_mul(q[:, tb * 512:(tb + 1) * 512], qrot[:, tb * 512:(tb + 1) * 512], p[:, :]),
                             inc="v" if q_ == 1 else None)
                steps.append(s_qhat)
                return steps

            def make_epi(h):
                ao = aout1[h % 2]
                epi_rel = {}

                def e1():
                    P.wait("tensor", "v", P.cnt["v"])
                    P.do("tensor", lambda E, S: E.matmul(psC[:, :], onesrow[:], rowA[0:1, 0:512], start=True, stop=True))
                    P.do("tensor", lambda E, S: E.matmul(psD[:, :], onesrow[:], rowA[0:1, 512:1024], start=True, stop=True), inc="t")

                def e2():
                    P.wait("vector", "t")
                    P.waits("vector", rel_ao[0])
                    P.waits("vector", rel_ao[1])
                    for tb in range(2):
                        P.do("vector", lambda E, S, tb=tb, p=[psC, psD][tb]: E.reciprocal(rSb[:, tb * 512:(tb + 1) * 512], p[:, :]))
                        P.do("vector", lambda E, S, tb=tb, pv=[psE, psF][tb], a=ao:
                             E.tensor_mul(a[:, tb * 512:(tb + 1) * 512], pv[:, :], rSb[:, tb * 512:(tb + 1) * 512]), inc="v" if tb == 1 else None)
                    epi_rel["r"] = [("v", P.cnt["v"])]
                    P.wait("sync", "v")
                    P.do("sync", lambda E, S, h=h, a=ao: E.dma_start(out=aoscr[h * 128:(h + 1) * 128, :], in_=a[:]), inc=f"spl{h % 2}", n=16)
                    if phase == 3:
                        P.wait("gpsimd", "v")
                        P.do("gpsimd", lambda E, S, h=h, a=ao: E.dma_start(out=dbg[h * 128:(h + 1) * 128, 0:NQ], in_=a[:]), inc=f"gin{h % 2}", n=16)
                        rel_ao[h % 2] = [(f"spl{h % 2}", P.cnt[f"spl{h % 2}"]), (f"gin{h % 2}", P.cnt[f"gin{h % 2}"])]
                    else:
                        rel_ao[h % 2] = [(f"spl{h % 2}", P.cnt[f"spl{h % 2}"])]
                return [e1, e2], epi_rel

            sp_items = ([("g", hc, q) for hc in range(HC) for q in range(4)]
                        + [("u", hc, q) for hc in range(HC) for q in range(4)]
                        + [("d", fc, q) for fc in range(NCH) for q in range(16)])
            sp_state = {"i": 0, "rel": [[], []]}

            def sign_step(n=1):
                for _ in range(n):
                    i = sp_state["i"]
                    if i >= len(sp_items):
                        return
                    sp_state["i"] = i + 1
                    kind, idx, q = sp_items[i]
                    par = i % 2
                    src = {"g": wgT, "u": wuT, "d": wdT}[kind]
                    dst = {"g": sgg, "u": sgu, "d": sgd}[kind]
                    sap = src[idx, :, q * 512:(q + 1) * 512]
                    dap = dst[idx, :, q * 512:(q + 1) * 512]
                    P.waits("gpsimd", sp_state["rel"][par])
                    P.do("gpsimd", lambda E, S, sap=sap, b=par: E.dma_start(out=wsg[b][:], in_=sap), inc=f"sg{par}", n=16)
                    P.wait("scalar", f"sg{par}")
                    P.do("scalar", lambda E, S, b=par: E.sign(wsg[b][:], wsg[b][:]), inc="s")
                    P.wait("sync", "s")
                    P.do("sync", lambda E, S, dap=dap, b=par: E.dma_start(out=dap, in_=wsg[b][:]), inc=f"sw{par}", n=16)
                    sp_state["rel"][par] = [(f"sw{par}", P.cnt[f"sw{par}"])]

            for st_ in make_prep(0):
                st_()
            prep_rel = [("t", P.cnt["t"]), ("v", P.cnt["v"]), ("s", P.cnt["s"])]
            pend_epi = None

            for h in range(NH):
                par = h % 2
                krot, ksc, qhat = krot2[par], ksc2[par], qhat2[par]
                steps = list(pend_epi[0]) if pend_epi else []
                epi_rel_prev = pend_epi[1] if pend_epi else None
                steps += make_prep(h + 1) if h + 1 < NH else []
                si = 0
                P.waits("tensor", prep_rel)
                P.waits("scalar", prep_rel)
                P.wait("tensor", "s", P.cnt["s"])
                exp_done = []
                ready = []

                def emit_scores(i):
                    kt, b_, nU = ITERS[i]
                    qc0 = b_ * 512 + (4 - nU) * 128
                    ncols = nU * 128
                    psc = [psA, psB][i % 2]
                    eb = ebuf[i % 2]
                    if i >= 2:
                        P.wait("tensor", "s", exp_done[i - 2])
                    P.do("tensor", lambda E, S, kt=kt, qc0=qc0, n=ncols, p=psc, kr=krot, q=qhat:
                         E.matmul(p[:, 0:n], kr[:, kt * 128:(kt + 1) * 128], q[:, qc0:qc0 + n], start=True, stop=True), inc="t")
                    P.wait("scalar", "t")
                    P.do("scalar", lambda E, S, kt=kt, n=ncols, p=psc, e=eb, k=ksc:
                         E.activation(e[:, 0:n], p[:, 0:n], AF.Exp, scale=k[:, kt:kt + 1]), inc="s")
                    exp_done.append(P.cnt["s"])
                    mo = MASK_OFF.get((kt, b_))
                    if mo is not None:
                        o0, L = mo
                        P.wait("vector", "s")
                        P.do("vector", lambda E, S, e=eb, o0=o0, L=L:
                             E.tensor_mul(e[:, 0:L * 128], e[:, 0:L * 128], emask[:, o0 * 128:(o0 + L) * 128]), inc="v")
                        ready.append([("v", P.cnt["v"])])
                    else:
                        ready.append([("s", P.cnt["s"])])

                def emit_sumpv(i):
                    kt, b_, nU = ITERS[i]
                    qc0 = b_ * 512 + (4 - nU) * 128
                    ncols = nU * 128
                    eb = ebuf[i % 2]
                    first, last = (kt == 0), (kt == LAST_KT[b_])
                    if first and epi_rel_prev is not None:
                        P.waits("tensor", epi_rel_prev.get("r", []))
                    P.waits("tensor", ready[i])
                    P.do("tensor", lambda E, S, qc0=qc0, n=ncols, e=eb, f=first, l=last:
                         E.matmul(psS[0:1, qc0:qc0 + n], onescol[:], e[:, 0:n], start=f, stop=l))
                    P.do("tensor", lambda E, S, kt=kt, b_=b_, qc0=qc0, n=ncols, e=eb, f=first, l=last, h=h:
                         E.matmul([psE, psF][b_][:, qc0 - b_ * 512:qc0 - b_ * 512 + n],
                                  V[:, kt * T + h * 128:kt * T + (h + 1) * 128], e[:, 0:n], start=f, stop=l), inc="t")

                epi_n = 2 if pend_epi else 0
                for i in range(len(ITERS) + 1):
                    if i < len(ITERS):
                        emit_scores(i)
                    if i == 0:
                        while si < epi_n:
                            steps[si]()
                            si += 1
                    if i >= 1:
                        emit_sumpv(i - 1)
                    want = ((i + 1) * len(steps)) // (len(ITERS) + 1)
                    while si < want:
                        steps[si]()
                        si += 1
                while si < len(steps):
                    steps[si]()
                    si += 1
                sign_step(len(sp_items) // NH + 1)
                prep_rel = [("t", P.cnt["t"]), ("v", P.cnt["v"]), ("s", P.cnt["s"])]
                P.wait("vector", "t")
                P.do("vector", lambda E, S: E.tensor_copy(rowA[0:1, 0:NQ], psS[0:1, 0:1024]), inc="v")
                pend_epi = make_epi(h)
            for st_ in pend_epi[0]:
                st_()
            sign_step(len(sp_items))
            a3.close()
            P.barrier()

        if phase >= 4:
            a1.close()
            a4o = contextlib.ExitStack(); stacks.append(a4o); stacks.append(a4o)
            xqS = sbt(a4o, "xqS", [128, NCH * NQ], BF16)
            a4m = contextlib.ExitStack(); stacks.append(a4m); stacks.append(a4m)
            x2S = sbt(a4m, "x2S", [128, NCH * NQ])
            a4 = contextlib.ExitStack(); stacks.append(a4); stacks.append(a4)
            aoutS = sbt(a4, "aoutS", [128, NCH * NQ], BF16)
            wob = [sbt(a4, f"wob{i}", [128, C], BF16) for i in range(2)]
            abf = sbt(a4, "abf", [128, NQ], BF16)
            asq = sbt(a4, "asq", [128, NQ], BF16)
            xoc = [sbt(a4, f"xoc{i}", [128, NQ]) for i in range(2)]
            for j in range(NCH):
                P.do("gpsimd", lambda E, S, j=j: E.dma_start(out=aoutS[:, j * NQ:(j + 1) * NQ], in_=aoscr[j * 128:(j + 1) * 128, :]), inc="gin", n=16)
            relw = [[], []]
            relpp = [[("v", P.cnt["v"])], [("v", P.cnt["v"])]]
            for fc in range(NCH):
                bi = fc % 2
                pp = [psA, psB] if bi == 0 else [psC, psD]
                P.waits("gpsimd", relw[bi])
                P.do("gpsimd", lambda E, S, fc=fc, b=bi: E.dma_start(out=wob[b][:], in_=woT[fc]), inc=f"g{bi}", n=16)
                P.wait("tensor", f"g{bi}")
                P.wait("tensor", "gin")
                P.waits("tensor", relpp[bi])
                for ci in range(NCH):
                    for tb in range(2):
                        P.do("tensor", lambda E, S, ci=ci, tb=tb, b=bi, p=pp[tb]:
                             E.matmul(p[:, :], wob[b][:, ci * 128:(ci + 1) * 128], aoutS[:, ci * NQ + tb * 512:ci * NQ + (tb + 1) * 512],
                                      start=(ci == 0), stop=(ci == NCH - 1)), inc="t" if (ci == NCH - 1 and tb == 1) else None)
                relw[bi] = [("t", P.cnt["t"])]
                P.wait("vector", "t")
                for tb in range(2):
                    P.do("vector", lambda E, S, fc=fc, tb=tb, p=pp[tb]:
                         E.tensor_copy(x2S[:, fc * NQ + tb * 512:fc * NQ + (tb + 1) * 512], p[:, :]), inc="v" if tb == 1 else None)
                relpp[bi] = [("v", P.cnt["v"])]
                P.do("vector", lambda E, S, fc=fc: E.tensor_copy(abf[:], x2S[:, fc * NQ:(fc + 1) * NQ]), inc="v")
                P.wait("scalar", "v")
                P.do("scalar", lambda E, S: E.activation(asq[:], abf[:], AF.Square), inc="s")
                P.wait("tensor", "v")
                P.wait("tensor", "s")
                for tb in range(2):
                    P.do("tensor", lambda E, S, fc=fc, tb=tb, tg=[psS[0:1, 0:512], psS[0:1, 512:1024]][tb]:
                         E.matmul(tg, onescol[:], abf[:, tb * 512:(tb + 1) * 512], start=(fc == 0), stop=(fc == NCH - 1)))
                for tb in range(2):
                    P.do("tensor", lambda E, S, fc=fc, tb=tb, tg=[psE[0:1, :], psF[0:1, :]][tb]:
                         E.matmul(tg, onescol[:], asq[:, tb * 512:(tb + 1) * 512], start=(fc == 0), stop=(fc == NCH - 1)),
                         inc="t" if tb == 1 else None)
            P.wait("vector", "t")
            ln_rows(NQ, 0, [(psS[0:1, 0:512], 0, 512), (psS[0:1, 512:1024], 512, 512)],
                    [(psE[0:1, :], 0, 512), (psF[0:1, :], 512, 512)], rowA, rowB)
            P.wait("tensor", "v")
            bcast(rowA, NQ, [psA, psB])
            bcast(rowB, NQ, [psC, psD])
            P.wait("vector", "t")
            relx = [[], []]
            for j in range(NCH):
                bi = j % 2
                P.waits("sync", relx[bi])
                P.do("sync", lambda E, S, j=j, b=bi: E.dma_start(out=xoc[b][:], in_=xown[j * 128:(j + 1) * 128, :]), inc=f"xb{bi}", n=16)
                P.wait("vector", f"xb{bi}")
                for tb in range(2):
                    sl = slice(j * NQ + tb * 512, j * NQ + (tb + 1) * 512)
                    P.do("vector", lambda E, S, sl=sl, p=[psA, psB][tb]: E.tensor_mul(x2S[:, sl], x2S[:, sl], p[:, :]))
                    P.do("vector", lambda E, S, sl=sl, p=[psC, psD][tb]: E.tensor_add(x2S[:, sl], x2S[:, sl], p[:, :]))
                    P.do("vector", lambda E, S, sl=sl, b=bi, tb=tb:
                         E.tensor_add(x2S[:, sl], x2S[:, sl], xoc[b][:, tb * 512:(tb + 1) * 512]), inc="v")
                relx[bi] = [("v", P.cnt["v"])]
                P.wait("sync", "v")
                P.do("sync", lambda E, S, j=j: E.dma_start(out=x2scr[j * 128:(j + 1) * 128, :], in_=x2S[:, j * NQ:(j + 1) * NQ]), inc="sx2", n=16)
            a4.close()
            P.barrier()
            if phase == 4:
                for j in range(NCH):
                    P.do("gpsimd", lambda E, S, j=j: E.dma_start(out=dbg[j * 128:(j + 1) * 128, 0:NQ], in_=x2S[:, j * NQ:(j + 1) * NQ]), inc="gin", n=16)

        if phase >= 5:
            a5 = contextlib.ExitStack(); stacks.append(a5); stacks.append(a5)
            RB2 = sbt(a5, "RB2", [128, 2 * NQ])
            M3s = sbt(a5, "M3s", [128, NQ])
            Abc = sbt(a5, "Abc", [128, NQ])
            ybuf = sbt(a5, "ybuf", [128, NQ])
            yabs = sbt(a5, "yabs", [128, NQ])
            macc = sbt(a5, "macc", [128, NQ])
            tq = sbt(a5, "tq", [128, NQ])
            cf = sbt(a5, "cf", [128, NQ])
            b1t = sbt(a5, "b1t", [128, NQ])
            xi = sbt(a5, "xi", [128, NQ], I32)
            qbf = sbt(a5, "qbf", [128, NQ], BF16)
            qsq = sbt(a5, "qsq", [128, NQ], BF16)
            tcol = sbt(a5, "tcol", [128, 8])

            def stats_from(src_emit):
                for j in range(NCH):
                    src_emit(j)
                    P.wait("tensor", "v")
                    P.wait("tensor", "s")
                    for tb in range(2):
                        P.do("tensor", lambda E, S, j=j, tb=tb, tg=[psS[0:1, 0:512], psS[0:1, 512:1024]][tb]:
                             E.matmul(tg, onescol[:], qbf[:, tb * 512:(tb + 1) * 512], start=(j == 0), stop=(j == NCH - 1)))
                    for tb in range(2):
                        P.do("tensor", lambda E, S, j=j, tb=tb, tg=[psE[0:1, :], psF[0:1, :]][tb]:
                             E.matmul(tg, onescol[:], qsq[:, tb * 512:(tb + 1) * 512], start=(j == 0), stop=(j == NCH - 1)),
                             inc="t" if tb == 1 else None)
                P.wait("vector", "t")

            def src_x2(j):
                P.wait("vector", "t", P.cnt["t"])
                P.do("vector", lambda E, S, j=j: E.tensor_copy(qbf[:], x2S[:, j * NQ:(j + 1) * NQ]), inc="v")
                P.wait("scalar", "v")
                P.do("scalar", lambda E, S: E.activation(qsq[:], qbf[:], AF.Square), inc="s")

            stats_from(src_x2)
            ln_rows(NQ, 0, [(psS[0:1, 0:512], 0, 512), (psS[0:1, 512:1024], 512, 512)],
                    [(psE[0:1, :], 0, 512), (psF[0:1, :], 512, 512)], rowA, rowB)
            P.wait("tensor", "v")
            bcast(rowA, NQ, [psA, psB])
            bcast(rowB, NQ, [psC, psD])
            P.wait("vector", "t")
            for tb in range(2):
                P.do("vector", lambda E, S, tb=tb, p=[psA, psB][tb]: E.tensor_copy(RB2[:, tb * 512:(tb + 1) * 512], p[:, :]))
                P.do("vector", lambda E, S, tb=tb, p=[psC, psD][tb]: E.tensor_copy(RB2[:, NQ + tb * 512:NQ + (tb + 1) * 512], p[:, :]), inc="v" if tb == 1 else None)

            P.wait("vector", "sx2")
            for j in range(NCH):
                P.do("vector", lambda E, S, j=j: E.tensor_mul(x2S[:, j * NQ:(j + 1) * NQ], x2S[:, j * NQ:(j + 1) * NQ], RB2[:, 0:NQ]))
                P.do("vector", lambda E, S, j=j: E.tensor_add(x2S[:, j * NQ:(j + 1) * NQ], x2S[:, j * NQ:(j + 1) * NQ], RB2[:, NQ:2 * NQ]), inc="v")

            def src_y(j):
                P.wait("vector", "t", P.cnt["t"])
                P.do("vector", lambda E, S, j=j: E.tensor_copy(qbf[:], x2S[:, j * NQ:(j + 1) * NQ]), inc="v")
                P.wait("scalar", "v")
                P.do("scalar", lambda E, S: E.activation(qsq[:], qbf[:], AF.Square), inc="s")

            stats_from(src_y)
            P.do("vector", lambda E, S: E.tensor_scalar_mul(rowB[0:1, 0:NQ], psS[0:1, 0:1024], 1.0 / C))
            P.do("vector", lambda E, S: E.tensor_scalar_mul(rowA[0:1, 0:512], psE[0:1, :], 1.0 / C))
            P.do("vector", lambda E, S: E.tensor_scalar_mul(rowA[0:1, 512:1024], psF[0:1, :], 1.0 / C))
            P.do("vector", lambda E, S: E.tensor_mul(psE[0:1, :], rowB[0:1, 0:512], rowB[0:1, 0:512]))
            P.do("vector", lambda E, S: E.tensor_mul(psF[0:1, :], rowB[0:1, 512:1024], rowB[0:1, 512:1024]))
            P.do("vector", lambda E, S: E.tensor_sub(rowA[0:1, 0:512], rowA[0:1, 0:512], psE[0:1, :]))
            P.do("vector", lambda E, S: E.tensor_sub(rowA[0:1, 512:1024], rowA[0:1, 512:1024], psF[0:1, :]), inc="v")
            P.wait("scalar", "v")
            P.do("scalar", lambda E, S: E.activation(rowA[0:1, 0:NQ], rowA[0:1, 0:NQ], AF.Sqrt, bias=epsT[0:1, 2:3]), inc="s")
            P.wait("vector", "s")
            P.do("vector", lambda E, S: E.reciprocal(rowA[0:1, 0:NQ], rowA[0:1, 0:NQ]), inc="v")
            P.wait("tensor", "v")
            bcast(rowB, NQ, [psA, psB])
            P.wait("vector", "t")
            for tb in range(2):
                P.do("vector", lambda E, S, tb=tb, p=[psA, psB][tb]:
                     E.tensor_copy(M3s[:, tb * 512:(tb + 1) * 512], p[:, :]), inc="v" if tb == 1 else None)
            for j in range(NCH):
                P.do("vector", lambda E, S, j=j: E.tensor_sub(ybuf[:], x2S[:, j * NQ:(j + 1) * NQ], M3s[:]), inc="v")
                P.wait("scalar", "v")
                P.do("scalar", lambda E, S: E.activation(yabs[:], ybuf[:], AF.Abs), inc="s")
                P.wait("vector", "s")
                if j == 0:
                    P.do("vector", lambda E, S: E.tensor_copy(macc[:], yabs[:]), inc="v")
                else:
                    P.do("vector", lambda E, S: E.tensor_max(macc[:], macc[:], yabs[:]), inc="v")
            P.wait("tensor", "v")
            for jb in range(8):
                p = [psA, psB][jb % 2]
                P.wait("tensor", "v", P.cnt["v"])
                P.do("tensor", lambda E, S, jb=jb, p=p:
                     E.transpose(p[:, 0:128], macc[:, jb * 128:(jb + 1) * 128], ident[:]), inc="t")
                P.wait("vector", "t")
                P.do("vector", lambda E, S, jb=jb, p=p:
                     E.reduce_max(tcol[:, jb:jb + 1], p[:, 0:128], axis=mybir.AxisListType.X), inc="v")
            P.wait("tensor", "v")
            for jb in range(8):
                p = [psC, psD][jb % 2]
                P.wait("tensor", "v", P.cnt["v"])
                P.do("tensor", lambda E, S, jb=jb, p=p:
                     E.matmul(p[0:1, 0:128], tcol[:, jb:jb + 1], ident[:], start=True, stop=True), inc="t")
                P.wait("vector", "t")
                P.do("vector", lambda E, S, jb=jb, p=p:
                     E.tensor_copy(rowB[0:1, jb * 128:(jb + 1) * 128], p[0:1, 0:128]), inc="v")
            P.do("vector", lambda E, S: E.tensor_mul(rowA[0:1, 0:NQ], rowA[0:1, 0:NQ], rowB[0:1, 0:NQ]))
            P.do("vector", lambda E, S: E.tensor_scalar_mul(rowA[0:1, 0:NQ], rowA[0:1, 0:NQ], 1.0 / 127.0))
            P.do("vector", lambda E, S: E.reciprocal(rowB[0:1, 0:NQ], rowB[0:1, 0:NQ]))
            P.do("vector", lambda E, S: E.tensor_scalar_mul(rowB[0:1, 0:NQ], rowB[0:1, 0:NQ], 127.0), inc="v")
            P.wait("tensor", "v")
            bcast(rowB, NQ, [psA, psB])
            bcast(rowA, NQ, [psC, psD])
            P.wait("vector", "t")
            for tb in range(2):
                P.do("vector", lambda E, S, tb=tb, p=[psA, psB][tb]: E.tensor_copy(Abc[:, tb * 512:(tb + 1) * 512], p[:, :]))
                P.do("vector", lambda E, S, tb=tb, p=[psC, psD][tb]: E.tensor_copy(invsbc[:, tb * 512:(tb + 1) * 512], p[:, :]), inc="v" if tb == 1 else None)
            for j in range(NCH):
                P.do("vector", lambda E, S, j=j: E.tensor_sub(ybuf[:], x2S[:, j * NQ:(j + 1) * NQ], M3s[:]))
                P.do("vector", lambda E, S: E.tensor_mul(tq[:], ybuf[:], Abc[:]), inc="v")
                P.wait("scalar", "v")
                P.do("scalar", lambda E, S: E.copy(xi[:], tq[:]))
                P.do("scalar", lambda E, S: E.copy(cf[:], xi[:]), inc="s")
                P.wait("vector", "s")
                P.do("vector", lambda E, S: E.tensor_sub(ybuf[:], tq[:], cf[:]))
                P.do("vector", lambda E, S: E.tensor_single_scalar(b1t[:], ybuf[:], 0.5, ALU.is_ge))
                P.do("vector", lambda E, S: E.tensor_single_scalar(ybuf[:], ybuf[:], -0.5, ALU.is_lt))
                P.do("vector", lambda E, S: E.tensor_add(cf[:], cf[:], b1t[:]))
                P.do("vector", lambda E, S: E.tensor_sub(cf[:], cf[:], ybuf[:]))
                P.do("vector", lambda E, S, j=j: E.tensor_scalar(xqS[:, j * NQ:(j + 1) * NQ], cf[:], -128.0, 127.0, ALU.max, ALU.min), inc="v")
            a5.close()
            a4m.close()
            P.barrier()
            if phase == 5:
                for j in range(NCH):
                    P.do("gpsimd", lambda E, S, j=j: E.dma_start(out=dbg[j * 128:(j + 1) * 128, 0:NQ], in_=xqS[:, j * NQ:(j + 1) * NQ]), inc="gin", n=16)
                P.do("gpsimd", lambda E, S: E.dma_start(out=dbg[0:128, NQ:2 * NQ], in_=invsbc[:]), inc="gin", n=16)

        if phase >= 6:
            a6 = contextlib.ExitStack(); stacks.append(a6); stacks.append(a6)
            pS = sbt(a6, "pS", [128, HC * 512], BF16)
            wgb = [sbt(a6, f"wgb{i}", [128, C], BF16) for i in range(2)]
            wub = [sbt(a6, f"wub{i}", [128, C], BF16) for i in range(2)]
            gt0 = sbt(a6, "gt0", [128, 512])
            gt1 = sbt(a6, "gt1", [128, 512])
            ffS = sbt(a6, "ffS", [128, NCH * 512])
            fbf = sbt(a6, "fbf", [128, 512], BF16)
            fsq = sbt(a6, "fsq", [128, 512], BF16)
            wdb = sbt(a6, "wdb", [128, 8192], BF16)
            x2c = [sbt(a6, f"x2c{i}", [128, 512]) for i in range(2)]
            RB4 = sbt(a6, "RB4", [128, 1024])
            for tb in range(2):
                tsl = slice(tb * 512, (tb + 1) * 512)
                relw = [[("s", P.cnt["s"]), ("t", P.cnt["t"])] for _ in range(2)]
                relgu = [[("v", P.cnt["v"])], [("v", P.cnt["v"])]]
                gl_done = [0] * HC

                def gu_load(hc):
                    bi = hc % 2
                    P.waits("gpsimd", relw[bi])
                    P.do("gpsimd", lambda E, S, hc=hc, b=bi: E.dma_start(out=wgb[b][:], in_=sgg[hc]), inc=f"g{bi}", n=16)
                    P.do("gpsimd", lambda E, S, hc=hc, b=bi: E.dma_start(out=wub[b][:], in_=sgu[hc]), inc=f"g{bi}", n=16)
                    gl_done[hc] = P.cnt[f"g{bi}"]

                gu_load(0)
                gu_load(1)
                for hc in range(HC):
                    bi = hc % 2
                    ppG = [psA, psC][bi]
                    ppU = [psB, psD][bi]
                    P.wait("tensor", f"g{bi}", gl_done[hc])
                    P.waits("tensor", relgu[bi])
                    for ci in range(NCH):
                        P.do("tensor", lambda E, S, ci=ci, b=bi, p=ppG, tb=tb:
                             E.matmul(p[:, :], wgb[b][:, ci * 128:(ci + 1) * 128], xqS[:, ci * NQ + tb * 512:ci * NQ + (tb + 1) * 512],
                                      start=(ci == 0), stop=(ci == NCH - 1)))
                        P.do("tensor", lambda E, S, ci=ci, b=bi, p=ppU, tb=tb:
                             E.matmul(p[:, :], wub[b][:, ci * 128:(ci + 1) * 128], xqS[:, ci * NQ + tb * 512:ci * NQ + (tb + 1) * 512],
                                      start=(ci == 0), stop=(ci == NCH - 1)), inc="t" if ci == NCH - 1 else None)
                    relw[bi] = [("t", P.cnt["t"])]
                    if hc + 2 < HC:
                        gu_load(hc + 2)
                    P.wait("vector", "t")
                    P.do("vector", lambda E, S, p=ppG, tsl=tsl: E.tensor_mul(gt0[:], p[:, :], invsbc[:, tsl]), inc="v")
                    P.wait("scalar", "v")
                    P.do("scalar", lambda E, S: E.activation(gt1[:], gt0[:], AF.Silu), inc="s")
                    P.wait("vector", "s")
                    P.do("vector", lambda E, S, p=ppU: E.tensor_mul(gt0[:], gt1[:], p[:, :]))
                    P.do("vector", lambda E, S, hc=hc, tsl=tsl: E.tensor_mul(pS[:, hc * 512:(hc + 1) * 512], gt0[:], invsbc[:, tsl]), inc="v")
                    relgu[bi] = [("v", P.cnt["v"])]
                P.barrier(("gpsimd", "tensor", "scalar", "vector"))
                relwd = [[("s", P.cnt["s"]), ("t", P.cnt["t"])] for _ in range(2)]
                relpw = [[("v", P.cnt["v"])], [("v", P.cnt["v"])]]
                gdone = [0, 0]
                for fc in range(NCH):
                    pw = [psE, psF][fc % 2]
                    for hf in range(2):
                        P.waits("gpsimd", relwd[hf])
                        P.do("gpsimd", lambda E, S, fc=fc, hf=hf:
                             E.dma_start(out=wdb[:, hf * 4096:(hf + 1) * 4096], in_=sgd[fc, :, hf * 4096:(hf + 1) * 4096]), inc=f"g{hf}", n=16)
                        gdone[hf] = P.cnt[f"g{hf}"]
                    P.waits("tensor", relpw[fc % 2])
                    for hc in range(HC):
                        if hc == 0:
                            P.wait("tensor", "g0", gdone[0])
                        if hc == HC // 2:
                            P.wait("tensor", "g1", gdone[1])
                        P.do("tensor", lambda E, S, hc=hc, p=pw:
                             E.matmul(p[:, :], wdb[:, hc * 128:(hc + 1) * 128], pS[:, hc * 512:(hc + 1) * 512],
                                      start=(hc == 0), stop=(hc == HC - 1)), inc="t" if hc in (HC // 2 - 1, HC - 1) else None)
                        if hc == HC // 2 - 1:
                            relwd[0] = [("t", P.cnt["t"])]
                    relwd[1] = [("t", P.cnt["t"])]
                    P.wait("vector", "t")
                    P.do("vector", lambda E, S, fc=fc, p=pw: E.tensor_copy(ffS[:, fc * 512:(fc + 1) * 512], p[:, :]))
                    P.do("vector", lambda E, S, fc=fc: E.tensor_copy(fbf[:], ffS[:, fc * 512:(fc + 1) * 512]), inc="v")
                    relpw[fc % 2] = [("v", P.cnt["v"])]
                    P.wait("scalar", "v")
                    P.do("scalar", lambda E, S: E.activation(fsq[:], fbf[:], AF.Square), inc="s")
                    P.wait("tensor", "v")
                    P.wait("tensor", "s")
                    P.do("tensor", lambda E, S, fc=fc:
                         E.matmul(psS[0:1, 0:512], onescol[:], fbf[:], start=(fc == 0), stop=(fc == NCH - 1)))
                    P.do("tensor", lambda E, S, fc=fc:
                         E.matmul(psS[0:1, 512:1024], onescol[:], fsq[:], start=(fc == 0), stop=(fc == NCH - 1)),
                         inc="t" if fc == NCH - 1 else None)
                P.wait("vector", "t")
                ln_rows(512, 0, [(psS[0:1, 0:512], 0, 512)], [(psS[0:1, 512:1024], 0, 512)], rowA, rowB)
                P.wait("tensor", "v")
                bcast(rowA, 512, [psA])
                bcast(rowB, 512, [psB])
                P.wait("vector", "t")
                P.do("vector", lambda E, S: E.tensor_copy(RB4[:, 0:512], psA[:, :]))
                P.do("vector", lambda E, S: E.tensor_copy(RB4[:, 512:1024], psB[:, :]), inc="v")
                relx = [[], []]
                for fc in range(NCH):
                    bi = fc % 2
                    P.waits("sync", relx[bi])
                    P.do("sync", lambda E, S, fc=fc, b=bi, tb=tb:
                         E.dma_start(out=x2c[b][:], in_=x2scr[fc * 128:(fc + 1) * 128, tb * 512:(tb + 1) * 512]), inc=f"xb{bi}", n=16)
                    P.wait("vector", f"xb{bi}")
                    P.do("vector", lambda E, S, fc=fc: E.tensor_mul(gt0[:], ffS[:, fc * 512:(fc + 1) * 512], RB4[:, 0:512]))
                    P.do("vector", lambda E, S: E.tensor_add(gt0[:], gt0[:], RB4[:, 512:1024]))
                    P.do("vector", lambda E, S, b=bi: E.tensor_add(gt0[:], gt0[:], x2c[b][:]))
                    P.do("vector", lambda E, S: E.tensor_copy(gt1[:], gt0[:]), inc="v")
                    relx[bi] = [("v", P.cnt["v"])]
                    P.wait("sync", "v")
                    P.do("sync", lambda E, S, fc=fc, tb=tb:
                         E.dma_start(out=out[fc * 128:(fc + 1) * 128, tb * 512:(tb + 1) * 512], in_=gt1[:]), inc="sout", n=16)
                    P.wait("vector", "sout")
            a6.close()

        P.barrier()
        for st_ in reversed(stacks):
            st_.close()
        ops = P.ops

        @blk.sync
        def _(E):
            for f in ops["sync"]:
                f(E, sems)

        @blk.gpsimd
        def _(E):
            for f in ops["gpsimd"]:
                f(E, sems)

        @blk.vector
        def _(E):
            for f in ops["vector"]:
                f(E, sems)

        @blk.scalar
        def _(E):
            for f in ops["scalar"]:
                f(E, sems)

        @blk.tensor
        def _(E):
            for f in ops["tensor"]:
                f(E, sems)

    return nc



def _lhst_blocks(W):
    F_, Ci = W.shape
    A = np.asarray(W).reshape(F_ // 128, 128, Ci // 128, 128)
    return np.ascontiguousarray(A.transpose(0, 3, 2, 1).reshape(F_ // 128, 128, Ci))


def _rope_tables():
    import jax
    import jax.numpy as jnp
    with jax.default_device(jax.devices("cpu")[0]):
        inv = 1.0 / 10000.0 ** (jnp.arange(0, 128, 2, dtype=jnp.float32) / 128)
        t = jnp.arange(T, dtype=jnp.float32)
        fr = t[:, None] * inv[None, :]
        sin = np.asarray(jnp.repeat(jnp.sin(fr), 2, axis=-1))
        cos = np.asarray(jnp.repeat(jnp.cos(fr), 2, axis=-1))
    return sin, cos


def _own_positions(p):
    return np.concatenate([np.arange(128 * t, 128 * (t + 1)) for t in OWN[p]])


def _perm_positions(p):
    return np.concatenate([np.arange(128 * t, 128 * (t + 1)) for t in PERM[p]])


def _emask_for(p):
    tri = (np.arange(128)[:, None] <= np.arange(128)[None, :]).astype(ml_dtypes.bfloat16)
    m = np.ones((128, max(MASK_SLOTS, 1) * 128), dtype=ml_dtypes.bfloat16)
    for (ks, b_), (o0, L) in MASK_OFF.items():
        nU = PROFS[b_][ks]
        g = PERM[p][ks]
        for s_ in range(L):
            qslot = b_ * 4 + (4 - nU) + s_
            qt = PERM[p][qslot]
            sl = slice((o0 + s_) * 128, (o0 + s_ + 1) * 128)
            if qt < g:
                m[:, sl] = 0
            elif qt == g:
                m[:, sl] = tri
    return m


def make_inmaps(L0_attn_x, L0_attn_W_qkv, L0_attn_W_o, L0_ffn_W_g, L0_ffn_W_u, L0_ffn_W_d):
    x = np.asarray(L0_attn_x)
    Wqkv = np.asarray(L0_attn_W_qkv)
    shared = {
        "wqT": _lhst_blocks(Wqkv[0:2048]),
        "wkT": _lhst_blocks(Wqkv[2048:4096]),
        "wvT": np.ascontiguousarray(Wqkv[4096:6144].T.reshape(NCH, 128, C)),
        "woT": _lhst_blocks(np.asarray(L0_attn_W_o)),
        "wgT": _lhst_blocks(np.asarray(L0_ffn_W_g)),
        "wuT": _lhst_blocks(np.asarray(L0_ffn_W_u)),
        "wdT": _lhst_blocks(np.asarray(L0_ffn_W_d)),
        "ident": np.eye(128, dtype=np.float32),
    }
    sin, cos = _rope_tables()
    sgn = np.where(np.arange(128) % 2 == 0, -1.0, 1.0).astype(np.float32)
    rkc = np.ascontiguousarray(cos.T)
    rks = np.ascontiguousarray(sin.T * sgn[:, None])
    in_maps = []
    for core in range(8):
        b, p = core // 2, core % 2
        pp = _perm_positions(p)
        xt = np.ascontiguousarray(x[b].T[:, pp])
        d = dict(shared)
        d["xT"] = xt
        d["xown"] = np.ascontiguousarray(xt[:, 0:NQ])
        d["rkc"] = np.ascontiguousarray(rkc[:, pp])
        d["rks"] = np.ascontiguousarray(rks[:, pp])
        d["emask"] = _emask_for(p)
        in_maps.append(d)
    return in_maps


def kernel(L0_attn_x, L0_attn_W_qkv, L0_attn_W_o, L0_ffn_W_g, L0_ffn_W_u, L0_ffn_W_d):
    in_maps = make_inmaps(L0_attn_x, L0_attn_W_qkv, L0_attn_W_o, L0_ffn_W_g, L0_ffn_W_u, L0_ffn_W_d)
    nc = build_nc(phase=6)
    res = run_bass_kernel_spmd(nc, in_maps, core_ids=list(range(8)))
    y = np.empty((4, T, C), dtype=np.float32)
    for core in range(8):
        b, p = core // 2, core % 2
        pos = _own_positions(p)
        y[b, pos, :] = np.asarray(res.results[core]["out"]).T
    return y


# version 12
# speedup vs baseline: 1.4257x; 1.4257x over previous
import contextlib
import numpy as np
import ml_dtypes
import concourse.bass as bass
import concourse.mybir as mybir
from concourse.bass_utils import run_bass_kernel_spmd

F32 = mybir.dt.float32
BF16 = mybir.dt.bfloat16
I32 = mybir.dt.int32
AF = mybir.ActivationFunctionType
ALU = mybir.AluOpType

C = 2048
T = 2048
NH = 16
NCH = 16
NQ = 1024
HC = 64
RMS_EPS = 1.1920929e-07
LN_EPS = 1e-5
FFN_EPS = 1e-8
SCALE = float(np.float32(1.0) / np.float32(np.sqrt(np.float32(128.0))))
SHUF = [i ^ 1 for i in range(32)]

OWN = {0: [0, 3, 4, 7, 8, 11, 12, 15], 1: [1, 2, 5, 6, 9, 10, 13, 14]}
PERM = {p: OWN[p] + sorted(set(range(16)) - set(OWN[p])) for p in (0, 1)}

def _profiles():
    na, nb = [0] * 16, [0] * 16
    for ks in range(16):
        for p in (0, 1):
            g = PERM[p][ks]
            ca = sum(1 for j in range(0, 4) if PERM[p][j] >= g)
            cb = sum(1 for j in range(4, 8) if PERM[p][j] >= g)
            na[ks] = max(na[ks], ca)
            nb[ks] = max(nb[ks], cb)
    return na, nb

NA_U, NB_U = _profiles()
PROFS = (NA_U, NB_U)


def _mask_layout():
    off, o = {}, 0
    for b_ in (0, 1):
        for ks in range(16):
            nU = PROFS[b_][ks]
            if nU == 0:
                continue
            worst = 0
            for p in (0, 1):
                g = PERM[p][ks]
                qs = [PERM[p][j] for j in range(b_ * 4, b_ * 4 + 4)]
                n_p = sum(1 for t in qs if t >= g)
                worst = max(worst, (nU - n_p) + (1 if g in qs else 0))
            if worst > 0:
                off[(ks, b_)] = (o, worst)
                o += worst
    return off, o


MASK_OFF, MASK_SLOTS = _mask_layout()
for _p in (0, 1):
    for _b in (0, 1):
        _qs = [PERM[_p][j] for j in range(_b * 4, _b * 4 + 4)]
        assert all(_qs[i] < _qs[i + 1] for i in range(3))

ENGINES = ("sync", "gpsimd", "vector", "scalar", "tensor")
SEMS = ("t", "v", "s", "din", "xb0", "xb1", "spl0", "spl1", "sx2", "sout", "g0", "g1", "gin", "gin0", "gin1", "gw", "sw0", "sw1")


class Prog:
    def __init__(self):
        self.ops = {e: [] for e in ENGINES}
        self.cnt = {n: 0 for n in SEMS}

    def emit(self, eng, fn):
        self.ops[eng].append(fn)

    def wait(self, eng, sem, val=None):
        v = self.cnt[sem] if val is None else val
        if v > 0:
            self.emit(eng, lambda E, S, _s=sem, _v=v: E.wait_ge(S[_s], _v))

    def waits(self, eng, rel):
        for sem, v in rel:
            self.wait(eng, sem, v)

    def do(self, eng, fn, inc=None, n=1):
        if eng == "vector" or eng == "scalar":
            sem = "v" if eng == "vector" else "s"
            prev = self.cnt[sem]
            self.cnt[sem] = prev + 1
            def wrapped(E, S, _f=fn, _s=sem, _p=prev):
                if _p > 0:
                    E.wait_ge(S[_s], _p)
                return _f(E, S).then_inc(S[_s], 1)
            self.emit(eng, wrapped)
            return
        if inc is not None:
            self.cnt[inc] += n
            self.emit(eng, lambda E, S, _f=fn, _i=inc, _n=n: _f(E, S).then_inc(S[_i], _n))
        else:
            self.emit(eng, lambda E, S, _f=fn: _f(E, S))

    def snap(self):
        return [(s, v) for s, v in self.cnt.items() if v > 0]

    def barrier(self, engs=ENGINES):
        snap = self.snap()
        for e in engs:
            self.waits(e, snap)


def build_nc(phase=6):
    nc = bass.Bass("TRN2", target_bir_lowering=False)
    dtn = nc.dram_tensor
    xT = dtn("xT", [C, T], F32, kind="ExternalInput").ap()
    xown = dtn("xown", [C, NQ], F32, kind="ExternalInput").ap()
    wqT = dtn("wqT", [NH, 128, C], F32, kind="ExternalInput").ap()
    wkT = dtn("wkT", [NH, 128, C], F32, kind="ExternalInput").ap()
    wvT = dtn("wvT", [NCH, 128, C], F32, kind="ExternalInput").ap()
    woT = dtn("woT", [NCH, 128, C], F32, kind="ExternalInput").ap()
    wgT = dtn("wgT", [HC, 128, C], F32, kind="ExternalInput").ap()
    wuT = dtn("wuT", [HC, 128, C], F32, kind="ExternalInput").ap()
    wdT = dtn("wdT", [NCH, 128, 8192], F32, kind="ExternalInput").ap()
    rkc = dtn("rkc", [128, T], F32, kind="ExternalInput").ap()
    rks = dtn("rks", [128, T], F32, kind="ExternalInput").ap()
    emaskD = dtn("emask", [128, max(MASK_SLOTS, 1) * 128], BF16, kind="ExternalInput").ap()
    identD = dtn("ident", [128, 128], F32, kind="ExternalInput").ap()
    out = dtn("out", [C, NQ], F32, kind="ExternalOutput").ap()
    aoscr = dtn("aoscr", [C, NQ], BF16, kind="Internal").ap()
    sgg = dtn("sgg", [HC, 128, C], BF16, kind="Internal").ap()
    sgu = dtn("sgu", [HC, 128, C], BF16, kind="Internal").ap()
    sgd = dtn("sgd", [NCH, 128, 8192], BF16, kind="Internal").ap()
    x2scr = dtn("x2scr", [C, NQ], F32, kind="Internal").ap()
    dbg = dtn("dbg", [C, T], F32, kind="ExternalOutput").ap() if phase < 6 else None

    P = Prog()
    ctx = contextlib.ExitStack()
    stacks = []
    with ctx:
        def sbt(st, nm, sh, d=F32):
            return st.enter_context(nc.sbuf_tensor(nm, sh, d))
        onescol = sbt(ctx, "onescol", [128, 1], BF16)
        onesrow = sbt(ctx, "onesrow", [1, 128], F32)
        ident = sbt(ctx, "ident_s", [128, 128], F32)
        rowA = sbt(ctx, "rowA", [1, T])
        rowB = sbt(ctx, "rowB", [1, T])
        invsbc = sbt(ctx, "invsbc", [128, NQ])
        epsT = sbt(ctx, "epsT", [128, 3])
        psA = ctx.enter_context(nc.psum_tensor("psA", [128, 512], F32))
        psB = ctx.enter_context(nc.psum_tensor("psB", [128, 512], F32))
        psC = ctx.enter_context(nc.psum_tensor("psC", [128, 512], F32))
        psD = ctx.enter_context(nc.psum_tensor("psD", [128, 512], F32))
        psE = ctx.enter_context(nc.psum_tensor("psE", [128, 512], F32))
        psF = ctx.enter_context(nc.psum_tensor("psF", [128, 512], F32))
        psS = ctx.enter_context(nc.psum_tensor("psS", [1, 1024], F32))
        PS4 = [psA, psB, psC, psD]
        sems = {n: ctx.enter_context(nc.semaphore("sem_" + n)) for n in SEMS}
        blk = ctx.enter_context(nc.Block())

        P.do("sync", lambda E, S: E.dma_start(out=ident[:], in_=identD), inc="din", n=16)
        P.do("vector", lambda E, S: E.memset(onescol[:], 1.0))
        P.do("vector", lambda E, S: E.memset(onesrow[:], 1.0))
        P.do("vector", lambda E, S: E.memset(epsT[:, 0:1], LN_EPS))
        P.do("vector", lambda E, S: E.memset(epsT[:, 1:2], RMS_EPS))
        P.do("vector", lambda E, S: E.memset(epsT[:, 2:3], FFN_EPS), inc="v")

        def ln_rows(ntok, epscol, s1ap, s2ap, rR, rB):
            for a_, c0, n in s1ap:
                P.do("vector", lambda E, S, a=a_, c=c0, n=n: E.tensor_scalar_mul(rB[0:1, c:c + n], a, 1.0 / C))
            for a_, c0, n in s2ap:
                P.do("vector", lambda E, S, a=a_, c=c0, n=n: E.tensor_scalar_mul(rR[0:1, c:c + n], a, 1.0 / C))
            for a_, c0, n in s1ap:
                P.do("vector", lambda E, S, a=a_, c=c0, n=n: E.tensor_mul(a, rB[0:1, c:c + n], rB[0:1, c:c + n]))
            for a_, c0, n in s1ap:
                P.do("vector", lambda E, S, a=a_, c=c0, n=n: E.tensor_sub(rR[0:1, c:c + n], rR[0:1, c:c + n], a), inc="v")
            P.wait("scalar", "v")
            P.do("scalar", lambda E, S: E.activation(rR[0:1, 0:ntok], rR[0:1, 0:ntok], AF.Sqrt, bias=epsT[0:1, epscol:epscol + 1]), inc="s")
            P.wait("vector", "s")
            P.do("vector", lambda E, S: E.reciprocal(rR[0:1, 0:ntok], rR[0:1, 0:ntok]))
            P.do("vector", lambda E, S: E.scalar_tensor_tensor(rB[0:1, 0:ntok], rB[0:1, 0:ntok], -1.0, rR[0:1, 0:ntok], ALU.mult, ALU.mult), inc="v")

        def bcast(rowap, ncols, pslist):
            nb = (ncols + 511) // 512
            for i in range(nb):
                n = min(512, ncols - i * 512)
                P.do("tensor", lambda E, S, p=pslist[i], c=i * 512, n=n, r=rowap:
                     E.matmul(p[:, 0:n], onesrow[:], r[0:1, c:c + n], start=True, stop=True),
                     inc="t" if i == nb - 1 else None)

        a1 = contextlib.ExitStack(); stacks.append(a1)
        LNx = sbt(a1, "LNx", [128, NCH * T], BF16)
        V = sbt(a1, "V", [128, NCH * T], BF16)
        a1b = contextlib.ExitStack(); stacks.append(a1b)
        xbuf = [sbt(a1b, f"xbuf{i}", [128, T]) for i in range(2)]
        xbf = [sbt(a1b, f"xbf{i}", [128, T], BF16) for i in range(1)]
        xsq = [sbt(a1b, f"xsq{i}", [128, T], BF16) for i in range(1)]
        RBs = sbt(a1b, "RBs", [128, 2 * T])

        s1rows = [(psA[0:1, :], 0, 512), (psB[0:1, :], 512, 512), (psC[0:1, :], 1024, 512), (psD[0:1, :], 1536, 512)]
        s2rows = [(psE[0:1, :], 0, 512), (psF[0:1, :], 512, 512), (psS[0:1, 0:512], 1024, 512), (psS[0:1, 512:1024], 1536, 512)]
        rel = [[], []]
        for j in range(NCH):
            bi = j % 2
            P.waits("sync", rel[bi])
            P.do("sync", lambda E, S, j=j, b=bi: E.dma_start(out=xbuf[b][:], in_=xT[j * 128:(j + 1) * 128, :]), inc=f"xb{bi}", n=16)
            P.wait("vector", f"xb{bi}")
            P.wait("vector", "t", P.cnt["t"])
            P.do("vector", lambda E, S, b=bi: E.tensor_copy(xbf[0][:], xbuf[b][:]), inc="v")
            P.wait("scalar", f"xb{bi}")
            P.wait("scalar", "t", P.cnt["t"])
            P.do("scalar", lambda E, S, b=bi: E.activation(xsq[0][:], xbuf[b][:], AF.Square), inc="s")
            P.wait("tensor", "v")
            P.wait("tensor", "s")
            for tb in range(4):
                P.do("tensor", lambda E, S, j=j, tb=tb, p=PS4[tb]:
                     E.matmul(p[0:1, :], onescol[:], xbf[0][:, tb * 512:(tb + 1) * 512], start=(j == 0), stop=(j == NCH - 1)))
            for tb in range(4):
                tgt = [psE[0:1, :], psF[0:1, :], psS[0:1, 0:512], psS[0:1, 512:1024]][tb]
                P.do("tensor", lambda E, S, j=j, tb=tb, tg=tgt:
                     E.matmul(tg, onescol[:], xsq[0][:, tb * 512:(tb + 1) * 512], start=(j == 0), stop=(j == NCH - 1)),
                     inc="t" if tb == 3 else None)
            rel[bi] = [("v", P.cnt["v"]), ("s", P.cnt["s"])]
        P.wait("vector", "t")
        ln_rows(T, 0, s1rows, s2rows, rowA, rowB)
        P.wait("tensor", "v")
        for which, row in ((0, rowA), (1, rowB)):
            bcast(row, T, PS4)
            P.wait("vector", "t")
            for tb in range(4):
                P.do("vector", lambda E, S, w=which, tb=tb, p=PS4[tb]:
                     E.tensor_copy(RBs[:, w * T + tb * 512:w * T + (tb + 1) * 512], p[:, :]), inc="v" if tb == 3 else None)
            P.wait("tensor", "v")
        rel = [P.snap(), P.snap()]
        for j in range(NCH):
            bi = j % 2
            P.waits("sync", rel[bi])
            P.do("sync", lambda E, S, j=j, b=bi: E.dma_start(out=xbuf[b][:], in_=xT[j * 128:(j + 1) * 128, :]), inc=f"xb{bi}", n=16)
            P.wait("vector", f"xb{bi}")
            P.do("vector", lambda E, S, b=bi: E.tensor_mul(xbuf[b][:], xbuf[b][:], RBs[:, 0:T]))
            P.do("vector", lambda E, S, j=j, b=bi: E.tensor_add(LNx[:, j * T:(j + 1) * T], xbuf[b][:], RBs[:, T:2 * T]), inc="v")
            rel[bi] = [("v", P.cnt["v"])]
        P.barrier()
        a1b.close()
        if phase == 1:
            for j in range(NCH):
                P.do("gpsimd", lambda E, S, j=j: E.dma_start(out=dbg[j * 128:(j + 1) * 128, :], in_=LNx[:, j * T:(j + 1) * T]), inc="gin", n=16)

        if phase >= 2:
            a2 = contextlib.ExitStack(); stacks.append(a2); stacks.append(a2)
            wvb = [sbt(a2, f"wvb{i}", [128, NCH * 512], BF16) for i in range(2)]
            relw = [[], []]
            relv4 = [[("v", P.cnt["v"])] for _ in range(4)]
            for fb in range(4):
                bi = fb % 2
                P.waits("gpsimd", relw[bi])
                for ci in range(NCH):
                    P.do("gpsimd", lambda E, S, fb=fb, ci=ci, b=bi:
                         E.dma_start(out=wvb[b][:, ci * 512:(ci + 1) * 512], in_=wvT[ci, :, fb * 512:(fb + 1) * 512]), inc=f"g{bi}", n=16)
                P.wait("tensor", f"g{bi}")
                for ti in range(NCH):
                    pst = PS4[ti % 4]
                    P.waits("tensor", relv4[ti % 4])
                    for ci in range(NCH):
                        P.do("tensor", lambda E, S, ti=ti, ci=ci, b=bi, p=pst:
                             E.matmul(p[:, :], LNx[:, ci * T + ti * 128:ci * T + (ti + 1) * 128], wvb[b][:, ci * 512:(ci + 1) * 512],
                                      start=(ci == 0), stop=(ci == NCH - 1)), inc="t" if ci == NCH - 1 else None)
                    P.wait("vector", "t")
                    P.do("vector", lambda E, S, ti=ti, fb=fb, p=pst:
                         E.tensor_copy(V[:, ti * T + fb * 512:ti * T + (fb + 1) * 512], p[:, :]), inc="v")
                    relv4[ti % 4] = [("v", P.cnt["v"])]
                relw[bi] = [("t", P.cnt["t"])]
            a2.close()
            P.barrier()
            if phase == 2:
                for ti in range(NCH):
                    P.do("gpsimd", lambda E, S, ti=ti: E.dma_start(out=dbg[ti * 128:(ti + 1) * 128, :], in_=V[:, ti * T:(ti + 1) * T]), inc="gin", n=16)

        if phase >= 3:
            a3 = contextlib.ExitStack(); stacks.append(a3)
            rkcS = sbt(a3, "rkcS", [128, T])
            rksS = sbt(a3, "rksS", [128, T])
            emask = sbt(a3, "emask_s", [128, max(MASK_SLOTS, 1) * 128], BF16)
            wkq = [sbt(a3, f"wkq{i}", [128, C], BF16) for i in range(2)]
            krot2 = [sbt(a3, f"krot{i}", [128, T], BF16) for i in range(2)]
            qhat2 = [sbt(a3, f"qhat{i}", [128, NQ], BF16) for i in range(2)]
            ksc2 = [sbt(a3, f"ksc{i}", [128, 16]) for i in range(2)]
            qrot = sbt(a3, "qrot", [128, NQ], BF16)
            sqt = sbt(a3, "sqt", [128, T], BF16)
            tmpf = sbt(a3, "tmpf", [128, 1024], BF16)
            rSb = sbt(a3, "rSb", [128, NQ])
            tshuf = rSb
            ebuf = [sbt(a3, f"ebuf{i}", [128, 512], BF16) for i in range(2)]
            aout1 = [sbt(a3, f"aout1_{i}", [128, NQ], BF16) for i in range(2)]
            for src, dst in ((rkc, rkcS), (rks, rksS)):
                P.do("sync", lambda E, S, s_=src, d_=dst: E.dma_start(out=d_[:], in_=s_), inc="din", n=16)
            if MASK_SLOTS > 0:
                P.do("sync", lambda E, S: E.dma_start(out=emask[:], in_=emaskD), inc="din", n=16)
            P.wait("vector", "din")
            P.wait("scalar", "din")
            rel_ao = [[], []]
            rel_wkq = [[], []]
            LAST_KT = {0: max(kt for kt in range(16) if NA_U[kt] > 0),
                       1: max(kt for kt in range(16) if NB_U[kt] > 0)}
            ITERS = [(kt, b_, PROFS[b_][kt]) for kt in range(16) for b_ in (0, 1) if PROFS[b_][kt] > 0]

            def make_prep(h):
                par = h % 2
                krot, ksc, qhat = krot2[par], ksc2[par], qhat2[par]
                steps = []

                def s_loadk():
                    P.waits("gpsimd", rel_wkq[par])
                    P.do("gpsimd", lambda E, S, h=h, b=par: E.dma_start(out=wkq[b][:], in_=wkT[h]), inc=f"g{par}", n=16)
                steps.append(s_loadk)

                def mk_kprod(half):
                    def f():
                        P.wait("tensor", f"g{par}")
                        P.wait("tensor", "v", P.cnt["v"])
                        for ci in range(NCH):
                            for q_, p in ((0, psC), (1, psD)):
                                tb = 2 * half + q_
                                P.do("tensor", lambda E, S, ci=ci, tb=tb, b=par, p=p:
                                     E.matmul(p[:, :], wkq[b][:, ci * 128:(ci + 1) * 128],
                                              LNx[:, ci * T + tb * 512:ci * T + (tb + 1) * 512],
                                              start=(ci == 0), stop=(ci == NCH - 1)),
                                     inc="t" if (ci == NCH - 1 and q_ == 1) else None)
                    return f

                def mk_krope(half):
                    def f():
                        P.wait("vector", "t")
                        for q_, p in ((0, psC), (1, psD)):
                            tb = 2 * half + q_
                            sl = slice(tb * 512, (tb + 1) * 512)
                            P.do("vector", lambda E, S, p=p, sl=sl: E.tensor_mul(tmpf[:, 0:512], p[:, :], rkcS[:, sl]))
                            P.do("vector", lambda E, S, p=p: E.stream_shuffle(tshuf[:, 0:512], p[:, :], SHUF))
                            P.do("vector", lambda E, S, sl=sl: E.tensor_mul(tmpf[:, 512:1024], tshuf[:, 0:512], rksS[:, sl]))
                            P.do("vector", lambda E, S, sl=sl, kr=krot: E.tensor_add(kr[:, sl], tmpf[:, 0:512], tmpf[:, 512:1024]), inc="v")
                    return f

                for half in (0, 1):
                    steps.append(mk_kprod(half))
                    steps.append(mk_krope(half))

                def s_ksq():
                    P.wait("scalar", "v")
                    P.do("scalar", lambda E, S, kr=krot: E.activation(sqt[:, 0:T], kr[:, 0:T], AF.Square), inc="s")
                steps.append(s_ksq)

                def mk_ksum(r):
                    def f():
                        P.wait("tensor", "s")
                        P.wait("tensor", "v", P.cnt["v"])
                        for q_, p in ((0, psC), (1, psD)):
                            kt = 2 * r + q_
                            P.do("tensor", lambda E, S, kt=kt, p=p:
                                 E.matmul(p[:, 0:1], sqt[:, kt * 128:(kt + 1) * 128], onescol[:], start=True, stop=True),
                                 inc="t" if q_ == 1 else None)
                        P.wait("vector", "t")
                        for q_, p in ((0, psC), (1, psD)):
                            kt = 2 * r + q_
                            P.do("vector", lambda E, S, kt=kt, p=p, k=ksc: E.tensor_copy(k[:, kt:kt + 1], p[:, 0:1]), inc="v")
                    return f

                for r in range(8):
                    steps.append(mk_ksum(r))

                def s_kscrow():
                    P.wait("scalar", "v")
                    P.do("scalar", lambda E, S, k=ksc: E.activation(k[:, 0:16], k[:, 0:16], AF.Sqrt, bias=epsT[:, 1:2], scale=1.0 / 128.0), inc="s")
                    P.wait("vector", "s")
                    P.do("vector", lambda E, S, k=ksc: E.reciprocal(k[:, 0:16], k[:, 0:16]))
                    P.do("vector", lambda E, S, k=ksc: E.tensor_scalar_mul(k[:, 0:16], k[:, 0:16], SCALE), inc="v")
                    P.wait("gpsimd", "t", P.cnt["t"])
                    P.do("gpsimd", lambda E, S, h=h, b=1 - par: E.dma_start(out=wkq[b][:], in_=wqT[h]), inc=f"g{1 - par}", n=16)
                steps.append(s_kscrow)

                def s_qprod():
                    P.wait("tensor", f"g{1 - par}")
                    P.wait("tensor", "v", P.cnt["v"])
                    for ci in range(NCH):
                        for q_, p in ((0, psC), (1, psD)):
                            P.do("tensor", lambda E, S, ci=ci, tb=q_, b=1 - par, p=p:
                                 E.matmul(p[:, :], wkq[b][:, ci * 128:(ci + 1) * 128],
                                          LNx[:, ci * T + tb * 512:ci * T + (tb + 1) * 512],
                                          start=(ci == 0), stop=(ci == NCH - 1)),
                                 inc="t" if (ci == NCH - 1 and q_ == 1) else None)
                    rel_wkq[1 - par] = [("t", P.cnt["t"])]
                steps.append(s_qprod)

                def s_qrope():
                    P.wait("vector", "t")
                    for q_, p in ((0, psC), (1, psD)):
                        sl = slice(q_ * 512, (q_ + 1) * 512)
                        P.do("vector", lambda E, S, p=p, sl=sl: E.tensor_mul(tmpf[:, 0:512], p[:, :], rkcS[:, sl]))
                        P.do("vector", lambda E, S, p=p: E.stream_shuffle(tshuf[:, 0:512], p[:, :], SHUF))
                        P.do("vector", lambda E, S, sl=sl: E.tensor_mul(tmpf[:, 512:1024], tshuf[:, 0:512], rksS[:, sl]))
                        P.do("vector", lambda E, S, sl=sl: E.tensor_add(qrot[:, sl], tmpf[:, 0:512], tmpf[:, 512:1024]), inc="v")
                    P.wait("scalar", "v")
                    P.do("scalar", lambda E, S: E.activation(sqt[:, 0:NQ], qrot[:, 0:NQ], AF.Square), inc="s")
                steps.append(s_qrope)

                def s_qstats():
                    P.wait("tensor", "s")
                    P.wait("tensor", "v", P.cnt["v"])
                    for q_, p in ((0, psC), (1, psD)):
                        P.do("tensor", lambda E, S, tb=q_, p=p:
                             E.matmul(p[0:1, :], onescol[:], sqt[:, tb * 512:(tb + 1) * 512], start=True, stop=True),
                             inc="t" if q_ == 1 else None)
                    P.wait("vector", "t")
                    P.do("vector", lambda E, S: E.tensor_copy(rowB[0:1, 0:512], psC[0:1, :]))
                    P.do("vector", lambda E, S: E.tensor_copy(rowB[0:1, 512:1024], psD[0:1, :]), inc="v")
                    P.wait("scalar", "v")
                    P.do("scalar", lambda E, S: E.activation(rowB[0:1, 0:NQ], rowB[0:1, 0:NQ], AF.Sqrt, bias=epsT[0:1, 1:2], scale=1.0 / 128.0), inc="s")
                    P.wait("vector", "s")
                    P.do("vector", lambda E, S: E.reciprocal(rowB[0:1, 0:NQ], rowB[0:1, 0:NQ]), inc="v")
                steps.append(s_qstats)

                def s_qhat():
                    P.wait("tensor", "v")
                    P.do("tensor", lambda E, S: E.matmul(psC[:, :], onesrow[:], rowB[0:1, 0:512], start=True, stop=True))
                    P.do("tensor", lambda E, S: E.matmul(psD[:, :], onesrow[:], rowB[0:1, 512:1024], start=True, stop=True), inc="t")
                    P.wait("vector", "t")
                    for q_, p in ((0, psC), (1, psD)):
                        P.do("vector", lambda E, S, tb=q_, p=p, q=qhat:
                             E.tensor_mul(q[:, tb * 512:(tb + 1) * 512], qrot[:, tb * 512:(tb + 1) * 512], p[:, :]),
                             inc="v" if q_ == 1 else None)
                steps.append(s_qhat)
                return steps

            def make_epi(h):
                ao = aout1[h % 2]
                epi_rel = {}

                def e1():
                    P.wait("tensor", "v", P.cnt["v"])
                    P.do("tensor", lambda E, S: E.matmul(psC[:, :], onesrow[:], rowA[0:1, 0:512], start=True, stop=True))
                    P.do("tensor", lambda E, S: E.matmul(psD[:, :], onesrow[:], rowA[0:1, 512:1024], start=True, stop=True), inc="t")

                def e2():
                    P.wait("vector", "t")
                    P.waits("vector", rel_ao[h % 2])
                    for tb in range(2):
                        P.do("vector", lambda E, S, tb=tb, p=[psC, psD][tb]: E.reciprocal(rSb[:, tb * 512:(tb + 1) * 512], p[:, :]))
                        P.do("vector", lambda E, S, tb=tb, pv=[psE, psF][tb], a=ao:
                             E.tensor_mul(a[:, tb * 512:(tb + 1) * 512], pv[:, :], rSb[:, tb * 512:(tb + 1) * 512]), inc="v" if tb == 1 else None)
                    epi_rel["r"] = [("v", P.cnt["v"])]
                    P.wait("sync", "v")
                    P.do("sync", lambda E, S, h=h, a=ao: E.dma_start(out=aoscr[h * 128:(h + 1) * 128, :], in_=a[:]), inc=f"spl{h % 2}", n=16)
                    if phase == 3:
                        P.wait("gpsimd", "v")
                        P.do("gpsimd", lambda E, S, h=h, a=ao: E.dma_start(out=dbg[h * 128:(h + 1) * 128, 0:NQ], in_=a[:]), inc=f"gin{h % 2}", n=16)
                        rel_ao[h % 2] = [(f"spl{h % 2}", P.cnt[f"spl{h % 2}"]), (f"gin{h % 2}", P.cnt[f"gin{h % 2}"])]
                    else:
                        rel_ao[h % 2] = [(f"spl{h % 2}", P.cnt[f"spl{h % 2}"])]
                return [e1, e2], epi_rel

            for st_ in make_prep(0):
                st_()
            prep_rel = [("t", P.cnt["t"]), ("v", P.cnt["v"]), ("s", P.cnt["s"])]
            pend_epi = None

            for h in range(NH):
                par = h % 2
                krot, ksc, qhat = krot2[par], ksc2[par], qhat2[par]
                steps = list(pend_epi[0]) if pend_epi else []
                epi_rel_prev = pend_epi[1] if pend_epi else None
                steps += make_prep(h + 1) if h + 1 < NH else []
                si = 0
                P.waits("tensor", prep_rel)
                P.waits("scalar", prep_rel)
                P.wait("tensor", "s", P.cnt["s"])
                exp_done = []
                ready = []

                def emit_scores(i):
                    kt, b_, nU = ITERS[i]
                    qc0 = b_ * 512 + (4 - nU) * 128
                    ncols = nU * 128
                    psc = [psA, psB][i % 2]
                    eb = ebuf[i % 2]
                    if i >= 2:
                        P.wait("tensor", "s", exp_done[i - 2])
                    P.do("tensor", lambda E, S, kt=kt, qc0=qc0, n=ncols, p=psc, kr=krot, q=qhat:
                         E.matmul(p[:, 0:n], kr[:, kt * 128:(kt + 1) * 128], q[:, qc0:qc0 + n], start=True, stop=True), inc="t")
                    P.wait("scalar", "t")
                    P.do("scalar", lambda E, S, kt=kt, n=ncols, p=psc, e=eb, k=ksc:
                         E.activation(e[:, 0:n], p[:, 0:n], AF.Exp, scale=k[:, kt:kt + 1]), inc="s")
                    exp_done.append(P.cnt["s"])
                    mo = MASK_OFF.get((kt, b_))
                    if mo is not None:
                        o0, L = mo
                        P.wait("vector", "s")
                        P.do("vector", lambda E, S, e=eb, o0=o0, L=L:
                             E.tensor_mul(e[:, 0:L * 128], e[:, 0:L * 128], emask[:, o0 * 128:(o0 + L) * 128]), inc="v")
                        ready.append([("v", P.cnt["v"])])
                    else:
                        ready.append([("s", P.cnt["s"])])

                def emit_sumpv(i):
                    kt, b_, nU = ITERS[i]
                    qc0 = b_ * 512 + (4 - nU) * 128
                    ncols = nU * 128
                    eb = ebuf[i % 2]
                    first, last = (kt == 0), (kt == LAST_KT[b_])
                    if first and epi_rel_prev is not None:
                        P.waits("tensor", epi_rel_prev.get("r", []))
                    P.waits("tensor", ready[i])
                    P.do("tensor", lambda E, S, qc0=qc0, n=ncols, e=eb, f=first, l=last:
                         E.matmul(psS[0:1, qc0:qc0 + n], onescol[:], e[:, 0:n], start=f, stop=l))
                    P.do("tensor", lambda E, S, kt=kt, b_=b_, qc0=qc0, n=ncols, e=eb, f=first, l=last, h=h:
                         E.matmul([psE, psF][b_][:, qc0 - b_ * 512:qc0 - b_ * 512 + n],
                                  V[:, kt * T + h * 128:kt * T + (h + 1) * 128], e[:, 0:n], start=f, stop=l), inc="t")

                epi_n = 2 if pend_epi else 0
                for i in range(len(ITERS) + 1):
                    if i < len(ITERS):
                        emit_scores(i)
                    if i == 0:
                        while si < epi_n:
                            steps[si]()
                            si += 1
                    if i >= 1:
                        emit_sumpv(i - 1)
                    want = ((i + 1) * len(steps)) // (len(ITERS) + 1)
                    while si < want:
                        steps[si]()
                        si += 1
                while si < len(steps):
                    steps[si]()
                    si += 1
                prep_rel = [("t", P.cnt["t"]), ("v", P.cnt["v"]), ("s", P.cnt["s"])]
                P.wait("vector", "t")
                P.do("vector", lambda E, S: E.tensor_copy(rowA[0:1, 0:NQ], psS[0:1, 0:1024]), inc="v")
                pend_epi = make_epi(h)
            for st_ in pend_epi[0]:
                st_()
            a3.close()
            P.barrier()

        if phase >= 4:
            a1.close()
            a4o = contextlib.ExitStack(); stacks.append(a4o); stacks.append(a4o)
            xqS = sbt(a4o, "xqS", [128, NCH * NQ], BF16)
            a4m = contextlib.ExitStack(); stacks.append(a4m); stacks.append(a4m)
            x2S = sbt(a4m, "x2S", [128, NCH * NQ])
            a4 = contextlib.ExitStack(); stacks.append(a4); stacks.append(a4)
            aoutS = sbt(a4, "aoutS", [128, NCH * NQ], BF16)
            wob = [sbt(a4, f"wob{i}", [128, C], BF16) for i in range(2)]
            abf = sbt(a4, "abf", [128, NQ], BF16)
            asq = sbt(a4, "asq", [128, NQ], BF16)
            xoc = [sbt(a4, f"xoc{i}", [128, NQ]) for i in range(2)]
            for j in range(NCH):
                P.do("gpsimd", lambda E, S, j=j: E.dma_start(out=aoutS[:, j * NQ:(j + 1) * NQ], in_=aoscr[j * 128:(j + 1) * 128, :]), inc="gin", n=16)
            relw = [[], []]
            relpp = [[("v", P.cnt["v"])], [("v", P.cnt["v"])]]
            for fc in range(NCH):
                bi = fc % 2
                pp = [psA, psB] if bi == 0 else [psC, psD]
                P.waits("gpsimd", relw[bi])
                P.do("gpsimd", lambda E, S, fc=fc, b=bi: E.dma_start(out=wob[b][:], in_=woT[fc]), inc=f"g{bi}", n=16)
                P.wait("tensor", f"g{bi}")
                P.wait("tensor", "gin")
                P.waits("tensor", relpp[bi])
                for ci in range(NCH):
                    for tb in range(2):
                        P.do("tensor", lambda E, S, ci=ci, tb=tb, b=bi, p=pp[tb]:
                             E.matmul(p[:, :], wob[b][:, ci * 128:(ci + 1) * 128], aoutS[:, ci * NQ + tb * 512:ci * NQ + (tb + 1) * 512],
                                      start=(ci == 0), stop=(ci == NCH - 1)), inc="t" if (ci == NCH - 1 and tb == 1) else None)
                relw[bi] = [("t", P.cnt["t"])]
                P.wait("vector", "t")
                for tb in range(2):
                    P.do("vector", lambda E, S, fc=fc, tb=tb, p=pp[tb]:
                         E.tensor_copy(x2S[:, fc * NQ + tb * 512:fc * NQ + (tb + 1) * 512], p[:, :]), inc="v" if tb == 1 else None)
                relpp[bi] = [("v", P.cnt["v"])]
                P.do("vector", lambda E, S, fc=fc: E.tensor_copy(abf[:], x2S[:, fc * NQ:(fc + 1) * NQ]), inc="v")
                P.wait("scalar", "v")
                P.do("scalar", lambda E, S: E.activation(asq[:], abf[:], AF.Square), inc="s")
                P.wait("tensor", "v")
                P.wait("tensor", "s")
                for tb in range(2):
                    P.do("tensor", lambda E, S, fc=fc, tb=tb, tg=[psS[0:1, 0:512], psS[0:1, 512:1024]][tb]:
                         E.matmul(tg, onescol[:], abf[:, tb * 512:(tb + 1) * 512], start=(fc == 0), stop=(fc == NCH - 1)))
                for tb in range(2):
                    P.do("tensor", lambda E, S, fc=fc, tb=tb, tg=[psE[0:1, :], psF[0:1, :]][tb]:
                         E.matmul(tg, onescol[:], asq[:, tb * 512:(tb + 1) * 512], start=(fc == 0), stop=(fc == NCH - 1)),
                         inc="t" if tb == 1 else None)
            P.wait("vector", "t")
            ln_rows(NQ, 0, [(psS[0:1, 0:512], 0, 512), (psS[0:1, 512:1024], 512, 512)],
                    [(psE[0:1, :], 0, 512), (psF[0:1, :], 512, 512)], rowA, rowB)
            P.wait("tensor", "v")
            bcast(rowA, NQ, [psA, psB])
            bcast(rowB, NQ, [psC, psD])
            P.wait("vector", "t")
            relx = [[], []]
            for j in range(NCH):
                bi = j % 2
                P.waits("sync", relx[bi])
                P.do("sync", lambda E, S, j=j, b=bi: E.dma_start(out=xoc[b][:], in_=xown[j * 128:(j + 1) * 128, :]), inc=f"xb{bi}", n=16)
                P.wait("vector", f"xb{bi}")
                for tb in range(2):
                    sl = slice(j * NQ + tb * 512, j * NQ + (tb + 1) * 512)
                    P.do("vector", lambda E, S, sl=sl, p=[psA, psB][tb]: E.tensor_mul(x2S[:, sl], x2S[:, sl], p[:, :]))
                    P.do("vector", lambda E, S, sl=sl, p=[psC, psD][tb]: E.tensor_add(x2S[:, sl], x2S[:, sl], p[:, :]))
                    P.do("vector", lambda E, S, sl=sl, b=bi, tb=tb:
                         E.tensor_add(x2S[:, sl], x2S[:, sl], xoc[b][:, tb * 512:(tb + 1) * 512]), inc="v")
                relx[bi] = [("v", P.cnt["v"])]
                P.wait("sync", "v")
                P.do("sync", lambda E, S, j=j: E.dma_start(out=x2scr[j * 128:(j + 1) * 128, :], in_=x2S[:, j * NQ:(j + 1) * NQ]), inc="sx2", n=16)
            a4.close()
            P.barrier()
            if phase == 4:
                for j in range(NCH):
                    P.do("gpsimd", lambda E, S, j=j: E.dma_start(out=dbg[j * 128:(j + 1) * 128, 0:NQ], in_=x2S[:, j * NQ:(j + 1) * NQ]), inc="gin", n=16)

        if phase >= 5:
            a5 = contextlib.ExitStack(); stacks.append(a5); stacks.append(a5)
            RB2 = sbt(a5, "RB2", [128, 2 * NQ])
            M3s = sbt(a5, "M3s", [128, NQ])
            Abc = sbt(a5, "Abc", [128, NQ])
            ybuf = sbt(a5, "ybuf", [128, NQ])
            yabs = sbt(a5, "yabs", [128, NQ])
            macc = sbt(a5, "macc", [128, NQ])
            tq = sbt(a5, "tq", [128, NQ])
            cf = sbt(a5, "cf", [128, NQ])
            b1t = sbt(a5, "b1t", [128, NQ])
            xi = sbt(a5, "xi", [128, NQ], I32)
            qbf = sbt(a5, "qbf", [128, NQ], BF16)
            qsq = sbt(a5, "qsq", [128, NQ], BF16)
            tcol = sbt(a5, "tcol", [128, 8])

            def stats_from(src_emit):
                for j in range(NCH):
                    src_emit(j)
                    P.wait("tensor", "v")
                    P.wait("tensor", "s")
                    for tb in range(2):
                        P.do("tensor", lambda E, S, j=j, tb=tb, tg=[psS[0:1, 0:512], psS[0:1, 512:1024]][tb]:
                             E.matmul(tg, onescol[:], qbf[:, tb * 512:(tb + 1) * 512], start=(j == 0), stop=(j == NCH - 1)))
                    for tb in range(2):
                        P.do("tensor", lambda E, S, j=j, tb=tb, tg=[psE[0:1, :], psF[0:1, :]][tb]:
                             E.matmul(tg, onescol[:], qsq[:, tb * 512:(tb + 1) * 512], start=(j == 0), stop=(j == NCH - 1)),
                             inc="t" if tb == 1 else None)
                P.wait("vector", "t")

            def src_x2(j):
                P.wait("vector", "t", P.cnt["t"])
                P.do("vector", lambda E, S, j=j: E.tensor_copy(qbf[:], x2S[:, j * NQ:(j + 1) * NQ]), inc="v")
                P.wait("scalar", "v")
                P.do("scalar", lambda E, S: E.activation(qsq[:], qbf[:], AF.Square), inc="s")

            stats_from(src_x2)
            ln_rows(NQ, 0, [(psS[0:1, 0:512], 0, 512), (psS[0:1, 512:1024], 512, 512)],
                    [(psE[0:1, :], 0, 512), (psF[0:1, :], 512, 512)], rowA, rowB)
            P.wait("tensor", "v")
            bcast(rowA, NQ, [psA, psB])
            bcast(rowB, NQ, [psC, psD])
            P.wait("vector", "t")
            for tb in range(2):
                P.do("vector", lambda E, S, tb=tb, p=[psA, psB][tb]: E.tensor_copy(RB2[:, tb * 512:(tb + 1) * 512], p[:, :]))
                P.do("vector", lambda E, S, tb=tb, p=[psC, psD][tb]: E.tensor_copy(RB2[:, NQ + tb * 512:NQ + (tb + 1) * 512], p[:, :]), inc="v" if tb == 1 else None)

            P.wait("vector", "sx2")
            for j in range(NCH):
                P.do("vector", lambda E, S, j=j: E.tensor_mul(x2S[:, j * NQ:(j + 1) * NQ], x2S[:, j * NQ:(j + 1) * NQ], RB2[:, 0:NQ]))
                P.do("vector", lambda E, S, j=j: E.tensor_add(x2S[:, j * NQ:(j + 1) * NQ], x2S[:, j * NQ:(j + 1) * NQ], RB2[:, NQ:2 * NQ]), inc="v")

            def src_y(j):
                P.wait("vector", "t", P.cnt["t"])
                P.do("vector", lambda E, S, j=j: E.tensor_copy(qbf[:], x2S[:, j * NQ:(j + 1) * NQ]), inc="v")
                P.wait("scalar", "v")
                P.do("scalar", lambda E, S: E.activation(qsq[:], qbf[:], AF.Square), inc="s")

            stats_from(src_y)
            P.do("vector", lambda E, S: E.tensor_scalar_mul(rowB[0:1, 0:NQ], psS[0:1, 0:1024], 1.0 / C))
            P.do("vector", lambda E, S: E.tensor_scalar_mul(rowA[0:1, 0:512], psE[0:1, :], 1.0 / C))
            P.do("vector", lambda E, S: E.tensor_scalar_mul(rowA[0:1, 512:1024], psF[0:1, :], 1.0 / C))
            P.do("vector", lambda E, S: E.tensor_mul(psE[0:1, :], rowB[0:1, 0:512], rowB[0:1, 0:512]))
            P.do("vector", lambda E, S: E.tensor_mul(psF[0:1, :], rowB[0:1, 512:1024], rowB[0:1, 512:1024]))
            P.do("vector", lambda E, S: E.tensor_sub(rowA[0:1, 0:512], rowA[0:1, 0:512], psE[0:1, :]))
            P.do("vector", lambda E, S: E.tensor_sub(rowA[0:1, 512:1024], rowA[0:1, 512:1024], psF[0:1, :]), inc="v")
            P.wait("scalar", "v")
            P.do("scalar", lambda E, S: E.activation(rowA[0:1, 0:NQ], rowA[0:1, 0:NQ], AF.Sqrt, bias=epsT[0:1, 2:3]), inc="s")
            P.wait("vector", "s")
            P.do("vector", lambda E, S: E.reciprocal(rowA[0:1, 0:NQ], rowA[0:1, 0:NQ]), inc="v")
            P.wait("tensor", "v")
            bcast(rowB, NQ, [psA, psB])
            P.wait("vector", "t")
            for tb in range(2):
                P.do("vector", lambda E, S, tb=tb, p=[psA, psB][tb]:
                     E.tensor_copy(M3s[:, tb * 512:(tb + 1) * 512], p[:, :]), inc="v" if tb == 1 else None)
            for j in range(NCH):
                P.do("vector", lambda E, S, j=j: E.tensor_sub(ybuf[:], x2S[:, j * NQ:(j + 1) * NQ], M3s[:]), inc="v")
                P.wait("scalar", "v")
                P.do("scalar", lambda E, S: E.activation(yabs[:], ybuf[:], AF.Abs), inc="s")
                P.wait("vector", "s")
                if j == 0:
                    P.do("vector", lambda E, S: E.tensor_copy(macc[:], yabs[:]), inc="v")
                else:
                    P.do("vector", lambda E, S: E.tensor_max(macc[:], macc[:], yabs[:]), inc="v")
            P.wait("tensor", "v")
            for jb in range(8):
                p = [psA, psB][jb % 2]
                P.wait("tensor", "v", P.cnt["v"])
                P.do("tensor", lambda E, S, jb=jb, p=p:
                     E.transpose(p[:, 0:128], macc[:, jb * 128:(jb + 1) * 128], ident[:]), inc="t")
                P.wait("vector", "t")
                P.do("vector", lambda E, S, jb=jb, p=p:
                     E.reduce_max(tcol[:, jb:jb + 1], p[:, 0:128], axis=mybir.AxisListType.X), inc="v")
            P.wait("tensor", "v")
            for jb in range(8):
                p = [psC, psD][jb % 2]
                P.wait("tensor", "v", P.cnt["v"])
                P.do("tensor", lambda E, S, jb=jb, p=p:
                     E.matmul(p[0:1, 0:128], tcol[:, jb:jb + 1], ident[:], start=True, stop=True), inc="t")
                P.wait("vector", "t")
                P.do("vector", lambda E, S, jb=jb, p=p:
                     E.tensor_copy(rowB[0:1, jb * 128:(jb + 1) * 128], p[0:1, 0:128]), inc="v")
            P.do("vector", lambda E, S: E.tensor_mul(rowA[0:1, 0:NQ], rowA[0:1, 0:NQ], rowB[0:1, 0:NQ]))
            P.do("vector", lambda E, S: E.tensor_scalar_mul(rowA[0:1, 0:NQ], rowA[0:1, 0:NQ], 1.0 / 127.0))
            P.do("vector", lambda E, S: E.reciprocal(rowB[0:1, 0:NQ], rowB[0:1, 0:NQ]))
            P.do("vector", lambda E, S: E.tensor_scalar_mul(rowB[0:1, 0:NQ], rowB[0:1, 0:NQ], 127.0), inc="v")
            P.wait("tensor", "v")
            bcast(rowB, NQ, [psA, psB])
            bcast(rowA, NQ, [psC, psD])
            P.wait("vector", "t")
            for tb in range(2):
                P.do("vector", lambda E, S, tb=tb, p=[psA, psB][tb]: E.tensor_copy(Abc[:, tb * 512:(tb + 1) * 512], p[:, :]))
                P.do("vector", lambda E, S, tb=tb, p=[psC, psD][tb]: E.tensor_copy(invsbc[:, tb * 512:(tb + 1) * 512], p[:, :]), inc="v" if tb == 1 else None)
            for j in range(NCH):
                P.do("vector", lambda E, S, j=j: E.tensor_sub(ybuf[:], x2S[:, j * NQ:(j + 1) * NQ], M3s[:]))
                P.do("vector", lambda E, S: E.tensor_mul(tq[:], ybuf[:], Abc[:]), inc="v")
                P.wait("scalar", "v")
                P.do("scalar", lambda E, S: E.copy(xi[:], tq[:]))
                P.do("scalar", lambda E, S: E.copy(cf[:], xi[:]), inc="s")
                P.wait("vector", "s")
                P.do("vector", lambda E, S: E.tensor_sub(ybuf[:], tq[:], cf[:]))
                P.do("vector", lambda E, S: E.tensor_single_scalar(b1t[:], ybuf[:], 0.5, ALU.is_ge))
                P.do("vector", lambda E, S: E.tensor_single_scalar(ybuf[:], ybuf[:], -0.5, ALU.is_lt))
                P.do("vector", lambda E, S: E.tensor_add(cf[:], cf[:], b1t[:]))
                P.do("vector", lambda E, S: E.tensor_sub(cf[:], cf[:], ybuf[:]))
                P.do("vector", lambda E, S, j=j: E.tensor_scalar(xqS[:, j * NQ:(j + 1) * NQ], cf[:], -128.0, 127.0, ALU.max, ALU.min), inc="v")
            a5.close()
            a4m.close()
            P.barrier()
            if phase == 5:
                for j in range(NCH):
                    P.do("gpsimd", lambda E, S, j=j: E.dma_start(out=dbg[j * 128:(j + 1) * 128, 0:NQ], in_=xqS[:, j * NQ:(j + 1) * NQ]), inc="gin", n=16)
                P.do("gpsimd", lambda E, S: E.dma_start(out=dbg[0:128, NQ:2 * NQ], in_=invsbc[:]), inc="gin", n=16)

        if phase >= 6:
            a6 = contextlib.ExitStack(); stacks.append(a6); stacks.append(a6)
            pS = sbt(a6, "pS", [128, HC * 512], BF16)
            wgb = [sbt(a6, f"wgb{i}", [128, C], BF16) for i in range(2)]
            wub = [sbt(a6, f"wub{i}", [128, C], BF16) for i in range(2)]
            gt0 = sbt(a6, "gt0", [128, 512])
            gt1 = sbt(a6, "gt1", [128, 512])
            ffS = sbt(a6, "ffS", [128, NCH * 512])
            fbf = sbt(a6, "fbf", [128, 512], BF16)
            fsq = sbt(a6, "fsq", [128, 512], BF16)
            wdb = sbt(a6, "wdb", [128, 8192], BF16)
            x2c = [sbt(a6, f"x2c{i}", [128, 512]) for i in range(2)]
            RB4 = sbt(a6, "RB4", [128, 1024])
            for tb in range(2):
                tsl = slice(tb * 512, (tb + 1) * 512)
                relw = [[("s", P.cnt["s"]), ("t", P.cnt["t"])] for _ in range(2)]
                relgu = [[("v", P.cnt["v"])], [("v", P.cnt["v"])]]
                sg_done = [(0, 0, 0)] * HC

                def gu_load_sign(hc):
                    bi = hc % 2
                    P.waits("gpsimd", relw[bi])
                    if tb == 0:
                        P.do("gpsimd", lambda E, S, hc=hc, b=bi: E.dma_start(out=wgb[b][:], in_=wgT[hc]), inc=f"g{bi}", n=16)
                        P.do("gpsimd", lambda E, S, hc=hc, b=bi: E.dma_start(out=wub[b][:], in_=wuT[hc]), inc=f"g{bi}", n=16)
                        P.wait("scalar", f"g{bi}")
                        P.do("scalar", lambda E, S, b=bi: E.sign(wgb[b][:], wgb[b][:]))
                        P.do("scalar", lambda E, S, b=bi: E.sign(wub[b][:], wub[b][:]), inc="s")
                        P.wait("sync", "s")
                        P.do("sync", lambda E, S, hc=hc, b=bi: E.dma_start(out=sgg[hc], in_=wgb[b][:]), inc=f"sw{bi}", n=16)
                        P.do("sync", lambda E, S, hc=hc, b=bi: E.dma_start(out=sgu[hc], in_=wub[b][:]), inc=f"sw{bi}", n=16)
                    else:
                        P.do("gpsimd", lambda E, S, hc=hc, b=bi: E.dma_start(out=wgb[b][:], in_=sgg[hc]), inc=f"g{bi}", n=16)
                        P.do("gpsimd", lambda E, S, hc=hc, b=bi: E.dma_start(out=wub[b][:], in_=sgu[hc]), inc=f"g{bi}", n=16)
                    sg_done[hc] = (P.cnt["s"], P.cnt[f"g{bi}"], P.cnt[f"sw{bi}"])

                gu_load_sign(0)
                for hc in range(HC):
                    bi = hc % 2
                    ppG = [psA, psC][bi]
                    ppU = [psB, psD][bi]
                    if hc + 1 < HC:
                        gu_load_sign(hc + 1)
                    if tb == 0:
                        P.wait("tensor", "s", sg_done[hc][0])
                    else:
                        P.wait("tensor", f"g{bi}", sg_done[hc][1])
                    P.waits("tensor", relgu[bi])
                    for ci in range(NCH):
                        P.do("tensor", lambda E, S, ci=ci, b=bi, p=ppG, tb=tb:
                             E.matmul(p[:, :], wgb[b][:, ci * 128:(ci + 1) * 128], xqS[:, ci * NQ + tb * 512:ci * NQ + (tb + 1) * 512],
                                      start=(ci == 0), stop=(ci == NCH - 1)))
                        P.do("tensor", lambda E, S, ci=ci, b=bi, p=ppU, tb=tb:
                             E.matmul(p[:, :], wub[b][:, ci * 128:(ci + 1) * 128], xqS[:, ci * NQ + tb * 512:ci * NQ + (tb + 1) * 512],
                                      start=(ci == 0), stop=(ci == NCH - 1)), inc="t" if ci == NCH - 1 else None)
                    relw[bi] = [("t", P.cnt["t"])] + ([(f"sw{bi}", sg_done[hc][2])] if tb == 0 else [])
                    P.wait("vector", "t")
                    P.do("vector", lambda E, S, p=ppG, tsl=tsl: E.tensor_mul(gt0[:], p[:, :], invsbc[:, tsl]), inc="v")
                    P.wait("scalar", "v")
                    P.do("scalar", lambda E, S: E.activation(gt1[:], gt0[:], AF.Silu), inc="s")
                    P.wait("vector", "s")
                    P.do("vector", lambda E, S, p=ppU: E.tensor_mul(gt0[:], gt1[:], p[:, :]))
                    P.do("vector", lambda E, S, hc=hc, tsl=tsl: E.tensor_mul(pS[:, hc * 512:(hc + 1) * 512], gt0[:], invsbc[:, tsl]), inc="v")
                    relgu[bi] = [("v", P.cnt["v"])]
                P.barrier(("gpsimd", "tensor", "scalar", "vector"))
                relwd = [[("s", P.cnt["s"]), ("t", P.cnt["t"])] for _ in range(2)]
                relpw = [[("v", P.cnt["v"])], [("v", P.cnt["v"])]]
                sgdone = [(0, 0, 0)] * 2
                for fc in range(NCH):
                    pw = [psE, psF][fc % 2]
                    for hf in range(2):
                        P.waits("gpsimd", relwd[hf])
                        if tb == 0:
                            P.do("gpsimd", lambda E, S, fc=fc, hf=hf:
                                 E.dma_start(out=wdb[:, hf * 4096:(hf + 1) * 4096], in_=wdT[fc, :, hf * 4096:(hf + 1) * 4096]), inc=f"g{hf}", n=16)
                            P.wait("scalar", f"g{hf}")
                            P.do("scalar", lambda E, S, hf=hf: E.sign(wdb[:, hf * 4096:(hf + 1) * 4096], wdb[:, hf * 4096:(hf + 1) * 4096]), inc="s")
                            P.wait("sync", "s")
                            P.do("sync", lambda E, S, fc=fc, hf=hf:
                                 E.dma_start(out=sgd[fc, :, hf * 4096:(hf + 1) * 4096], in_=wdb[:, hf * 4096:(hf + 1) * 4096]), inc=f"sw{hf}", n=16)
                        else:
                            P.do("gpsimd", lambda E, S, fc=fc, hf=hf:
                                 E.dma_start(out=wdb[:, hf * 4096:(hf + 1) * 4096], in_=sgd[fc, :, hf * 4096:(hf + 1) * 4096]), inc=f"g{hf}", n=16)
                        sgdone[hf] = (P.cnt["s"], P.cnt[f"g{hf}"], P.cnt[f"sw{hf}"])
                    P.waits("tensor", relpw[fc % 2])
                    for hc in range(HC):
                        if hc == 0:
                            P.wait("tensor", "s" if tb == 0 else "g0", sgdone[0][0 if tb == 0 else 1])
                        if hc == HC // 2:
                            P.wait("tensor", "s" if tb == 0 else "g1", sgdone[1][0 if tb == 0 else 1])
                        P.do("tensor", lambda E, S, hc=hc, p=pw:
                             E.matmul(p[:, :], wdb[:, hc * 128:(hc + 1) * 128], pS[:, hc * 512:(hc + 1) * 512],
                                      start=(hc == 0), stop=(hc == HC - 1)), inc="t" if hc in (HC // 2 - 1, HC - 1) else None)
                        if hc == HC // 2 - 1:
                            relwd[0] = [("t", P.cnt["t"])] + ([("sw0", sgdone[0][2])] if tb == 0 else [])
                    relwd[1] = [("t", P.cnt["t"])] + ([("sw1", sgdone[1][2])] if tb == 0 else [])
                    P.wait("vector", "t")
                    P.do("vector", lambda E, S, fc=fc, p=pw: E.tensor_copy(ffS[:, fc * 512:(fc + 1) * 512], p[:, :]))
                    P.do("vector", lambda E, S, fc=fc: E.tensor_copy(fbf[:], ffS[:, fc * 512:(fc + 1) * 512]), inc="v")
                    relpw[fc % 2] = [("v", P.cnt["v"])]
                    P.wait("scalar", "v")
                    P.do("scalar", lambda E, S: E.activation(fsq[:], fbf[:], AF.Square), inc="s")
                    P.wait("tensor", "v")
                    P.wait("tensor", "s")
                    P.do("tensor", lambda E, S, fc=fc:
                         E.matmul(psS[0:1, 0:512], onescol[:], fbf[:], start=(fc == 0), stop=(fc == NCH - 1)))
                    P.do("tensor", lambda E, S, fc=fc:
                         E.matmul(psS[0:1, 512:1024], onescol[:], fsq[:], start=(fc == 0), stop=(fc == NCH - 1)),
                         inc="t" if fc == NCH - 1 else None)
                P.wait("vector", "t")
                ln_rows(512, 0, [(psS[0:1, 0:512], 0, 512)], [(psS[0:1, 512:1024], 0, 512)], rowA, rowB)
                P.wait("tensor", "v")
                bcast(rowA, 512, [psA])
                bcast(rowB, 512, [psB])
                P.wait("vector", "t")
                P.do("vector", lambda E, S: E.tensor_copy(RB4[:, 0:512], psA[:, :]))
                P.do("vector", lambda E, S: E.tensor_copy(RB4[:, 512:1024], psB[:, :]), inc="v")
                relx = [[], []]
                for fc in range(NCH):
                    bi = fc % 2
                    P.waits("sync", relx[bi])
                    P.do("sync", lambda E, S, fc=fc, b=bi, tb=tb:
                         E.dma_start(out=x2c[b][:], in_=x2scr[fc * 128:(fc + 1) * 128, tb * 512:(tb + 1) * 512]), inc=f"xb{bi}", n=16)
                    P.wait("vector", f"xb{bi}")
                    P.do("vector", lambda E, S, fc=fc: E.tensor_mul(gt0[:], ffS[:, fc * 512:(fc + 1) * 512], RB4[:, 0:512]))
                    P.do("vector", lambda E, S: E.tensor_add(gt0[:], gt0[:], RB4[:, 512:1024]))
                    P.do("vector", lambda E, S, b=bi: E.tensor_add(gt0[:], gt0[:], x2c[b][:]))
                    P.do("vector", lambda E, S: E.tensor_copy(gt1[:], gt0[:]), inc="v")
                    relx[bi] = [("v", P.cnt["v"])]
                    P.wait("sync", "v")
                    P.do("sync", lambda E, S, fc=fc, tb=tb:
                         E.dma_start(out=out[fc * 128:(fc + 1) * 128, tb * 512:(tb + 1) * 512], in_=gt1[:]), inc="sout", n=16)
                    P.wait("vector", "sout")
            a6.close()

        P.barrier()
        for st_ in reversed(stacks):
            st_.close()
        ops = P.ops

        @blk.sync
        def _(E):
            for f in ops["sync"]:
                f(E, sems)

        @blk.gpsimd
        def _(E):
            for f in ops["gpsimd"]:
                f(E, sems)

        @blk.vector
        def _(E):
            for f in ops["vector"]:
                f(E, sems)

        @blk.scalar
        def _(E):
            for f in ops["scalar"]:
                f(E, sems)

        @blk.tensor
        def _(E):
            for f in ops["tensor"]:
                f(E, sems)

    return nc



def _lhst_blocks(W):
    F_, Ci = W.shape
    A = np.asarray(W).reshape(F_ // 128, 128, Ci // 128, 128)
    return np.ascontiguousarray(A.transpose(0, 3, 2, 1).reshape(F_ // 128, 128, Ci))


def _rope_tables():
    import jax
    import jax.numpy as jnp
    with jax.default_device(jax.devices("cpu")[0]):
        inv = 1.0 / 10000.0 ** (jnp.arange(0, 128, 2, dtype=jnp.float32) / 128)
        t = jnp.arange(T, dtype=jnp.float32)
        fr = t[:, None] * inv[None, :]
        sin = np.asarray(jnp.repeat(jnp.sin(fr), 2, axis=-1))
        cos = np.asarray(jnp.repeat(jnp.cos(fr), 2, axis=-1))
    return sin, cos


def _own_positions(p):
    return np.concatenate([np.arange(128 * t, 128 * (t + 1)) for t in OWN[p]])


def _perm_positions(p):
    return np.concatenate([np.arange(128 * t, 128 * (t + 1)) for t in PERM[p]])


def _emask_for(p):
    tri = (np.arange(128)[:, None] <= np.arange(128)[None, :]).astype(ml_dtypes.bfloat16)
    m = np.ones((128, max(MASK_SLOTS, 1) * 128), dtype=ml_dtypes.bfloat16)
    for (ks, b_), (o0, L) in MASK_OFF.items():
        nU = PROFS[b_][ks]
        g = PERM[p][ks]
        for s_ in range(L):
            qslot = b_ * 4 + (4 - nU) + s_
            qt = PERM[p][qslot]
            sl = slice((o0 + s_) * 128, (o0 + s_ + 1) * 128)
            if qt < g:
                m[:, sl] = 0
            elif qt == g:
                m[:, sl] = tri
    return m


def make_inmaps(L0_attn_x, L0_attn_W_qkv, L0_attn_W_o, L0_ffn_W_g, L0_ffn_W_u, L0_ffn_W_d):
    x = np.asarray(L0_attn_x)
    Wqkv = np.asarray(L0_attn_W_qkv)
    shared = {
        "wqT": _lhst_blocks(Wqkv[0:2048]),
        "wkT": _lhst_blocks(Wqkv[2048:4096]),
        "wvT": np.ascontiguousarray(Wqkv[4096:6144].T.reshape(NCH, 128, C)),
        "woT": _lhst_blocks(np.asarray(L0_attn_W_o)),
        "wgT": _lhst_blocks(np.asarray(L0_ffn_W_g)),
        "wuT": _lhst_blocks(np.asarray(L0_ffn_W_u)),
        "wdT": _lhst_blocks(np.asarray(L0_ffn_W_d)),
        "ident": np.eye(128, dtype=np.float32),
    }
    sin, cos = _rope_tables()
    sgn = np.where(np.arange(128) % 2 == 0, -1.0, 1.0).astype(np.float32)
    rkc = np.ascontiguousarray(cos.T)
    rks = np.ascontiguousarray(sin.T * sgn[:, None])
    in_maps = []
    for core in range(8):
        b, p = core // 2, core % 2
        pp = _perm_positions(p)
        xt = np.ascontiguousarray(x[b].T[:, pp])
        d = dict(shared)
        d["xT"] = xt
        d["xown"] = np.ascontiguousarray(xt[:, 0:NQ])
        d["rkc"] = np.ascontiguousarray(rkc[:, pp])
        d["rks"] = np.ascontiguousarray(rks[:, pp])
        d["emask"] = _emask_for(p)
        in_maps.append(d)
    return in_maps


def kernel(L0_attn_x, L0_attn_W_qkv, L0_attn_W_o, L0_ffn_W_g, L0_ffn_W_u, L0_ffn_W_d):
    in_maps = make_inmaps(L0_attn_x, L0_attn_W_qkv, L0_attn_W_o, L0_ffn_W_g, L0_ffn_W_u, L0_ffn_W_d)
    nc = build_nc(phase=6)
    res = run_bass_kernel_spmd(nc, in_maps, core_ids=list(range(8)))
    y = np.empty((4, T, C), dtype=np.float32)
    for core in range(8):
        b, p = core // 2, core % 2
        pos = _own_positions(p)
        y[b, pos, :] = np.asarray(res.results[core]["out"]).T
    return y


# version 14
# speedup vs baseline: 1.5125x; 1.0609x over previous
import contextlib
import numpy as np
import ml_dtypes
import concourse.bass as bass
import concourse.mybir as mybir
from concourse.bass_utils import run_bass_kernel_spmd

F32 = mybir.dt.float32
BF16 = mybir.dt.bfloat16
I32 = mybir.dt.int32
AF = mybir.ActivationFunctionType
ALU = mybir.AluOpType

C = 2048
T = 2048
NH = 16
NCH = 16
NQ = 1024
HC = 64
RMS_EPS = 1.1920929e-07
LN_EPS = 1e-5
FFN_EPS = 1e-8
SCALE = float(np.float32(1.0) / np.float32(np.sqrt(np.float32(128.0))))
SHUF = [i ^ 1 for i in range(32)]

OWN = {0: [0, 3, 4, 7, 8, 11, 12, 15], 1: [1, 2, 5, 6, 9, 10, 13, 14]}
PERM = {p: OWN[p] + sorted(set(range(16)) - set(OWN[p])) for p in (0, 1)}

def _profiles():
    na, nb = [0] * 16, [0] * 16
    for ks in range(16):
        for p in (0, 1):
            g = PERM[p][ks]
            ca = sum(1 for j in range(0, 4) if PERM[p][j] >= g)
            cb = sum(1 for j in range(4, 8) if PERM[p][j] >= g)
            na[ks] = max(na[ks], ca)
            nb[ks] = max(nb[ks], cb)
    return na, nb

NA_U, NB_U = _profiles()
PROFS = (NA_U, NB_U)


def _mask_layout():
    off, o = {}, 0
    for b_ in (0, 1):
        for ks in range(16):
            nU = PROFS[b_][ks]
            if nU == 0:
                continue
            worst = 0
            for p in (0, 1):
                g = PERM[p][ks]
                qs = [PERM[p][j] for j in range(b_ * 4, b_ * 4 + 4)]
                n_p = sum(1 for t in qs if t >= g)
                worst = max(worst, (nU - n_p) + (1 if g in qs else 0))
            if worst > 0:
                off[(ks, b_)] = (o, worst)
                o += worst
    return off, o


MASK_OFF, MASK_SLOTS = _mask_layout()
for _p in (0, 1):
    for _b in (0, 1):
        _qs = [PERM[_p][j] for j in range(_b * 4, _b * 4 + 4)]
        assert all(_qs[i] < _qs[i + 1] for i in range(3))

ENGINES = ("sync", "gpsimd", "vector", "scalar", "tensor")
SEMS = ("t", "v", "s", "din", "xb0", "xb1", "spl0", "spl1", "sx2", "sout", "gout", "g0", "g1", "gin", "gin0", "gin1", "gw", "sw0", "sw1")


class Prog:
    def __init__(self):
        self.ops = {e: [] for e in ENGINES}
        self.cnt = {n: 0 for n in SEMS}

    def emit(self, eng, fn):
        self.ops[eng].append(fn)

    def wait(self, eng, sem, val=None):
        v = self.cnt[sem] if val is None else val
        if v > 0:
            self.emit(eng, lambda E, S, _s=sem, _v=v: E.wait_ge(S[_s], _v))

    def waits(self, eng, rel):
        for sem, v in rel:
            self.wait(eng, sem, v)

    def do(self, eng, fn, inc=None, n=1):
        if eng == "vector" or eng == "scalar":
            sem = "v" if eng == "vector" else "s"
            prev = self.cnt[sem]
            self.cnt[sem] = prev + 1
            def wrapped(E, S, _f=fn, _s=sem, _p=prev):
                if _p > 0:
                    E.wait_ge(S[_s], _p)
                return _f(E, S).then_inc(S[_s], 1)
            self.emit(eng, wrapped)
            return
        if inc is not None:
            self.cnt[inc] += n
            self.emit(eng, lambda E, S, _f=fn, _i=inc, _n=n: _f(E, S).then_inc(S[_i], _n))
        else:
            self.emit(eng, lambda E, S, _f=fn: _f(E, S))

    def snap(self):
        return [(s, v) for s, v in self.cnt.items() if v > 0]

    def barrier(self, engs=ENGINES):
        snap = self.snap()
        for e in engs:
            self.waits(e, snap)


def build_nc(phase=6):
    nc = bass.Bass("TRN2", target_bir_lowering=False)
    dtn = nc.dram_tensor
    xT = dtn("xT", [C, T], F32, kind="ExternalInput").ap()
    xown = dtn("xown", [C, NQ], F32, kind="ExternalInput").ap()
    wqT = dtn("wqT", [NH, 128, C], F32, kind="ExternalInput").ap()
    wkT = dtn("wkT", [NH, 128, C], F32, kind="ExternalInput").ap()
    wvT = dtn("wvT", [NCH, 128, C], F32, kind="ExternalInput").ap()
    woT = dtn("woT", [NCH, 128, C], F32, kind="ExternalInput").ap()
    wgT = dtn("wgT", [HC, 128, C], F32, kind="ExternalInput").ap()
    wuT = dtn("wuT", [HC, 128, C], F32, kind="ExternalInput").ap()
    wdT = dtn("wdT", [NCH, 128, 8192], F32, kind="ExternalInput").ap()
    rkc = dtn("rkc", [128, T], F32, kind="ExternalInput").ap()
    rks = dtn("rks", [128, T], F32, kind="ExternalInput").ap()
    emaskD = dtn("emask", [128, max(MASK_SLOTS, 1) * 128], BF16, kind="ExternalInput").ap()
    identD = dtn("ident", [128, 128], F32, kind="ExternalInput").ap()
    out = dtn("out", [C, NQ], F32, kind="ExternalOutput").ap()
    aoscr = dtn("aoscr", [C, NQ], BF16, kind="Internal").ap()
    sgg = dtn("sgg", [HC, 128, C], BF16, kind="Internal").ap()
    sgu = dtn("sgu", [HC, 128, C], BF16, kind="Internal").ap()
    sgd = dtn("sgd", [NCH, 128, 8192], BF16, kind="Internal").ap()
    x2scr = dtn("x2scr", [C, NQ], F32, kind="Internal").ap()
    dbg = dtn("dbg", [C, T], F32, kind="ExternalOutput").ap() if phase < 6 else None

    P = Prog()
    ctx = contextlib.ExitStack()
    stacks = []
    with ctx:
        def sbt(st, nm, sh, d=F32):
            return st.enter_context(nc.sbuf_tensor(nm, sh, d))
        onescol = sbt(ctx, "onescol", [128, 1], BF16)
        onesrow = sbt(ctx, "onesrow", [1, 128], F32)
        ident = sbt(ctx, "ident_s", [128, 128], F32)
        rowA = sbt(ctx, "rowA", [1, T])
        rowB = sbt(ctx, "rowB", [1, T])
        invsbc = sbt(ctx, "invsbc", [128, NQ])
        epsT = sbt(ctx, "epsT", [128, 3])
        psA = ctx.enter_context(nc.psum_tensor("psA", [128, 512], F32))
        psB = ctx.enter_context(nc.psum_tensor("psB", [128, 512], F32))
        psC = ctx.enter_context(nc.psum_tensor("psC", [128, 512], F32))
        psD = ctx.enter_context(nc.psum_tensor("psD", [128, 512], F32))
        psE = ctx.enter_context(nc.psum_tensor("psE", [128, 512], F32))
        psF = ctx.enter_context(nc.psum_tensor("psF", [128, 512], F32))
        psS = ctx.enter_context(nc.psum_tensor("psS", [1, 1024], F32))
        PS4 = [psA, psB, psC, psD]
        sems = {n: ctx.enter_context(nc.semaphore("sem_" + n)) for n in SEMS}
        blk = ctx.enter_context(nc.Block())

        P.do("sync", lambda E, S: E.dma_start(out=ident[:], in_=identD), inc="din", n=16)
        P.do("vector", lambda E, S: E.memset(onescol[:], 1.0))
        P.do("vector", lambda E, S: E.memset(onesrow[:], 1.0))
        P.do("vector", lambda E, S: E.memset(epsT[:, 0:1], LN_EPS))
        P.do("vector", lambda E, S: E.memset(epsT[:, 1:2], RMS_EPS))
        P.do("vector", lambda E, S: E.memset(epsT[:, 2:3], FFN_EPS), inc="v")

        def ln_rows(ntok, epscol, s1ap, s2ap, rR, rB):
            for a_, c0, n in s1ap:
                P.do("vector", lambda E, S, a=a_, c=c0, n=n: E.tensor_scalar_mul(rB[0:1, c:c + n], a, 1.0 / C))
            for a_, c0, n in s2ap:
                P.do("vector", lambda E, S, a=a_, c=c0, n=n: E.tensor_scalar_mul(rR[0:1, c:c + n], a, 1.0 / C))
            for a_, c0, n in s1ap:
                P.do("vector", lambda E, S, a=a_, c=c0, n=n: E.tensor_mul(a, rB[0:1, c:c + n], rB[0:1, c:c + n]))
            for a_, c0, n in s1ap:
                P.do("vector", lambda E, S, a=a_, c=c0, n=n: E.tensor_sub(rR[0:1, c:c + n], rR[0:1, c:c + n], a), inc="v")
            P.wait("scalar", "v")
            P.do("scalar", lambda E, S: E.activation(rR[0:1, 0:ntok], rR[0:1, 0:ntok], AF.Sqrt, bias=epsT[0:1, epscol:epscol + 1]), inc="s")
            P.wait("vector", "s")
            P.do("vector", lambda E, S: E.reciprocal(rR[0:1, 0:ntok], rR[0:1, 0:ntok]))
            P.do("vector", lambda E, S: E.scalar_tensor_tensor(rB[0:1, 0:ntok], rB[0:1, 0:ntok], -1.0, rR[0:1, 0:ntok], ALU.mult, ALU.mult), inc="v")

        def bcast(rowap, ncols, pslist):
            nb = (ncols + 511) // 512
            for i in range(nb):
                n = min(512, ncols - i * 512)
                P.do("tensor", lambda E, S, p=pslist[i], c=i * 512, n=n, r=rowap:
                     E.matmul(p[:, 0:n], onesrow[:], r[0:1, c:c + n], start=True, stop=True),
                     inc="t" if i == nb - 1 else None)

        a1 = contextlib.ExitStack(); stacks.append(a1)
        LNx = sbt(a1, "LNx", [128, NCH * T], BF16)
        V = sbt(a1, "V", [128, NCH * T], BF16)
        a1b = contextlib.ExitStack(); stacks.append(a1b)
        xbuf = [sbt(a1b, f"xbuf{i}", [128, T]) for i in range(2)]
        xbf = [sbt(a1b, f"xbf{i}", [128, T], BF16) for i in range(1)]
        xsq = [sbt(a1b, f"xsq{i}", [128, T], BF16) for i in range(1)]
        RBs = sbt(a1b, "RBs", [128, 2 * T])

        s1rows = [(psA[0:1, :], 0, 512), (psB[0:1, :], 512, 512), (psC[0:1, :], 1024, 512), (psD[0:1, :], 1536, 512)]
        s2rows = [(psE[0:1, :], 0, 512), (psF[0:1, :], 512, 512), (psS[0:1, 0:512], 1024, 512), (psS[0:1, 512:1024], 1536, 512)]
        rel = [[], []]
        for j in range(NCH):
            bi = j % 2
            P.waits("sync", rel[bi])
            P.do("sync", lambda E, S, j=j, b=bi: E.dma_start(out=xbuf[b][:], in_=xT[j * 128:(j + 1) * 128, :]), inc=f"xb{bi}", n=16)
            P.wait("vector", f"xb{bi}")
            P.wait("vector", "t", P.cnt["t"])
            P.do("vector", lambda E, S, b=bi: E.tensor_copy(xbf[0][:], xbuf[b][:]), inc="v")
            P.wait("scalar", f"xb{bi}")
            P.wait("scalar", "t", P.cnt["t"])
            P.do("scalar", lambda E, S, b=bi: E.activation(xsq[0][:], xbuf[b][:], AF.Square), inc="s")
            P.wait("tensor", "v")
            P.wait("tensor", "s")
            for tb in range(4):
                P.do("tensor", lambda E, S, j=j, tb=tb, p=PS4[tb]:
                     E.matmul(p[0:1, :], onescol[:], xbf[0][:, tb * 512:(tb + 1) * 512], start=(j == 0), stop=(j == NCH - 1)))
            for tb in range(4):
                tgt = [psE[0:1, :], psF[0:1, :], psS[0:1, 0:512], psS[0:1, 512:1024]][tb]
                P.do("tensor", lambda E, S, j=j, tb=tb, tg=tgt:
                     E.matmul(tg, onescol[:], xsq[0][:, tb * 512:(tb + 1) * 512], start=(j == 0), stop=(j == NCH - 1)),
                     inc="t" if tb == 3 else None)
            rel[bi] = [("v", P.cnt["v"]), ("s", P.cnt["s"])]
        P.wait("vector", "t")
        ln_rows(T, 0, s1rows, s2rows, rowA, rowB)
        P.wait("tensor", "v")
        for which, row in ((0, rowA), (1, rowB)):
            bcast(row, T, PS4)
            P.wait("vector", "t")
            for tb in range(4):
                P.do("vector", lambda E, S, w=which, tb=tb, p=PS4[tb]:
                     E.tensor_copy(RBs[:, w * T + tb * 512:w * T + (tb + 1) * 512], p[:, :]), inc="v" if tb == 3 else None)
            P.wait("tensor", "v")
        rel = [P.snap(), P.snap()]
        for j in range(NCH):
            bi = j % 2
            P.waits("sync", rel[bi])
            P.do("sync", lambda E, S, j=j, b=bi: E.dma_start(out=xbuf[b][:], in_=xT[j * 128:(j + 1) * 128, :]), inc=f"xb{bi}", n=16)
            P.wait("vector", f"xb{bi}")
            P.do("vector", lambda E, S, b=bi: E.tensor_mul(xbuf[b][:], xbuf[b][:], RBs[:, 0:T]))
            P.do("vector", lambda E, S, j=j, b=bi: E.tensor_add(LNx[:, j * T:(j + 1) * T], xbuf[b][:], RBs[:, T:2 * T]), inc="v")
            rel[bi] = [("v", P.cnt["v"])]
        P.barrier()
        a1b.close()
        if phase == 1:
            for j in range(NCH):
                P.do("gpsimd", lambda E, S, j=j: E.dma_start(out=dbg[j * 128:(j + 1) * 128, :], in_=LNx[:, j * T:(j + 1) * T]), inc="gin", n=16)

        if phase >= 2:
            a2 = contextlib.ExitStack(); stacks.append(a2); stacks.append(a2)
            wvb = [sbt(a2, f"wvb{i}", [128, NCH * 512], BF16) for i in range(2)]
            relw = [[], []]
            relv4 = [[("v", P.cnt["v"])] for _ in range(4)]
            for fb in range(4):
                bi = fb % 2
                P.waits("gpsimd", relw[bi])
                for ci in range(NCH):
                    P.do("gpsimd", lambda E, S, fb=fb, ci=ci, b=bi:
                         E.dma_start(out=wvb[b][:, ci * 512:(ci + 1) * 512], in_=wvT[ci, :, fb * 512:(fb + 1) * 512]), inc=f"g{bi}", n=16)
                P.wait("tensor", f"g{bi}")
                for ti in range(NCH):
                    pst = PS4[ti % 4]
                    P.waits("tensor", relv4[ti % 4])
                    for ci in range(NCH):
                        P.do("tensor", lambda E, S, ti=ti, ci=ci, b=bi, p=pst:
                             E.matmul(p[:, :], LNx[:, ci * T + ti * 128:ci * T + (ti + 1) * 128], wvb[b][:, ci * 512:(ci + 1) * 512],
                                      start=(ci == 0), stop=(ci == NCH - 1)), inc="t" if ci == NCH - 1 else None)
                    P.wait("vector", "t")
                    P.do("vector", lambda E, S, ti=ti, fb=fb, p=pst:
                         E.tensor_copy(V[:, ti * T + fb * 512:ti * T + (fb + 1) * 512], p[:, :]), inc="v")
                    relv4[ti % 4] = [("v", P.cnt["v"])]
                relw[bi] = [("t", P.cnt["t"])]
            a2.close()
            P.barrier()
            if phase == 2:
                for ti in range(NCH):
                    P.do("gpsimd", lambda E, S, ti=ti: E.dma_start(out=dbg[ti * 128:(ti + 1) * 128, :], in_=V[:, ti * T:(ti + 1) * T]), inc="gin", n=16)

        if phase >= 3:
            a3 = contextlib.ExitStack(); stacks.append(a3)
            rkcS = sbt(a3, "rkcS", [128, T])
            rksS = sbt(a3, "rksS", [128, T])
            emask = sbt(a3, "emask_s", [128, max(MASK_SLOTS, 1) * 128], BF16)
            wkq = [sbt(a3, f"wkq{i}", [128, C], BF16) for i in range(2)]
            krot2 = [sbt(a3, f"krot{i}", [128, T], BF16) for i in range(2)]
            qhat2 = [sbt(a3, f"qhat{i}", [128, NQ], BF16) for i in range(2)]
            ksc2 = [sbt(a3, f"ksc{i}", [128, 16]) for i in range(2)]
            qrot = sbt(a3, "qrot", [128, NQ], BF16)
            sqt = sbt(a3, "sqt", [128, T], BF16)
            tmpf = sbt(a3, "tmpf", [128, 1024], BF16)
            rSb = sbt(a3, "rSb", [128, NQ])
            tshuf = rSb
            ebuf = [sbt(a3, f"ebuf{i}", [128, 512], BF16) for i in range(2)]
            aout1 = [sbt(a3, f"aout1_{i}", [128, NQ], BF16) for i in range(2)]
            for src, dst in ((rkc, rkcS), (rks, rksS)):
                P.do("sync", lambda E, S, s_=src, d_=dst: E.dma_start(out=d_[:], in_=s_), inc="din", n=16)
            if MASK_SLOTS > 0:
                P.do("sync", lambda E, S: E.dma_start(out=emask[:], in_=emaskD), inc="din", n=16)
            P.wait("vector", "din")
            P.wait("scalar", "din")
            rel_ao = [[], []]
            rel_wkq = [[], []]
            LAST_KT = {0: max(kt for kt in range(16) if NA_U[kt] > 0),
                       1: max(kt for kt in range(16) if NB_U[kt] > 0)}
            ITERS = [(kt, b_, PROFS[b_][kt]) for kt in range(16) for b_ in (0, 1) if PROFS[b_][kt] > 0]

            def make_prep(h):
                par = h % 2
                krot, ksc, qhat = krot2[par], ksc2[par], qhat2[par]
                steps = []

                def s_loadk():
                    P.waits("gpsimd", rel_wkq[par])
                    P.do("gpsimd", lambda E, S, h=h, b=par: E.dma_start(out=wkq[b][:], in_=wkT[h]), inc=f"g{par}", n=16)
                steps.append(s_loadk)

                def mk_kprod(half):
                    def f():
                        P.wait("tensor", f"g{par}")
                        P.wait("tensor", "v", P.cnt["v"])
                        for ci in range(NCH):
                            for q_, p in ((0, psC), (1, psD)):
                                tb = 2 * half + q_
                                P.do("tensor", lambda E, S, ci=ci, tb=tb, b=par, p=p:
                                     E.matmul(p[:, :], wkq[b][:, ci * 128:(ci + 1) * 128],
                                              LNx[:, ci * T + tb * 512:ci * T + (tb + 1) * 512],
                                              start=(ci == 0), stop=(ci == NCH - 1)),
                                     inc="t" if (ci == NCH - 1 and q_ == 1) else None)
                    return f

                def mk_krope(half):
                    def f():
                        P.wait("vector", "t")
                        for q_, p in ((0, psC), (1, psD)):
                            tb = 2 * half + q_
                            sl = slice(tb * 512, (tb + 1) * 512)
                            P.do("vector", lambda E, S, p=p, sl=sl: E.tensor_mul(tmpf[:, 0:512], p[:, :], rkcS[:, sl]))
                            P.do("vector", lambda E, S, p=p: E.stream_shuffle(tshuf[:, 0:512], p[:, :], SHUF))
                            P.do("vector", lambda E, S, sl=sl: E.tensor_mul(tmpf[:, 512:1024], tshuf[:, 0:512], rksS[:, sl]))
                            P.do("vector", lambda E, S, sl=sl, kr=krot: E.tensor_add(kr[:, sl], tmpf[:, 0:512], tmpf[:, 512:1024]), inc="v")
                    return f

                for half in (0, 1):
                    steps.append(mk_kprod(half))
                    steps.append(mk_krope(half))

                def s_ksq():
                    P.wait("scalar", "v")
                    P.do("scalar", lambda E, S, kr=krot: E.activation(sqt[:, 0:T], kr[:, 0:T], AF.Square), inc="s")
                steps.append(s_ksq)

                def mk_ksum(r):
                    def f():
                        P.wait("tensor", "s")
                        P.wait("tensor", "v", P.cnt["v"])
                        for q_, p in ((0, psC), (1, psD)):
                            kt = 2 * r + q_
                            P.do("tensor", lambda E, S, kt=kt, p=p:
                                 E.matmul(p[:, 0:1], sqt[:, kt * 128:(kt + 1) * 128], onescol[:], start=True, stop=True),
                                 inc="t" if q_ == 1 else None)
                        P.wait("vector", "t")
                        for q_, p in ((0, psC), (1, psD)):
                            kt = 2 * r + q_
                            P.do("vector", lambda E, S, kt=kt, p=p, k=ksc: E.tensor_copy(k[:, kt:kt + 1], p[:, 0:1]), inc="v")
                    return f

                for r in range(8):
                    steps.append(mk_ksum(r))

                def s_kscrow():
                    P.wait("scalar", "v")
                    P.do("scalar", lambda E, S, k=ksc: E.activation(k[:, 0:16], k[:, 0:16], AF.Sqrt, bias=epsT[:, 1:2], scale=1.0 / 128.0), inc="s")
                    P.wait("vector", "s")
                    P.do("vector", lambda E, S, k=ksc: E.reciprocal(k[:, 0:16], k[:, 0:16]))
                    P.do("vector", lambda E, S, k=ksc: E.tensor_scalar_mul(k[:, 0:16], k[:, 0:16], SCALE), inc="v")
                    P.wait("gpsimd", "t", P.cnt["t"])
                    P.do("gpsimd", lambda E, S, h=h, b=1 - par: E.dma_start(out=wkq[b][:], in_=wqT[h]), inc=f"g{1 - par}", n=16)
                steps.append(s_kscrow)

                def s_qprod():
                    P.wait("tensor", f"g{1 - par}")
                    P.wait("tensor", "v", P.cnt["v"])
                    for ci in range(NCH):
                        for q_, p in ((0, psC), (1, psD)):
                            P.do("tensor", lambda E, S, ci=ci, tb=q_, b=1 - par, p=p:
                                 E.matmul(p[:, :], wkq[b][:, ci * 128:(ci + 1) * 128],
                                          LNx[:, ci * T + tb * 512:ci * T + (tb + 1) * 512],
                                          start=(ci == 0), stop=(ci == NCH - 1)),
                                 inc="t" if (ci == NCH - 1 and q_ == 1) else None)
                    rel_wkq[1 - par] = [("t", P.cnt["t"])]
                steps.append(s_qprod)

                def s_qrope():
                    P.wait("vector", "t")
                    for q_, p in ((0, psC), (1, psD)):
                        sl = slice(q_ * 512, (q_ + 1) * 512)
                        P.do("vector", lambda E, S, p=p, sl=sl: E.tensor_mul(tmpf[:, 0:512], p[:, :], rkcS[:, sl]))
                        P.do("vector", lambda E, S, p=p: E.stream_shuffle(tshuf[:, 0:512], p[:, :], SHUF))
                        P.do("vector", lambda E, S, sl=sl: E.tensor_mul(tmpf[:, 512:1024], tshuf[:, 0:512], rksS[:, sl]))
                        P.do("vector", lambda E, S, sl=sl: E.tensor_add(qrot[:, sl], tmpf[:, 0:512], tmpf[:, 512:1024]), inc="v")
                    P.wait("scalar", "v")
                    P.do("scalar", lambda E, S: E.activation(sqt[:, 0:NQ], qrot[:, 0:NQ], AF.Square), inc="s")
                steps.append(s_qrope)

                def s_qstats():
                    P.wait("tensor", "s")
                    P.wait("tensor", "v", P.cnt["v"])
                    for q_, p in ((0, psC), (1, psD)):
                        P.do("tensor", lambda E, S, tb=q_, p=p:
                             E.matmul(p[0:1, :], onescol[:], sqt[:, tb * 512:(tb + 1) * 512], start=True, stop=True),
                             inc="t" if q_ == 1 else None)
                    P.wait("vector", "t")
                    P.do("vector", lambda E, S: E.tensor_copy(rowB[0:1, 0:512], psC[0:1, :]))
                    P.do("vector", lambda E, S: E.tensor_copy(rowB[0:1, 512:1024], psD[0:1, :]), inc="v")
                    P.wait("scalar", "v")
                    P.do("scalar", lambda E, S: E.activation(rowB[0:1, 0:NQ], rowB[0:1, 0:NQ], AF.Sqrt, bias=epsT[0:1, 1:2], scale=1.0 / 128.0), inc="s")
                    P.wait("vector", "s")
                    P.do("vector", lambda E, S: E.reciprocal(rowB[0:1, 0:NQ], rowB[0:1, 0:NQ]), inc="v")
                steps.append(s_qstats)

                def s_qhat():
                    P.wait("tensor", "v")
                    P.do("tensor", lambda E, S: E.matmul(psC[:, :], onesrow[:], rowB[0:1, 0:512], start=True, stop=True))
                    P.do("tensor", lambda E, S: E.matmul(psD[:, :], onesrow[:], rowB[0:1, 512:1024], start=True, stop=True), inc="t")
                    P.wait("vector", "t")
                    for q_, p in ((0, psC), (1, psD)):
                        P.do("vector", lambda E, S, tb=q_, p=p, q=qhat:
                             E.tensor_mul(q[:, tb * 512:(tb + 1) * 512], qrot[:, tb * 512:(tb + 1) * 512], p[:, :]),
                             inc="v" if q_ == 1 else None)
                steps.append(s_qhat)
                return steps

            def make_epi(h):
                ao = aout1[h % 2]
                epi_rel = {}

                def e1():
                    P.wait("tensor", "v", P.cnt["v"])
                    P.do("tensor", lambda E, S: E.matmul(psC[:, :], onesrow[:], rowA[0:1, 0:512], start=True, stop=True))
                    P.do("tensor", lambda E, S: E.matmul(psD[:, :], onesrow[:], rowA[0:1, 512:1024], start=True, stop=True), inc="t")

                def e2():
                    P.wait("vector", "t")
                    P.waits("vector", rel_ao[h % 2])
                    for tb in range(2):
                        P.do("vector", lambda E, S, tb=tb, p=[psC, psD][tb]: E.reciprocal(rSb[:, tb * 512:(tb + 1) * 512], p[:, :]))
                        P.do("vector", lambda E, S, tb=tb, pv=[psE, psF][tb], a=ao:
                             E.tensor_mul(a[:, tb * 512:(tb + 1) * 512], pv[:, :], rSb[:, tb * 512:(tb + 1) * 512]), inc="v" if tb == 1 else None)
                    epi_rel["r"] = [("v", P.cnt["v"])]
                    P.wait("sync", "v")
                    P.do("sync", lambda E, S, h=h, a=ao: E.dma_start(out=aoscr[h * 128:(h + 1) * 128, :], in_=a[:]), inc=f"spl{h % 2}", n=16)
                    if phase == 3:
                        P.wait("gpsimd", "v")
                        P.do("gpsimd", lambda E, S, h=h, a=ao: E.dma_start(out=dbg[h * 128:(h + 1) * 128, 0:NQ], in_=a[:]), inc=f"gin{h % 2}", n=16)
                        rel_ao[h % 2] = [(f"spl{h % 2}", P.cnt[f"spl{h % 2}"]), (f"gin{h % 2}", P.cnt[f"gin{h % 2}"])]
                    else:
                        rel_ao[h % 2] = [(f"spl{h % 2}", P.cnt[f"spl{h % 2}"])]
                return [e1, e2], epi_rel

            for st_ in make_prep(0):
                st_()
            prep_rel = [("t", P.cnt["t"]), ("v", P.cnt["v"]), ("s", P.cnt["s"])]
            pend_epi = None

            for h in range(NH):
                par = h % 2
                krot, ksc, qhat = krot2[par], ksc2[par], qhat2[par]
                steps = list(pend_epi[0]) if pend_epi else []
                epi_rel_prev = pend_epi[1] if pend_epi else None
                steps += make_prep(h + 1) if h + 1 < NH else []
                si = 0
                P.waits("tensor", prep_rel)
                P.waits("scalar", prep_rel)
                P.wait("tensor", "s", P.cnt["s"])
                exp_done = []
                ready = []

                def emit_scores(i):
                    kt, b_, nU = ITERS[i]
                    qc0 = b_ * 512 + (4 - nU) * 128
                    ncols = nU * 128
                    psc = [psA, psB][i % 2]
                    eb = ebuf[i % 2]
                    if i >= 2:
                        P.wait("tensor", "s", exp_done[i - 2])
                    P.do("tensor", lambda E, S, kt=kt, qc0=qc0, n=ncols, p=psc, kr=krot, q=qhat:
                         E.matmul(p[:, 0:n], kr[:, kt * 128:(kt + 1) * 128], q[:, qc0:qc0 + n], start=True, stop=True), inc="t")
                    P.wait("scalar", "t")
                    P.do("scalar", lambda E, S, kt=kt, n=ncols, p=psc, e=eb, k=ksc:
                         E.activation(e[:, 0:n], p[:, 0:n], AF.Exp, scale=k[:, kt:kt + 1]), inc="s")
                    exp_done.append(P.cnt["s"])
                    mo = MASK_OFF.get((kt, b_))
                    if mo is not None:
                        o0, L = mo
                        P.wait("vector", "s")
                        P.do("vector", lambda E, S, e=eb, o0=o0, L=L:
                             E.tensor_mul(e[:, 0:L * 128], e[:, 0:L * 128], emask[:, o0 * 128:(o0 + L) * 128]), inc="v")
                        ready.append([("v", P.cnt["v"])])
                    else:
                        ready.append([("s", P.cnt["s"])])

                def emit_sumpv(i):
                    kt, b_, nU = ITERS[i]
                    qc0 = b_ * 512 + (4 - nU) * 128
                    ncols = nU * 128
                    eb = ebuf[i % 2]
                    first, last = (kt == 0), (kt == LAST_KT[b_])
                    if first and epi_rel_prev is not None:
                        P.waits("tensor", epi_rel_prev.get("r", []))
                    P.waits("tensor", ready[i])
                    P.do("tensor", lambda E, S, qc0=qc0, n=ncols, e=eb, f=first, l=last:
                         E.matmul(psS[0:1, qc0:qc0 + n], onescol[:], e[:, 0:n], start=f, stop=l))
                    P.do("tensor", lambda E, S, kt=kt, b_=b_, qc0=qc0, n=ncols, e=eb, f=first, l=last, h=h:
                         E.matmul([psE, psF][b_][:, qc0 - b_ * 512:qc0 - b_ * 512 + n],
                                  V[:, kt * T + h * 128:kt * T + (h + 1) * 128], e[:, 0:n], start=f, stop=l), inc="t")

                epi_n = 2 if pend_epi else 0
                for i in range(len(ITERS) + 1):
                    if i < len(ITERS):
                        emit_scores(i)
                    if i == 0:
                        while si < epi_n:
                            steps[si]()
                            si += 1
                    if i >= 1:
                        emit_sumpv(i - 1)
                    want = ((i + 1) * len(steps)) // (len(ITERS) + 1)
                    while si < want:
                        steps[si]()
                        si += 1
                while si < len(steps):
                    steps[si]()
                    si += 1
                prep_rel = [("t", P.cnt["t"]), ("v", P.cnt["v"]), ("s", P.cnt["s"])]
                P.wait("vector", "t")
                P.do("vector", lambda E, S: E.tensor_copy(rowA[0:1, 0:NQ], psS[0:1, 0:1024]), inc="v")
                pend_epi = make_epi(h)
            for st_ in pend_epi[0]:
                st_()
            a3.close()
            P.barrier()

        if phase >= 4:
            a1.close()
            a4o = contextlib.ExitStack(); stacks.append(a4o); stacks.append(a4o)
            xqS = sbt(a4o, "xqS", [128, NCH * NQ], BF16)
            wgb = [sbt(a4o, f"wgb{i}", [128, C], BF16) for i in range(2)]
            wub = [sbt(a4o, f"wub{i}", [128, C], BF16) for i in range(2)]
            a4m = contextlib.ExitStack(); stacks.append(a4m); stacks.append(a4m)
            x2S = sbt(a4m, "x2S", [128, NCH * NQ])
            a4 = contextlib.ExitStack(); stacks.append(a4); stacks.append(a4)
            aoutS = sbt(a4, "aoutS", [128, NCH * NQ], BF16)
            wob = [sbt(a4, f"wob{i}", [128, C], BF16) for i in range(2)]
            abf = [sbt(a4, f"abf{i}", [128, NQ], BF16) for i in range(2)]
            asq = [sbt(a4, f"asq{i}", [128, NQ], BF16) for i in range(2)]
            xoc = [sbt(a4, f"xoc{i}", [128, NQ]) for i in range(2)]
            for j in range(NCH):
                P.do("gpsimd", lambda E, S, j=j: E.dma_start(out=aoutS[:, j * NQ:(j + 1) * NQ], in_=aoscr[j * 128:(j + 1) * 128, :]), inc="gin", n=16)
            relw = [[], []]
            relpp = [[("v", P.cnt["v"])], [("v", P.cnt["v"])]]
            pend_wstats = None
            for fc in range(NCH):
                bi = fc % 2
                pp = [psA, psB] if bi == 0 else [psC, psD]
                P.waits("gpsimd", relw[bi])
                P.do("gpsimd", lambda E, S, fc=fc, b=bi: E.dma_start(out=wob[b][:], in_=woT[fc]), inc=f"g{bi}", n=16)
                P.wait("tensor", f"g{bi}")
                P.wait("tensor", "gin")
                P.waits("tensor", relpp[bi])
                for ci in range(NCH):
                    for tb in range(2):
                        P.do("tensor", lambda E, S, ci=ci, tb=tb, b=bi, p=pp[tb]:
                             E.matmul(p[:, :], wob[b][:, ci * 128:(ci + 1) * 128], aoutS[:, ci * NQ + tb * 512:ci * NQ + (tb + 1) * 512],
                                      start=(ci == 0), stop=(ci == NCH - 1)), inc="t" if (ci == NCH - 1 and tb == 1) else None)
                relw[bi] = [("t", P.cnt["t"])]
                if pend_wstats is not None:
                    pend_wstats()
                P.wait("vector", "t")
                for tb in range(2):
                    P.do("vector", lambda E, S, fc=fc, tb=tb, p=pp[tb]:
                         E.tensor_copy(x2S[:, fc * NQ + tb * 512:fc * NQ + (tb + 1) * 512], p[:, :]), inc="v" if tb == 1 else None)
                relpp[bi] = [("v", P.cnt["v"])]
                P.do("vector", lambda E, S, fc=fc, ab=abf[bi]: E.tensor_copy(ab[:], x2S[:, fc * NQ:(fc + 1) * NQ]), inc="v")
                P.wait("scalar", "v")
                P.do("scalar", lambda E, S, ab=abf[bi], aq=asq[bi]: E.activation(aq[:], ab[:], AF.Square), inc="s")

                def mk_wstats(fc=fc, scnt=P.cnt["s"], ab=abf[bi], aq=asq[bi]):
                    def f():
                        P.wait("tensor", "s", scnt)
                        for tb in range(2):
                            P.do("tensor", lambda E, S, tb=tb, tg=[psS[0:1, 0:512], psS[0:1, 512:1024]][tb]:
                                 E.matmul(tg, onescol[:], ab[:, tb * 512:(tb + 1) * 512], start=(fc == 0), stop=(fc == NCH - 1)))
                        for tb in range(2):
                            P.do("tensor", lambda E, S, tb=tb, tg=[psE[0:1, :], psF[0:1, :]][tb]:
                                 E.matmul(tg, onescol[:], aq[:, tb * 512:(tb + 1) * 512], start=(fc == 0), stop=(fc == NCH - 1)),
                                 inc="t" if tb == 1 else None)
                    return f
                pend_wstats = mk_wstats()
            pend_wstats()
            P.wait("vector", "t")
            ln_rows(NQ, 0, [(psS[0:1, 0:512], 0, 512), (psS[0:1, 512:1024], 512, 512)],
                    [(psE[0:1, :], 0, 512), (psF[0:1, :], 512, 512)], rowA, rowB)
            P.wait("tensor", "v")
            bcast(rowA, NQ, [psA, psB])
            bcast(rowB, NQ, [psC, psD])
            P.wait("vector", "t")
            relx = [[], []]
            for j in range(NCH):
                bi = j % 2
                P.waits("sync", relx[bi])
                P.do("sync", lambda E, S, j=j, b=bi: E.dma_start(out=xoc[b][:], in_=xown[j * 128:(j + 1) * 128, :]), inc=f"xb{bi}", n=16)
                P.wait("vector", f"xb{bi}")
                for tb in range(2):
                    sl = slice(j * NQ + tb * 512, j * NQ + (tb + 1) * 512)
                    P.do("vector", lambda E, S, sl=sl, p=[psA, psB][tb]: E.tensor_mul(x2S[:, sl], x2S[:, sl], p[:, :]))
                    P.do("vector", lambda E, S, sl=sl, p=[psC, psD][tb]: E.tensor_add(x2S[:, sl], x2S[:, sl], p[:, :]))
                    P.do("vector", lambda E, S, sl=sl, b=bi, tb=tb:
                         E.tensor_add(x2S[:, sl], x2S[:, sl], xoc[b][:, tb * 512:(tb + 1) * 512]), inc="v")
                relx[bi] = [("v", P.cnt["v"])]
                P.wait("sync", "v")
                P.do("sync", lambda E, S, j=j: E.dma_start(out=x2scr[j * 128:(j + 1) * 128, :], in_=x2S[:, j * NQ:(j + 1) * NQ]), inc="sx2", n=16)
            a4.close()
            P.barrier()
            if phase == 4:
                for j in range(NCH):
                    P.do("gpsimd", lambda E, S, j=j: E.dma_start(out=dbg[j * 128:(j + 1) * 128, 0:NQ], in_=x2S[:, j * NQ:(j + 1) * NQ]), inc="gin", n=16)

        if phase >= 5:
            a5 = contextlib.ExitStack(); stacks.append(a5); stacks.append(a5)
            RB2 = sbt(a5, "RB2", [128, 2 * NQ])
            Abc = sbt(a5, "Abc", [128, NQ])
            yabs = sbt(a5, "yabs", [128, NQ])
            macc = sbt(a5, "macc", [128, NQ])
            tq = sbt(a5, "tq", [128, NQ])
            cf = sbt(a5, "cf", [128, NQ])
            b1t = sbt(a5, "b1t", [128, NQ])
            xi = sbt(a5, "xi", [128, NQ], I32)
            qbf = sbt(a5, "qbf", [128, NQ], BF16)
            qsq = sbt(a5, "qsq", [128, NQ], BF16)
            tcol = sbt(a5, "tcol", [128, 8])
            r3r = sbt(a5, "r3r", [1, NQ])

            gu_relw = [[], []]
            gu_sg = [(0, 0, 0)] * HC
            gu_spill = [0] * HC

            def gu_load_sign(hc, from_scratch):
                bi = hc % 2
                P.waits("gpsimd", gu_relw[bi])
                if not from_scratch:
                    P.do("gpsimd", lambda E, S, hc=hc, b=bi: E.dma_start(out=wgb[b][:], in_=wgT[hc]), inc=f"g{bi}", n=16)
                    P.do("gpsimd", lambda E, S, hc=hc, b=bi: E.dma_start(out=wub[b][:], in_=wuT[hc]), inc=f"g{bi}", n=16)
                    P.wait("scalar", f"g{bi}")
                    P.do("scalar", lambda E, S, b=bi: E.sign(wgb[b][:], wgb[b][:]))
                    P.do("scalar", lambda E, S, b=bi: E.sign(wub[b][:], wub[b][:]), inc="s")
                    P.wait("sync", "s")
                    P.do("sync", lambda E, S, hc=hc, b=bi: E.dma_start(out=sgg[hc], in_=wgb[b][:]), inc=f"sw{bi}", n=16)
                    P.do("sync", lambda E, S, hc=hc, b=bi: E.dma_start(out=sgu[hc], in_=wub[b][:]), inc=f"sw{bi}", n=16)
                    gu_spill[hc] = P.cnt[f"sw{bi}"]
                else:
                    P.wait("gpsimd", f"sw{bi}", gu_spill[hc])
                    P.do("gpsimd", lambda E, S, hc=hc, b=bi: E.dma_start(out=wgb[b][:], in_=sgg[hc]), inc=f"g{bi}", n=16)
                    P.do("gpsimd", lambda E, S, hc=hc, b=bi: E.dma_start(out=wub[b][:], in_=sgu[hc]), inc=f"g{bi}", n=16)
                gu_sg[hc] = (P.cnt["s"], P.cnt[f"g{bi}"], P.cnt[f"sw{bi}"])

            gu_load_sign(0, False)
            gu_load_sign(1, False)

            def stats_from(src_emit):
                for j in range(NCH):
                    src_emit(j)
                    P.wait("tensor", "v")
                    P.wait("tensor", "s")
                    for tb in range(2):
                        P.do("tensor", lambda E, S, j=j, tb=tb, tg=[psS[0:1, 0:512], psS[0:1, 512:1024]][tb]:
                             E.matmul(tg, onescol[:], qbf[:, tb * 512:(tb + 1) * 512], start=(j == 0), stop=(j == NCH - 1)))
                    for tb in range(2):
                        P.do("tensor", lambda E, S, j=j, tb=tb, tg=[psE[0:1, :], psF[0:1, :]][tb]:
                             E.matmul(tg, onescol[:], qsq[:, tb * 512:(tb + 1) * 512], start=(j == 0), stop=(j == NCH - 1)),
                             inc="t" if tb == 1 else None)
                P.wait("vector", "t")

            def src_x2(j):
                P.wait("vector", "t", P.cnt["t"])
                P.do("vector", lambda E, S, j=j: E.tensor_copy(qbf[:], x2S[:, j * NQ:(j + 1) * NQ]), inc="v")
                P.wait("scalar", "v")
                P.do("scalar", lambda E, S: E.activation(qsq[:], qbf[:], AF.Square), inc="s")

            stats_from(src_x2)
            ln_rows(NQ, 0, [(psS[0:1, 0:512], 0, 512), (psS[0:1, 512:1024], 512, 512)],
                    [(psE[0:1, :], 0, 512), (psF[0:1, :], 512, 512)], rowA, rowB)
            P.do("vector", lambda E, S: E.tensor_mul(r3r[0:1, :], rowA[0:1, 0:NQ], rowA[0:1, 0:NQ]))
            P.do("vector", lambda E, S: E.tensor_scalar(r3r[0:1, :], r3r[0:1, :], -LN_EPS, 1.0 + FFN_EPS, ALU.mult, ALU.add), inc="v")
            P.wait("scalar", "v")
            P.do("scalar", lambda E, S: E.activation(r3r[0:1, :], r3r[0:1, :], AF.Sqrt), inc="s")
            P.wait("vector", "s")
            P.do("vector", lambda E, S: E.reciprocal(r3r[0:1, :], r3r[0:1, :]), inc="v")
            P.wait("tensor", "v")
            bcast(rowA, NQ, [psA, psB])
            bcast(rowB, NQ, [psC, psD])
            P.wait("vector", "t")
            for tb in range(2):
                P.do("vector", lambda E, S, tb=tb, p=[psA, psB][tb]: E.tensor_copy(RB2[:, tb * 512:(tb + 1) * 512], p[:, :]))
                P.do("vector", lambda E, S, tb=tb, p=[psC, psD][tb]: E.tensor_copy(RB2[:, NQ + tb * 512:NQ + (tb + 1) * 512], p[:, :]), inc="v" if tb == 1 else None)

            P.wait("vector", "sx2")
            for j in range(NCH):
                P.do("vector", lambda E, S, j=j: E.tensor_mul(x2S[:, j * NQ:(j + 1) * NQ], x2S[:, j * NQ:(j + 1) * NQ], RB2[:, 0:NQ]))
                P.do("vector", lambda E, S, j=j: E.tensor_add(x2S[:, j * NQ:(j + 1) * NQ], x2S[:, j * NQ:(j + 1) * NQ], RB2[:, NQ:2 * NQ]), inc="v")
                P.wait("scalar", "v")
                P.do("scalar", lambda E, S, j=j: E.activation(yabs[:], x2S[:, j * NQ:(j + 1) * NQ], AF.Abs), inc="s")
                P.wait("vector", "s")
                if j == 0:
                    P.do("vector", lambda E, S: E.tensor_copy(macc[:], yabs[:]), inc="v")
                else:
                    P.do("vector", lambda E, S: E.tensor_max(macc[:], macc[:], yabs[:]), inc="v")
            P.wait("tensor", "v")
            for jb in range(8):
                p = [psA, psB][jb % 2]
                P.wait("tensor", "v", P.cnt["v"])
                P.do("tensor", lambda E, S, jb=jb, p=p:
                     E.transpose(p[:, 0:128], macc[:, jb * 128:(jb + 1) * 128], ident[:]), inc="t")
                P.wait("vector", "t")
                P.do("vector", lambda E, S, jb=jb, p=p:
                     E.reduce_max(tcol[:, jb:jb + 1], p[:, 0:128], axis=mybir.AxisListType.X), inc="v")
            P.wait("tensor", "v")
            for jb in range(8):
                p = [psC, psD][jb % 2]
                P.wait("tensor", "v", P.cnt["v"])
                P.do("tensor", lambda E, S, jb=jb, p=p:
                     E.matmul(p[0:1, 0:128], tcol[:, jb:jb + 1], ident[:], start=True, stop=True), inc="t")
                P.wait("vector", "t")
                P.do("vector", lambda E, S, jb=jb, p=p:
                     E.tensor_copy(rowB[0:1, jb * 128:(jb + 1) * 128], p[0:1, 0:128]), inc="v")
            P.do("vector", lambda E, S: E.tensor_mul(rowA[0:1, 0:NQ], r3r[0:1, :], rowB[0:1, 0:NQ]))
            P.do("vector", lambda E, S: E.tensor_scalar_mul(rowA[0:1, 0:NQ], rowA[0:1, 0:NQ], 1.0 / 127.0))
            P.do("vector", lambda E, S: E.reciprocal(rowB[0:1, 0:NQ], rowB[0:1, 0:NQ]))
            P.do("vector", lambda E, S: E.tensor_scalar_mul(rowB[0:1, 0:NQ], rowB[0:1, 0:NQ], 127.0), inc="v")
            P.wait("tensor", "v")
            bcast(rowB, NQ, [psA, psB])
            bcast(rowA, NQ, [psC, psD])
            P.wait("vector", "t")
            for tb in range(2):
                P.do("vector", lambda E, S, tb=tb, p=[psA, psB][tb]: E.tensor_copy(Abc[:, tb * 512:(tb + 1) * 512], p[:, :]))
                P.do("vector", lambda E, S, tb=tb, p=[psC, psD][tb]: E.tensor_copy(invsbc[:, tb * 512:(tb + 1) * 512], p[:, :]), inc="v" if tb == 1 else None)
            for j in range(NCH):
                P.do("vector", lambda E, S, j=j: E.tensor_mul(tq[:], x2S[:, j * NQ:(j + 1) * NQ], Abc[:]), inc="v")
                P.wait("scalar", "v")
                P.do("scalar", lambda E, S: E.copy(xi[:], tq[:]))
                P.do("scalar", lambda E, S: E.copy(cf[:], xi[:]), inc="s")
                P.wait("vector", "s")
                P.do("vector", lambda E, S: E.tensor_sub(yabs[:], tq[:], cf[:]))
                P.do("vector", lambda E, S: E.tensor_single_scalar(b1t[:], yabs[:], 0.5, ALU.is_ge))
                P.do("vector", lambda E, S: E.tensor_single_scalar(yabs[:], yabs[:], -0.5, ALU.is_lt))
                P.do("vector", lambda E, S: E.tensor_add(cf[:], cf[:], b1t[:]))
                P.do("vector", lambda E, S: E.tensor_sub(cf[:], cf[:], yabs[:]))
                P.do("vector", lambda E, S, j=j: E.tensor_scalar(xqS[:, j * NQ:(j + 1) * NQ], cf[:], -128.0, 127.0, ALU.max, ALU.min), inc="v")
            a5.close()
            a4m.close()
            P.barrier()
            if phase == 5:
                for j in range(NCH):
                    P.do("gpsimd", lambda E, S, j=j: E.dma_start(out=dbg[j * 128:(j + 1) * 128, 0:NQ], in_=xqS[:, j * NQ:(j + 1) * NQ]), inc="gin", n=16)
                P.do("gpsimd", lambda E, S: E.dma_start(out=dbg[0:128, NQ:2 * NQ], in_=invsbc[:]), inc="gin", n=16)

        if phase >= 6:
            a6 = contextlib.ExitStack(); stacks.append(a6); stacks.append(a6)
            pS = sbt(a6, "pS", [128, HC * 512], BF16)
            gt0 = sbt(a6, "gt0", [128, 512])
            gt1 = sbt(a6, "gt1", [128, 512])
            ffS = sbt(a6, "ffS", [128, NCH * 512])
            fbf = [sbt(a6, f"fbf{i}", [128, 512], BF16) for i in range(2)]
            fsq = [sbt(a6, f"fsq{i}", [128, 512], BF16) for i in range(2)]
            wdb = sbt(a6, "wdb", [128, 8192], BF16)
            x2c = [sbt(a6, f"x2c{i}", [128, 512]) for i in range(2)]
            ob = [sbt(a6, f"ob{i}", [128, 512]) for i in range(2)]
            RB4 = sbt(a6, "RB4", [128, 1024])
            wd_relw = [[], []]
            wd_spill = [0] * (NCH * 2)
            ob_rel = [[], []]
            relgu_carry = None
            for tb in range(2):
                tsl = slice(tb * 512, (tb + 1) * 512)
                if tb == 0:
                    relgu = [[("v", P.cnt["v"])], [("v", P.cnt["v"])]]
                else:
                    relgu = relgu_carry
                    gu_load_sign(0, True)
                    gu_load_sign(1, True)
                for hc in range(HC):
                    bi = hc % 2
                    ppG = [psA, psC][bi]
                    ppU = [psB, psD][bi]
                    if tb == 0:
                        P.wait("tensor", "s", gu_sg[hc][0])
                    else:
                        P.wait("tensor", f"g{bi}", gu_sg[hc][1])
                    P.waits("tensor", relgu[bi])
                    for ci in range(NCH):
                        P.do("tensor", lambda E, S, ci=ci, b=bi, p=ppG, tb=tb:
                             E.matmul(p[:, :], wgb[b][:, ci * 128:(ci + 1) * 128], xqS[:, ci * NQ + tb * 512:ci * NQ + (tb + 1) * 512],
                                      start=(ci == 0), stop=(ci == NCH - 1)))
                        P.do("tensor", lambda E, S, ci=ci, b=bi, p=ppU, tb=tb:
                             E.matmul(p[:, :], wub[b][:, ci * 128:(ci + 1) * 128], xqS[:, ci * NQ + tb * 512:ci * NQ + (tb + 1) * 512],
                                      start=(ci == 0), stop=(ci == NCH - 1)), inc="t" if ci == NCH - 1 else None)
                    gu_relw[bi] = [("t", P.cnt["t"])] + ([(f"sw{bi}", gu_sg[hc][2])] if tb == 0 else [])
                    if hc + 2 < HC:
                        gu_load_sign(hc + 2, tb == 1)
                    P.wait("vector", "t")
                    P.do("vector", lambda E, S, p=ppG, tsl=tsl: E.tensor_mul(gt0[:], p[:, :], invsbc[:, tsl]), inc="v")
                    P.wait("scalar", "v")
                    P.do("scalar", lambda E, S: E.activation(gt1[:], gt0[:], AF.Silu), inc="s")
                    P.wait("vector", "s")
                    P.do("vector", lambda E, S, p=ppU: E.tensor_mul(gt0[:], gt1[:], p[:, :]))
                    P.do("vector", lambda E, S, hc=hc, tsl=tsl: E.tensor_mul(pS[:, hc * 512:(hc + 1) * 512], gt0[:], invsbc[:, tsl]), inc="v")
                    relgu[bi] = [("v", P.cnt["v"])]
                relgu_carry = [list(relgu[0]), list(relgu[1])]
                relpw = [[("v", P.cnt["v"])], [("v", P.cnt["v"])]]
                sgdone = [(0, 0)] * 2
                pend_stats = None
                for fc in range(NCH):
                    pw = [psE, psF][fc % 2]
                    for hf in range(2):
                        P.waits("gpsimd", wd_relw[hf])
                        if tb == 0:
                            P.do("gpsimd", lambda E, S, fc=fc, hf=hf:
                                 E.dma_start(out=wdb[:, hf * 4096:(hf + 1) * 4096], in_=wdT[fc, :, hf * 4096:(hf + 1) * 4096]), inc=f"g{hf}", n=16)
                            P.wait("scalar", f"g{hf}")
                            P.do("scalar", lambda E, S, hf=hf: E.sign(wdb[:, hf * 4096:(hf + 1) * 4096], wdb[:, hf * 4096:(hf + 1) * 4096]), inc="s")
                            P.wait("sync", "s")
                            P.do("sync", lambda E, S, fc=fc, hf=hf:
                                 E.dma_start(out=sgd[fc, :, hf * 4096:(hf + 1) * 4096], in_=wdb[:, hf * 4096:(hf + 1) * 4096]), inc=f"sw{hf}", n=16)
                            wd_spill[fc * 2 + hf] = P.cnt[f"sw{hf}"]
                        else:
                            P.wait("gpsimd", f"sw{hf}", wd_spill[fc * 2 + hf])
                            P.do("gpsimd", lambda E, S, fc=fc, hf=hf:
                                 E.dma_start(out=wdb[:, hf * 4096:(hf + 1) * 4096], in_=sgd[fc, :, hf * 4096:(hf + 1) * 4096]), inc=f"g{hf}", n=16)
                        sgdone[hf] = (P.cnt["s"], P.cnt[f"g{hf}"])
                    P.waits("tensor", relpw[fc % 2])
                    for hc in range(HC):
                        if hc == 0:
                            P.wait("tensor", "s" if tb == 0 else "g0", sgdone[0][0 if tb == 0 else 1])
                        if hc == HC // 2:
                            P.wait("tensor", "s" if tb == 0 else "g1", sgdone[1][0 if tb == 0 else 1])
                        P.do("tensor", lambda E, S, hc=hc, p=pw:
                             E.matmul(p[:, :], wdb[:, hc * 128:(hc + 1) * 128], pS[:, hc * 512:(hc + 1) * 512],
                                      start=(hc == 0), stop=(hc == HC - 1)), inc="t" if hc in (HC // 2 - 1, HC - 1) else None)
                        if hc == HC // 2 - 1:
                            wd_relw[0] = [("t", P.cnt["t"])] + ([("sw0", wd_spill[fc * 2])] if tb == 0 else [])
                    wd_relw[1] = [("t", P.cnt["t"])] + ([("sw1", wd_spill[fc * 2 + 1])] if tb == 0 else [])
                    if pend_stats is not None:
                        pend_stats()
                    P.wait("vector", "t")
                    P.do("vector", lambda E, S, fc=fc, p=pw: E.tensor_copy(ffS[:, fc * 512:(fc + 1) * 512], p[:, :]))
                    P.do("vector", lambda E, S, fc=fc, fb=fbf[fc % 2]: E.tensor_copy(fb[:], ffS[:, fc * 512:(fc + 1) * 512]), inc="v")
                    relpw[fc % 2] = [("v", P.cnt["v"])]
                    P.wait("scalar", "v")
                    P.do("scalar", lambda E, S, fb=fbf[fc % 2], fq=fsq[fc % 2]: E.activation(fq[:], fb[:], AF.Square), inc="s")

                    def mk_stats(fc=fc, scnt=P.cnt["s"], fb=fbf[fc % 2], fq=fsq[fc % 2]):
                        def f():
                            P.wait("tensor", "s", scnt)
                            P.do("tensor", lambda E, S:
                                 E.matmul(psS[0:1, 0:512], onescol[:], fb[:], start=(fc == 0), stop=(fc == NCH - 1)))
                            P.do("tensor", lambda E, S:
                                 E.matmul(psS[0:1, 512:1024], onescol[:], fq[:], start=(fc == 0), stop=(fc == NCH - 1)),
                                 inc="t")
                        return f
                    pend_stats = mk_stats()
                pend_stats()
                P.wait("vector", "t")
                ln_rows(512, 0, [(psS[0:1, 0:512], 0, 512)], [(psS[0:1, 512:1024], 0, 512)], rowA, rowB)
                P.wait("tensor", "v")
                bcast(rowA, 512, [psE])
                bcast(rowB, 512, [psF])
                P.wait("vector", "t")
                P.do("vector", lambda E, S: E.tensor_copy(RB4[:, 0:512], psE[:, :]))
                P.do("vector", lambda E, S: E.tensor_copy(RB4[:, 512:1024], psF[:, :]), inc="v")
                relx = [[], []]
                for fc in range(NCH):
                    bi = fc % 2
                    P.waits("sync", relx[bi])
                    P.do("sync", lambda E, S, fc=fc, b=bi, tb=tb:
                         E.dma_start(out=x2c[b][:], in_=x2scr[fc * 128:(fc + 1) * 128, tb * 512:(tb + 1) * 512]), inc=f"xb{bi}", n=16)
                    P.wait("vector", f"xb{bi}")
                    P.waits("vector", ob_rel[bi])
                    P.do("vector", lambda E, S, fc=fc: E.tensor_mul(gt0[:], ffS[:, fc * 512:(fc + 1) * 512], RB4[:, 0:512]))
                    P.do("vector", lambda E, S: E.tensor_add(gt0[:], gt0[:], RB4[:, 512:1024]))
                    P.do("vector", lambda E, S, b=bi: E.tensor_add(ob[b][:], gt0[:], x2c[b][:]), inc="v")
                    relx[bi] = [("v", P.cnt["v"])]
                    if bi == 0:
                        P.wait("sync", "v")
                        P.do("sync", lambda E, S, fc=fc, tb=tb, b=bi:
                             E.dma_start(out=out[fc * 128:(fc + 1) * 128, tb * 512:(tb + 1) * 512], in_=ob[b][:]), inc="sout", n=16)
                        ob_rel[bi] = [("sout", P.cnt["sout"])]
                    else:
                        P.wait("gpsimd", "v")
                        P.do("gpsimd", lambda E, S, fc=fc, tb=tb, b=bi:
                             E.dma_start(out=out[fc * 128:(fc + 1) * 128, tb * 512:(tb + 1) * 512], in_=ob[b][:]), inc="gout", n=16)
                        ob_rel[bi] = [("gout", P.cnt["gout"])]
            a6.close()

        P.barrier()
        for st_ in reversed(stacks):
            st_.close()
        ops = P.ops

        @blk.sync
        def _(E):
            for f in ops["sync"]:
                f(E, sems)

        @blk.gpsimd
        def _(E):
            for f in ops["gpsimd"]:
                f(E, sems)

        @blk.vector
        def _(E):
            for f in ops["vector"]:
                f(E, sems)

        @blk.scalar
        def _(E):
            for f in ops["scalar"]:
                f(E, sems)

        @blk.tensor
        def _(E):
            for f in ops["tensor"]:
                f(E, sems)

    return nc



def _lhst_blocks(W):
    F_, Ci = W.shape
    A = np.asarray(W).reshape(F_ // 128, 128, Ci // 128, 128)
    return np.ascontiguousarray(A.transpose(0, 3, 2, 1).reshape(F_ // 128, 128, Ci))


def _rope_tables():
    import jax
    import jax.numpy as jnp
    with jax.default_device(jax.devices("cpu")[0]):
        inv = 1.0 / 10000.0 ** (jnp.arange(0, 128, 2, dtype=jnp.float32) / 128)
        t = jnp.arange(T, dtype=jnp.float32)
        fr = t[:, None] * inv[None, :]
        sin = np.asarray(jnp.repeat(jnp.sin(fr), 2, axis=-1))
        cos = np.asarray(jnp.repeat(jnp.cos(fr), 2, axis=-1))
    return sin, cos


def _own_positions(p):
    return np.concatenate([np.arange(128 * t, 128 * (t + 1)) for t in OWN[p]])


def _perm_positions(p):
    return np.concatenate([np.arange(128 * t, 128 * (t + 1)) for t in PERM[p]])


def _emask_for(p):
    tri = (np.arange(128)[:, None] <= np.arange(128)[None, :]).astype(ml_dtypes.bfloat16)
    m = np.ones((128, max(MASK_SLOTS, 1) * 128), dtype=ml_dtypes.bfloat16)
    for (ks, b_), (o0, L) in MASK_OFF.items():
        nU = PROFS[b_][ks]
        g = PERM[p][ks]
        for s_ in range(L):
            qslot = b_ * 4 + (4 - nU) + s_
            qt = PERM[p][qslot]
            sl = slice((o0 + s_) * 128, (o0 + s_ + 1) * 128)
            if qt < g:
                m[:, sl] = 0
            elif qt == g:
                m[:, sl] = tri
    return m


def make_inmaps(L0_attn_x, L0_attn_W_qkv, L0_attn_W_o, L0_ffn_W_g, L0_ffn_W_u, L0_ffn_W_d):
    x = np.asarray(L0_attn_x)
    Wqkv = np.asarray(L0_attn_W_qkv)
    shared = {
        "wqT": _lhst_blocks(Wqkv[0:2048]),
        "wkT": _lhst_blocks(Wqkv[2048:4096]),
        "wvT": np.ascontiguousarray(Wqkv[4096:6144].T.reshape(NCH, 128, C)),
        "woT": _lhst_blocks(np.asarray(L0_attn_W_o)),
        "wgT": _lhst_blocks(np.asarray(L0_ffn_W_g)),
        "wuT": _lhst_blocks(np.asarray(L0_ffn_W_u)),
        "wdT": _lhst_blocks(np.asarray(L0_ffn_W_d)),
        "ident": np.eye(128, dtype=np.float32),
    }
    sin, cos = _rope_tables()
    sgn = np.where(np.arange(128) % 2 == 0, -1.0, 1.0).astype(np.float32)
    rkc = np.ascontiguousarray(cos.T)
    rks = np.ascontiguousarray(sin.T * sgn[:, None])
    in_maps = []
    for core in range(8):
        b, p = core // 2, core % 2
        pp = _perm_positions(p)
        xt = np.ascontiguousarray(x[b].T[:, pp])
        d = dict(shared)
        d["xT"] = xt
        d["xown"] = np.ascontiguousarray(xt[:, 0:NQ])
        d["rkc"] = np.ascontiguousarray(rkc[:, pp])
        d["rks"] = np.ascontiguousarray(rks[:, pp])
        d["emask"] = _emask_for(p)
        in_maps.append(d)
    return in_maps


def kernel(L0_attn_x, L0_attn_W_qkv, L0_attn_W_o, L0_ffn_W_g, L0_ffn_W_u, L0_ffn_W_d):
    in_maps = make_inmaps(L0_attn_x, L0_attn_W_qkv, L0_attn_W_o, L0_ffn_W_g, L0_ffn_W_u, L0_ffn_W_d)
    nc = build_nc(phase=6)
    res = run_bass_kernel_spmd(nc, in_maps, core_ids=list(range(8)))
    y = np.empty((4, T, C), dtype=np.float32)
    for core in range(8):
        b, p = core // 2, core % 2
        pos = _own_positions(p)
        y[b, pos, :] = np.asarray(res.results[core]["out"]).T
    return y


# version 16
# speedup vs baseline: 1.5728x; 1.0399x over previous
import contextlib
import numpy as np
import ml_dtypes
import concourse.bass as bass
import concourse.mybir as mybir
from concourse.bass_utils import run_bass_kernel_spmd

F32 = mybir.dt.float32
BF16 = mybir.dt.bfloat16
I32 = mybir.dt.int32
AF = mybir.ActivationFunctionType
ALU = mybir.AluOpType

C = 2048
T = 2048
NH = 16
NCH = 16
NQ = 1024
HC = 64
RMS_EPS = 1.1920929e-07
LN_EPS = 1e-5
FFN_EPS = 1e-8
SCALE = float(np.float32(1.0) / np.float32(np.sqrt(np.float32(128.0))))
SHUF = [i ^ 1 for i in range(32)]

OWN = {0: [0, 3, 4, 7, 8, 11, 12, 15], 1: [1, 2, 5, 6, 9, 10, 13, 14]}
PERM = {p: OWN[p] + sorted(set(range(16)) - set(OWN[p])) for p in (0, 1)}

def _profiles():
    na, nb = [0] * 16, [0] * 16
    for ks in range(16):
        for p in (0, 1):
            g = PERM[p][ks]
            ca = sum(1 for j in range(0, 4) if PERM[p][j] >= g)
            cb = sum(1 for j in range(4, 8) if PERM[p][j] >= g)
            na[ks] = max(na[ks], ca)
            nb[ks] = max(nb[ks], cb)
    return na, nb

NA_U, NB_U = _profiles()
PROFS = (NA_U, NB_U)


def _mask_layout():
    off, o = {}, 0
    for b_ in (0, 1):
        for ks in range(16):
            nU = PROFS[b_][ks]
            if nU == 0:
                continue
            worst = 0
            for p in (0, 1):
                g = PERM[p][ks]
                qs = [PERM[p][j] for j in range(b_ * 4, b_ * 4 + 4)]
                n_p = sum(1 for t in qs if t >= g)
                worst = max(worst, (nU - n_p) + (1 if g in qs else 0))
            if worst > 0:
                off[(ks, b_)] = (o, worst)
                o += worst
    return off, o


MASK_OFF, MASK_SLOTS = _mask_layout()
for _p in (0, 1):
    for _b in (0, 1):
        _qs = [PERM[_p][j] for j in range(_b * 4, _b * 4 + 4)]
        assert all(_qs[i] < _qs[i + 1] for i in range(3))

ENGINES = ("sync", "gpsimd", "vector", "scalar", "tensor")
SEMS = ("t", "v", "s", "din", "xb0", "xb1", "spl0", "spl1", "sx2", "sout", "gout", "g0", "g1", "gin", "gin0", "gin1", "gw", "sw0", "sw1")


class Prog:
    def __init__(self):
        self.ops = {e: [] for e in ENGINES}
        self.cnt = {n: 0 for n in SEMS}

    def emit(self, eng, fn):
        self.ops[eng].append(fn)

    def wait(self, eng, sem, val=None):
        v = self.cnt[sem] if val is None else val
        if v > 0:
            self.emit(eng, lambda E, S, _s=sem, _v=v: E.wait_ge(S[_s], _v))

    def waits(self, eng, rel):
        for sem, v in rel:
            self.wait(eng, sem, v)

    def do(self, eng, fn, inc=None, n=1):
        if eng == "vector" or eng == "scalar":
            sem = "v" if eng == "vector" else "s"
            prev = self.cnt[sem]
            self.cnt[sem] = prev + 1
            def wrapped(E, S, _f=fn, _s=sem, _p=prev):
                if _p > 0:
                    E.wait_ge(S[_s], _p)
                return _f(E, S).then_inc(S[_s], 1)
            self.emit(eng, wrapped)
            return
        if inc is not None:
            self.cnt[inc] += n
            self.emit(eng, lambda E, S, _f=fn, _i=inc, _n=n: _f(E, S).then_inc(S[_i], _n))
        else:
            self.emit(eng, lambda E, S, _f=fn: _f(E, S))

    def dma(self, eng, fn, inc, n=16):
        self.cnt[inc] += n
        self.emit(eng, lambda E, S, _f=fn, _i=inc, _n=n: _f(E, S).then_inc(S[_i], _n))

    def snap(self):
        return [(s, v) for s, v in self.cnt.items() if v > 0]

    def barrier(self, engs=ENGINES):
        snap = self.snap()
        for e in engs:
            self.waits(e, snap)


def build_nc(phase=6):
    nc = bass.Bass("TRN2", target_bir_lowering=False)
    dtn = nc.dram_tensor
    xT = dtn("xT", [C, T], F32, kind="ExternalInput").ap()
    xown = dtn("xown", [C, NQ], F32, kind="ExternalInput").ap()
    wqT = dtn("wqT", [NH, 128, C], F32, kind="ExternalInput").ap()
    wkT = dtn("wkT", [NH, 128, C], F32, kind="ExternalInput").ap()
    wvT = dtn("wvT", [NCH, 128, C], F32, kind="ExternalInput").ap()
    woT = dtn("woT", [NCH, 128, C], F32, kind="ExternalInput").ap()
    wgT = dtn("wgT", [HC, 128, C], F32, kind="ExternalInput").ap()
    wuT = dtn("wuT", [HC, 128, C], F32, kind="ExternalInput").ap()
    wdT = dtn("wdT", [NCH, 128, 8192], F32, kind="ExternalInput").ap()
    rkc = dtn("rkc", [128, T], F32, kind="ExternalInput").ap()
    rks = dtn("rks", [128, T], F32, kind="ExternalInput").ap()
    emaskD = dtn("emask", [128, max(MASK_SLOTS, 1) * 128], BF16, kind="ExternalInput").ap()
    identD = dtn("ident", [128, 128], F32, kind="ExternalInput").ap()
    out = dtn("out", [C, NQ], F32, kind="ExternalOutput").ap()
    aoscr = dtn("aoscr", [C, NQ], BF16, kind="Internal").ap()
    sgg = dtn("sgg", [HC, 128, C], BF16, kind="Internal").ap()
    sgu = dtn("sgu", [HC, 128, C], BF16, kind="Internal").ap()
    sgd = dtn("sgd", [NCH, 128, 8192], BF16, kind="Internal").ap()
    x2scr = dtn("x2scr", [C, NQ], F32, kind="Internal").ap()
    dbg = dtn("dbg", [C, T], F32, kind="ExternalOutput").ap() if phase < 6 else None

    P = Prog()
    ctx = contextlib.ExitStack()
    stacks = []
    with ctx:
        def sbt(st, nm, sh, d=F32):
            return st.enter_context(nc.sbuf_tensor(nm, sh, d))
        onescol = sbt(ctx, "onescol", [128, 1], BF16)
        onesrow = sbt(ctx, "onesrow", [1, 128], F32)
        ident = sbt(ctx, "ident_s", [128, 128], F32)
        rowA = sbt(ctx, "rowA", [1, NQ])
        rowB = sbt(ctx, "rowB", [1, NQ])
        invsbc = sbt(ctx, "invsbc", [128, NQ])
        epsT = sbt(ctx, "epsT", [128, 3])
        psA = ctx.enter_context(nc.psum_tensor("psA", [128, 512], F32))
        psB = ctx.enter_context(nc.psum_tensor("psB", [128, 512], F32))
        psC = ctx.enter_context(nc.psum_tensor("psC", [128, 512], F32))
        psD = ctx.enter_context(nc.psum_tensor("psD", [128, 512], F32))
        psE = ctx.enter_context(nc.psum_tensor("psE", [128, 512], F32))
        psF = ctx.enter_context(nc.psum_tensor("psF", [128, 512], F32))
        psS = ctx.enter_context(nc.psum_tensor("psS", [1, 1024], F32))
        PS4 = [psA, psB, psC, psD]
        sems = {n: ctx.enter_context(nc.semaphore("sem_" + n)) for n in SEMS}
        blk = ctx.enter_context(nc.Block())

        P.do("sync", lambda E, S: E.dma_start(out=ident[:], in_=identD), inc="din", n=16)
        P.do("vector", lambda E, S: E.memset(onescol[:], 1.0))
        P.do("vector", lambda E, S: E.memset(onesrow[:], 1.0))
        P.do("vector", lambda E, S: E.memset(epsT[:, 0:1], LN_EPS))
        P.do("vector", lambda E, S: E.memset(epsT[:, 1:2], RMS_EPS))
        P.do("vector", lambda E, S: E.memset(epsT[:, 2:3], FFN_EPS), inc="v")

        def ln_rows(ntok, epscol, s1ap, s2ap, rR, rB):
            for a_, c0, n in s1ap:
                P.do("vector", lambda E, S, a=a_, c=c0, n=n: E.tensor_scalar_mul(rB[0:1, c:c + n], a, 1.0 / C))
            for a_, c0, n in s2ap:
                P.do("vector", lambda E, S, a=a_, c=c0, n=n: E.tensor_scalar_mul(rR[0:1, c:c + n], a, 1.0 / C))
            for a_, c0, n in s1ap:
                P.do("vector", lambda E, S, a=a_, c=c0, n=n: E.tensor_mul(a, rB[0:1, c:c + n], rB[0:1, c:c + n]))
            for a_, c0, n in s1ap:
                P.do("vector", lambda E, S, a=a_, c=c0, n=n: E.tensor_sub(rR[0:1, c:c + n], rR[0:1, c:c + n], a), inc="v")
            P.wait("scalar", "v")
            P.do("scalar", lambda E, S: E.activation(rR[0:1, 0:ntok], rR[0:1, 0:ntok], AF.Sqrt, bias=epsT[0:1, epscol:epscol + 1]), inc="s")
            P.wait("vector", "s")
            P.do("vector", lambda E, S: E.reciprocal(rR[0:1, 0:ntok], rR[0:1, 0:ntok]))
            P.do("vector", lambda E, S: E.scalar_tensor_tensor(rB[0:1, 0:ntok], rB[0:1, 0:ntok], -1.0, rR[0:1, 0:ntok], ALU.mult, ALU.mult), inc="v")

        def bcast(rowap, ncols, pslist):
            nb = (ncols + 511) // 512
            for i in range(nb):
                n = min(512, ncols - i * 512)
                P.do("tensor", lambda E, S, p=pslist[i], c=i * 512, n=n, r=rowap:
                     E.matmul(p[:, 0:n], onesrow[:], r[0:1, c:c + n], start=True, stop=True),
                     inc="t" if i == nb - 1 else None)

        a1 = contextlib.ExitStack(); stacks.append(a1)
        LNx = sbt(a1, "LNx", [128, NCH * T], BF16)
        V = sbt(a1, "V", [128, NCH * T], BF16)
        a1b = contextlib.ExitStack(); stacks.append(a1b)
        xbuf = [sbt(a1b, f"xbuf{i}", [128, T]) for i in range(2)]
        xbf = [sbt(a1b, f"xbf{i}", [128, T], BF16) for i in range(1)]
        xsq = [sbt(a1b, f"xsq{i}", [128, T], BF16) for i in range(1)]
        RBs = sbt(a1b, "RBs", [128, 2 * T])
        rowAT = sbt(a1b, "rowAT", [1, T])
        rowBT = sbt(a1b, "rowBT", [1, T])

        s1rows = [(psA[0:1, :], 0, 512), (psB[0:1, :], 512, 512), (psC[0:1, :], 1024, 512), (psD[0:1, :], 1536, 512)]
        s2rows = [(psE[0:1, :], 0, 512), (psF[0:1, :], 512, 512), (psS[0:1, 0:512], 1024, 512), (psS[0:1, 512:1024], 1536, 512)]
        rel = [[], []]
        for j in range(NCH):
            bi = j % 2
            P.waits("sync", rel[bi])
            P.do("sync", lambda E, S, j=j, b=bi: E.dma_start(out=xbuf[b][:], in_=xT[j * 128:(j + 1) * 128, :]), inc=f"xb{bi}", n=16)
            P.wait("vector", f"xb{bi}")
            P.wait("vector", "t", P.cnt["t"])
            P.do("vector", lambda E, S, b=bi: E.tensor_copy(xbf[0][:], xbuf[b][:]), inc="v")
            P.wait("scalar", f"xb{bi}")
            P.wait("scalar", "t", P.cnt["t"])
            P.do("scalar", lambda E, S, b=bi: E.activation(xsq[0][:], xbuf[b][:], AF.Square), inc="s")
            P.wait("tensor", "v")
            P.wait("tensor", "s")
            for tb in range(4):
                P.do("tensor", lambda E, S, j=j, tb=tb, p=PS4[tb]:
                     E.matmul(p[0:1, :], onescol[:], xbf[0][:, tb * 512:(tb + 1) * 512], start=(j == 0), stop=(j == NCH - 1)))
            for tb in range(4):
                tgt = [psE[0:1, :], psF[0:1, :], psS[0:1, 0:512], psS[0:1, 512:1024]][tb]
                P.do("tensor", lambda E, S, j=j, tb=tb, tg=tgt:
                     E.matmul(tg, onescol[:], xsq[0][:, tb * 512:(tb + 1) * 512], start=(j == 0), stop=(j == NCH - 1)),
                     inc="t" if tb == 3 else None)
            rel[bi] = [("v", P.cnt["v"]), ("s", P.cnt["s"])]
        P.wait("vector", "t")
        ln_rows(T, 0, s1rows, s2rows, rowAT, rowBT)
        P.wait("tensor", "v")
        for which, row in ((0, rowAT), (1, rowBT)):
            bcast(row, T, PS4)
            P.wait("vector", "t")
            for tb in range(4):
                P.do("vector", lambda E, S, w=which, tb=tb, p=PS4[tb]:
                     E.tensor_copy(RBs[:, w * T + tb * 512:w * T + (tb + 1) * 512], p[:, :]), inc="v" if tb == 3 else None)
            P.wait("tensor", "v")
        rel = [P.snap(), P.snap()]
        for j in range(NCH):
            bi = j % 2
            P.waits("sync", rel[bi])
            P.do("sync", lambda E, S, j=j, b=bi: E.dma_start(out=xbuf[b][:], in_=xT[j * 128:(j + 1) * 128, :]), inc=f"xb{bi}", n=16)
            P.wait("vector", f"xb{bi}")
            P.do("vector", lambda E, S, b=bi: E.tensor_mul(xbuf[b][:], xbuf[b][:], RBs[:, 0:T]))
            P.do("vector", lambda E, S, j=j, b=bi: E.tensor_add(LNx[:, j * T:(j + 1) * T], xbuf[b][:], RBs[:, T:2 * T]), inc="v")
            rel[bi] = [("v", P.cnt["v"])]
        P.barrier()
        a1b.close()
        if phase == 1:
            for j in range(NCH):
                P.do("gpsimd", lambda E, S, j=j: E.dma_start(out=dbg[j * 128:(j + 1) * 128, :], in_=LNx[:, j * T:(j + 1) * T]), inc="gin", n=16)

        if phase >= 2:
            a2 = contextlib.ExitStack(); stacks.append(a2); stacks.append(a2)
            wvb = [sbt(a2, f"wvb{i}", [128, NCH * 512], BF16) for i in range(2)]
            relw = [[], []]
            relv4 = [[("v", P.cnt["v"])] for _ in range(4)]
            for fb in range(4):
                bi = fb % 2
                P.waits("gpsimd", relw[bi])
                for ci in range(NCH):
                    P.do("gpsimd", lambda E, S, fb=fb, ci=ci, b=bi:
                         E.dma_start(out=wvb[b][:, ci * 512:(ci + 1) * 512], in_=wvT[ci, :, fb * 512:(fb + 1) * 512]), inc=f"g{bi}", n=16)
                P.wait("tensor", f"g{bi}")
                for ti in range(NCH):
                    pst = PS4[ti % 4]
                    P.waits("tensor", relv4[ti % 4])
                    for ci in range(NCH):
                        P.do("tensor", lambda E, S, ti=ti, ci=ci, b=bi, p=pst:
                             E.matmul(p[:, :], LNx[:, ci * T + ti * 128:ci * T + (ti + 1) * 128], wvb[b][:, ci * 512:(ci + 1) * 512],
                                      start=(ci == 0), stop=(ci == NCH - 1)), inc="t" if ci == NCH - 1 else None)
                    P.wait("vector", "t")
                    P.do("vector", lambda E, S, ti=ti, fb=fb, p=pst:
                         E.tensor_copy(V[:, ti * T + fb * 512:ti * T + (fb + 1) * 512], p[:, :]), inc="v")
                    relv4[ti % 4] = [("v", P.cnt["v"])]
                relw[bi] = [("t", P.cnt["t"])]
            a2.close()
            P.barrier()
            if phase == 2:
                for ti in range(NCH):
                    P.do("gpsimd", lambda E, S, ti=ti: E.dma_start(out=dbg[ti * 128:(ti + 1) * 128, :], in_=V[:, ti * T:(ti + 1) * T]), inc="gin", n=16)

        if phase >= 3:
            a3 = contextlib.ExitStack(); stacks.append(a3)
            rkcS = sbt(a3, "rkcS", [128, T], BF16)
            rksS = sbt(a3, "rksS", [128, T], BF16)
            emask = sbt(a3, "emask_s", [128, max(MASK_SLOTS, 1) * 128], BF16)
            wkb = [sbt(a3, f"wkb{i}", [128, C], BF16) for i in range(2)]
            wqb = [sbt(a3, f"wqb{i}", [128, C], BF16) for i in range(2)]
            krot3 = [sbt(a3, f"krot{i}", [128, T], BF16) for i in range(3)]
            qhat3 = [sbt(a3, f"qhat{i}", [128, NQ], BF16) for i in range(2)]
            ksc3 = [sbt(a3, f"ksc{i}", [128, 16]) for i in range(3)]
            qrot = sbt(a3, "qrot", [128, NQ], BF16)
            sqt = sbt(a3, "sqt", [128, T], BF16)
            tmpf = sbt(a3, "tmpf", [128, 1024], BF16)
            rSb = sbt(a3, "rSb", [128, 512])
            tshuf = rSb
            ebuf = [sbt(a3, f"ebuf{i}", [128, 512], BF16) for i in range(2)]
            aout1 = [sbt(a3, f"aout1_{i}", [128, NQ], BF16) for i in range(2)]
            for src, dst in ((rkc, rkcS), (rks, rksS)):
                P.do("gpsimd", lambda E, S, s_=src, d_=dst: E.dma_start(out=d_[:], in_=s_), inc="din", n=16)
            if MASK_SLOTS > 0:
                P.do("sync", lambda E, S: E.dma_start(out=emask[:], in_=emaskD), inc="din", n=16)
            P.wait("vector", "din")
            P.wait("scalar", "din")
            rel_ao = [[], []]
            rel_wkb = [[], []]
            rel_wqb = [[], []]
            ksum_done = [0]
            qstats_done = [0]
            LAST_KT = {0: max(kt for kt in range(16) if NA_U[kt] > 0),
                       1: max(kt for kt in range(16) if NB_U[kt] > 0)}
            ITERS = [(kt, b_, PROFS[b_][kt]) for kt in range(16) for b_ in (0, 1) if PROFS[b_][kt] > 0]

            def prepK(h):
                par, bi = h % 3, h % 2
                krot, ksc = krot3[par], ksc3[par]
                steps = []
                kld = [0]

                def s_loadk():
                    P.waits("gpsimd", rel_wkb[bi])
                    P.do("gpsimd", lambda E, S, h=h, b=bi: E.dma_start(out=wkb[b][:], in_=wkT[h]), inc=f"g{bi}", n=16)
                    kld[0] = P.cnt[f"g{bi}"]
                steps.append(s_loadk)

                def mk_kprod(half):
                    def f():
                        P.wait("tensor", f"g{bi}", kld[0])
                        P.wait("tensor", "v", P.cnt["v"])
                        for ci in range(NCH):
                            for q_, p in ((0, psC), (1, psD)):
                                tb = 2 * half + q_
                                P.do("tensor", lambda E, S, ci=ci, tb=tb, b=bi, p=p:
                                     E.matmul(p[:, :], wkb[b][:, ci * 128:(ci + 1) * 128],
                                              LNx[:, ci * T + tb * 512:ci * T + (tb + 1) * 512],
                                              start=(ci == 0), stop=(ci == NCH - 1)),
                                     inc="t" if (ci == NCH - 1 and q_ == 1) else None)
                        if half == 1:
                            rel_wkb[bi] = [("t", P.cnt["t"])]
                    return f

                def mk_krope(half):
                    def f():
                        P.wait("vector", "t")
                        for q_, p in ((0, psC), (1, psD)):
                            tb = 2 * half + q_
                            sl = slice(tb * 512, (tb + 1) * 512)
                            P.do("vector", lambda E, S, p=p, sl=sl: E.tensor_mul(tmpf[:, 0:512], p[:, :], rkcS[:, sl]))
                            P.do("vector", lambda E, S, p=p: E.stream_shuffle(tshuf[:, 0:512], p[:, :], SHUF))
                            P.do("vector", lambda E, S, sl=sl: E.tensor_mul(tmpf[:, 512:1024], tshuf[:, 0:512], rksS[:, sl]))
                            P.do("vector", lambda E, S, sl=sl, kr=krot: E.tensor_add(kr[:, sl], tmpf[:, 0:512], tmpf[:, 512:1024]), inc="v")
                    return f

                for half in (0, 1):
                    steps.append(mk_kprod(half))
                    steps.append(mk_krope(half))

                def s_ksq():
                    P.wait("scalar", "t", ksum_done[0])
                    P.wait("scalar", "t", qstats_done[0])
                    P.wait("scalar", "v")
                    P.do("scalar", lambda E, S, kr=krot: E.activation(sqt[:, 0:T], kr[:, 0:T], AF.Square), inc="s")
                steps.append(s_ksq)

                def mk_ksum(r):
                    def f():
                        P.wait("tensor", "s")
                        P.wait("tensor", "v", P.cnt["v"])
                        for q_, p in ((0, psC), (1, psD)):
                            kt = 2 * r + q_
                            P.do("tensor", lambda E, S, kt=kt, p=p:
                                 E.matmul(p[:, 0:1], sqt[:, kt * 128:(kt + 1) * 128], onescol[:], start=True, stop=True),
                                 inc="t" if q_ == 1 else None)
                        if r == 7:
                            ksum_done[0] = P.cnt["t"]
                        P.wait("vector", "t")
                        for q_, p in ((0, psC), (1, psD)):
                            kt = 2 * r + q_
                            P.do("vector", lambda E, S, kt=kt, p=p, k=ksc: E.tensor_copy(k[:, kt:kt + 1], p[:, 0:1]), inc="v")
                    return f

                for r in range(8):
                    steps.append(mk_ksum(r))

                def s_kscrow():
                    P.wait("scalar", "v")
                    P.do("scalar", lambda E, S, k=ksc: E.activation(k[:, 0:16], k[:, 0:16], AF.Sqrt, bias=epsT[:, 1:2], scale=1.0 / 128.0), inc="s")
                    P.wait("vector", "s")
                    P.do("vector", lambda E, S, k=ksc: E.reciprocal(k[:, 0:16], k[:, 0:16]))
                    P.do("vector", lambda E, S, k=ksc: E.tensor_scalar_mul(k[:, 0:16], k[:, 0:16], SCALE), inc="v")
                steps.append(s_kscrow)
                return steps

            def prepQ(h):
                bi = h % 2
                qhat = qhat3[bi]
                steps = []
                qld = [0]

                def s_loadq():
                    P.waits("gpsimd", rel_wqb[bi])
                    P.do("gpsimd", lambda E, S, h=h, b=bi: E.dma_start(out=wqb[b][:], in_=wqT[h]), inc=f"gin{bi}", n=16)
                    qld[0] = P.cnt[f"gin{bi}"]
                steps.append(s_loadq)

                def s_qprod():
                    P.wait("tensor", f"gin{bi}", qld[0])
                    P.wait("tensor", "v", P.cnt["v"])
                    for ci in range(NCH):
                        for q_, p in ((0, psC), (1, psD)):
                            P.do("tensor", lambda E, S, ci=ci, tb=q_, b=bi, p=p:
                                 E.matmul(p[:, :], wqb[b][:, ci * 128:(ci + 1) * 128],
                                          LNx[:, ci * T + tb * 512:ci * T + (tb + 1) * 512],
                                          start=(ci == 0), stop=(ci == NCH - 1)),
                                 inc="t" if (ci == NCH - 1 and q_ == 1) else None)
                    rel_wqb[bi] = [("t", P.cnt["t"])]
                steps.append(s_qprod)

                def s_qrope():
                    P.wait("vector", "t")
                    for q_, p in ((0, psC), (1, psD)):
                        sl = slice(q_ * 512, (q_ + 1) * 512)
                        P.do("vector", lambda E, S, p=p, sl=sl: E.tensor_mul(tmpf[:, 0:512], p[:, :], rkcS[:, sl]))
                        P.do("vector", lambda E, S, p=p: E.stream_shuffle(tshuf[:, 0:512], p[:, :], SHUF))
                        P.do("vector", lambda E, S, sl=sl: E.tensor_mul(tmpf[:, 512:1024], tshuf[:, 0:512], rksS[:, sl]))
                        P.do("vector", lambda E, S, sl=sl: E.tensor_add(qrot[:, sl], tmpf[:, 0:512], tmpf[:, 512:1024]), inc="v")
                    P.wait("scalar", "t", ksum_done[0])
                    P.wait("scalar", "v")
                    P.do("scalar", lambda E, S: E.activation(sqt[:, 0:NQ], qrot[:, 0:NQ], AF.Square), inc="s")
                steps.append(s_qrope)

                def s_qstats():
                    P.wait("tensor", "s")
                    P.wait("tensor", "v", P.cnt["v"])
                    for q_, p in ((0, psC), (1, psD)):
                        P.do("tensor", lambda E, S, tb=q_, p=p:
                             E.matmul(p[0:1, :], onescol[:], sqt[:, tb * 512:(tb + 1) * 512], start=True, stop=True),
                             inc="t" if q_ == 1 else None)
                    qstats_done[0] = P.cnt["t"]
                    P.wait("vector", "t")
                    P.do("vector", lambda E, S: E.tensor_copy(rowB[0:1, 0:512], psC[0:1, :]))
                    P.do("vector", lambda E, S: E.tensor_copy(rowB[0:1, 512:1024], psD[0:1, :]), inc="v")
                    P.wait("scalar", "v")
                    P.do("scalar", lambda E, S: E.activation(rowB[0:1, 0:NQ], rowB[0:1, 0:NQ], AF.Sqrt, bias=epsT[0:1, 1:2], scale=1.0 / 128.0), inc="s")
                    P.wait("vector", "s")
                    P.do("vector", lambda E, S: E.reciprocal(rowB[0:1, 0:NQ], rowB[0:1, 0:NQ]), inc="v")
                steps.append(s_qstats)

                def s_qhat():
                    P.wait("tensor", "v")
                    P.do("tensor", lambda E, S: E.matmul(psC[:, :], onesrow[:], rowB[0:1, 0:512], start=True, stop=True))
                    P.do("tensor", lambda E, S: E.matmul(psD[:, :], onesrow[:], rowB[0:1, 512:1024], start=True, stop=True), inc="t")
                    P.wait("vector", "t")
                    for q_, p in ((0, psC), (1, psD)):
                        P.do("vector", lambda E, S, tb=q_, p=p, q=qhat:
                             E.tensor_mul(q[:, tb * 512:(tb + 1) * 512], qrot[:, tb * 512:(tb + 1) * 512], p[:, :]),
                             inc="v" if q_ == 1 else None)
                steps.append(s_qhat)
                return steps

            def make_epi(h):
                ao = aout1[h % 2]
                epi_rel = {}

                def e1():
                    P.wait("tensor", "v", P.cnt["v"])
                    P.do("tensor", lambda E, S: E.matmul(psC[:, :], onesrow[:], rowA[0:1, 0:512], start=True, stop=True))
                    P.do("tensor", lambda E, S: E.matmul(psD[:, :], onesrow[:], rowA[0:1, 512:1024], start=True, stop=True), inc="t")

                def e2():
                    P.wait("vector", "t")
                    P.waits("vector", rel_ao[h % 2])
                    for tb in range(2):
                        P.do("vector", lambda E, S, tb=tb, p=[psC, psD][tb]: E.reciprocal(rSb[:, 0:512], p[:, :]))
                        P.do("vector", lambda E, S, tb=tb, pv=[psE, psF][tb], a=ao:
                             E.tensor_mul(a[:, tb * 512:(tb + 1) * 512], pv[:, :], rSb[:, 0:512]), inc="v" if tb == 1 else None)
                    epi_rel["r"] = [("v", P.cnt["v"])]
                    P.wait("sync", "v")
                    P.do("sync", lambda E, S, h=h, a=ao: E.dma_start(out=aoscr[h * 128:(h + 1) * 128, :], in_=a[:]), inc=f"spl{h % 2}", n=16)
                    if phase == 3:
                        P.wait("gpsimd", "v")
                        P.do("gpsimd", lambda E, S, h=h, a=ao: E.dma_start(out=dbg[h * 128:(h + 1) * 128, 0:NQ], in_=a[:]), inc=f"gin{h % 2}", n=16)
                        rel_ao[h % 2] = [(f"spl{h % 2}", P.cnt[f"spl{h % 2}"]), (f"gin{h % 2}", P.cnt[f"gin{h % 2}"])]
                    else:
                        rel_ao[h % 2] = [(f"spl{h % 2}", P.cnt[f"spl{h % 2}"])]
                return [e1, e2], epi_rel

            for st_ in prepK(0):
                st_()
            for st_ in prepQ(0):
                st_()
            for st_ in prepK(1):
                st_()
            prep_rel = [("t", P.cnt["t"]), ("v", P.cnt["v"]), ("s", P.cnt["s"])]
            pend_epi = None

            for h in range(NH):
                krot, ksc, qhat = krot3[h % 3], ksc3[h % 3], qhat3[h % 2]
                steps = list(pend_epi[0]) if pend_epi else []
                epi_rel_prev = pend_epi[1] if pend_epi else None
                steps += prepQ(h + 1) if h + 1 < NH else []
                steps += prepK(h + 2) if h + 2 < NH else []
                si = 0
                P.waits("tensor", prep_rel)
                P.waits("scalar", prep_rel)
                P.wait("tensor", "s", P.cnt["s"])
                exp_done = []
                ready = []

                def emit_scores(i):
                    kt, b_, nU = ITERS[i]
                    qc0 = b_ * 512 + (4 - nU) * 128
                    ncols = nU * 128
                    psc = [psA, psB][i % 2]
                    eb = ebuf[i % 2]
                    if i >= 2:
                        P.wait("tensor", "s", exp_done[i - 2])
                    P.do("tensor", lambda E, S, kt=kt, qc0=qc0, n=ncols, p=psc, kr=krot, q=qhat:
                         E.matmul(p[:, 0:n], kr[:, kt * 128:(kt + 1) * 128], q[:, qc0:qc0 + n], start=True, stop=True), inc="t")
                    P.wait("scalar", "t")
                    P.do("scalar", lambda E, S, kt=kt, n=ncols, p=psc, e=eb, k=ksc:
                         E.activation(e[:, 0:n], p[:, 0:n], AF.Exp, scale=k[:, kt:kt + 1]), inc="s")
                    exp_done.append(P.cnt["s"])
                    mo = MASK_OFF.get((kt, b_))
                    if mo is not None:
                        o0, L = mo
                        P.wait("vector", "s")
                        P.do("vector", lambda E, S, e=eb, o0=o0, L=L:
                             E.tensor_mul(e[:, 0:L * 128], e[:, 0:L * 128], emask[:, o0 * 128:(o0 + L) * 128]), inc="v")
                        ready.append([("v", P.cnt["v"])])
                    else:
                        ready.append([("s", P.cnt["s"])])

                def emit_sumpv(i):
                    kt, b_, nU = ITERS[i]
                    qc0 = b_ * 512 + (4 - nU) * 128
                    ncols = nU * 128
                    eb = ebuf[i % 2]
                    first, last = (kt == 0), (kt == LAST_KT[b_])
                    if first and epi_rel_prev is not None:
                        P.waits("tensor", epi_rel_prev.get("r", []))
                    P.waits("tensor", ready[i])
                    P.do("tensor", lambda E, S, qc0=qc0, n=ncols, e=eb, f=first, l=last:
                         E.matmul(psS[0:1, qc0:qc0 + n], onescol[:], e[:, 0:n], start=f, stop=l))
                    P.do("tensor", lambda E, S, kt=kt, b_=b_, qc0=qc0, n=ncols, e=eb, f=first, l=last, h=h:
                         E.matmul([psE, psF][b_][:, qc0 - b_ * 512:qc0 - b_ * 512 + n],
                                  V[:, kt * T + h * 128:kt * T + (h + 1) * 128], e[:, 0:n], start=f, stop=l), inc="t")

                epi_n = 2 if pend_epi else 0
                for i in range(len(ITERS) + 1):
                    if i < len(ITERS):
                        emit_scores(i)
                    if i == 0:
                        while si < epi_n:
                            steps[si]()
                            si += 1
                    if i >= 1:
                        emit_sumpv(i - 1)
                    want = ((i + 1) * len(steps)) // (len(ITERS) + 1)
                    while si < want:
                        steps[si]()
                        si += 1
                while si < len(steps):
                    steps[si]()
                    si += 1
                prep_rel = [("t", P.cnt["t"]), ("v", P.cnt["v"]), ("s", P.cnt["s"])]
                P.wait("vector", "t")
                P.do("vector", lambda E, S: E.tensor_copy(rowA[0:1, 0:NQ], psS[0:1, 0:1024]), inc="v")
                pend_epi = make_epi(h)
            for st_ in pend_epi[0]:
                st_()
            a3.close()
            P.barrier()

        if phase >= 4:
            a1.close()
            a4o = contextlib.ExitStack(); stacks.append(a4o); stacks.append(a4o)
            xqS = sbt(a4o, "xqS", [128, NCH * NQ], BF16)
            wgb = [sbt(a4o, f"wgb{i}", [128, C], BF16) for i in range(2)]
            wub = [sbt(a4o, f"wub{i}", [128, C], BF16) for i in range(2)]
            a4m = contextlib.ExitStack(); stacks.append(a4m); stacks.append(a4m)
            x2S = sbt(a4m, "x2S", [128, NCH * NQ])
            a4 = contextlib.ExitStack(); stacks.append(a4); stacks.append(a4)
            aoutS = sbt(a4, "aoutS", [128, NCH * NQ], BF16)
            wob = [sbt(a4, f"wob{i}", [128, C], BF16) for i in range(2)]
            abf = [sbt(a4, f"abf{i}", [128, NQ], BF16) for i in range(2)]
            asq = [sbt(a4, f"asq{i}", [128, NQ], BF16) for i in range(2)]
            xoc = [sbt(a4, f"xoc{i}", [128, NQ]) for i in range(2)]
            for j in range(NCH):
                P.do("gpsimd", lambda E, S, j=j: E.dma_start(out=aoutS[:, j * NQ:(j + 1) * NQ], in_=aoscr[j * 128:(j + 1) * 128, :]), inc="gin", n=16)
            relw = [[], []]
            relpp = [[("v", P.cnt["v"])], [("v", P.cnt["v"])]]
            pend_wstats = None
            for fc in range(NCH):
                bi = fc % 2
                pp = [psA, psB] if bi == 0 else [psC, psD]
                P.waits("gpsimd", relw[bi])
                P.do("gpsimd", lambda E, S, fc=fc, b=bi: E.dma_start(out=wob[b][:], in_=woT[fc]), inc=f"g{bi}", n=16)
                P.wait("tensor", f"g{bi}")
                P.wait("tensor", "gin")
                P.waits("tensor", relpp[bi])
                for ci in range(NCH):
                    for tb in range(2):
                        P.do("tensor", lambda E, S, ci=ci, tb=tb, b=bi, p=pp[tb]:
                             E.matmul(p[:, :], wob[b][:, ci * 128:(ci + 1) * 128], aoutS[:, ci * NQ + tb * 512:ci * NQ + (tb + 1) * 512],
                                      start=(ci == 0), stop=(ci == NCH - 1)), inc="t" if (ci == NCH - 1 and tb == 1) else None)
                relw[bi] = [("t", P.cnt["t"])]
                if pend_wstats is not None:
                    pend_wstats()
                P.wait("vector", "t")
                for tb in range(2):
                    P.do("vector", lambda E, S, fc=fc, tb=tb, p=pp[tb]:
                         E.tensor_copy(x2S[:, fc * NQ + tb * 512:fc * NQ + (tb + 1) * 512], p[:, :]), inc="v" if tb == 1 else None)
                relpp[bi] = [("v", P.cnt["v"])]
                P.do("vector", lambda E, S, fc=fc, ab=abf[bi]: E.tensor_copy(ab[:], x2S[:, fc * NQ:(fc + 1) * NQ]), inc="v")
                P.wait("scalar", "v")
                P.do("scalar", lambda E, S, ab=abf[bi], aq=asq[bi]: E.activation(aq[:], ab[:], AF.Square), inc="s")

                def mk_wstats(fc=fc, scnt=P.cnt["s"], ab=abf[bi], aq=asq[bi]):
                    def f():
                        P.wait("tensor", "s", scnt)
                        for tb in range(2):
                            P.do("tensor", lambda E, S, tb=tb, tg=[psS[0:1, 0:512], psS[0:1, 512:1024]][tb]:
                                 E.matmul(tg, onescol[:], ab[:, tb * 512:(tb + 1) * 512], start=(fc == 0), stop=(fc == NCH - 1)))
                        for tb in range(2):
                            P.do("tensor", lambda E, S, tb=tb, tg=[psE[0:1, :], psF[0:1, :]][tb]:
                                 E.matmul(tg, onescol[:], aq[:, tb * 512:(tb + 1) * 512], start=(fc == 0), stop=(fc == NCH - 1)),
                                 inc="t" if tb == 1 else None)
                    return f
                pend_wstats = mk_wstats()
            pend_wstats()
            P.wait("vector", "t")
            ln_rows(NQ, 0, [(psS[0:1, 0:512], 0, 512), (psS[0:1, 512:1024], 512, 512)],
                    [(psE[0:1, :], 0, 512), (psF[0:1, :], 512, 512)], rowA, rowB)
            P.wait("tensor", "v")
            bcast(rowA, NQ, [psA, psB])
            bcast(rowB, NQ, [psC, psD])
            P.wait("vector", "t")
            relx = [[], []]
            stA = [0, 0]
            for j in range(NCH):
                bi = j % 2
                P.waits("sync", relx[bi])
                P.do("sync", lambda E, S, j=j, b=bi: E.dma_start(out=xoc[b][:], in_=xown[j * 128:(j + 1) * 128, :]), inc=f"xb{bi}", n=16)
                P.wait("vector", f"xb{bi}")
                for tb in range(2):
                    sl = slice(j * NQ + tb * 512, j * NQ + (tb + 1) * 512)
                    P.do("vector", lambda E, S, sl=sl, p=[psA, psB][tb]: E.tensor_mul(x2S[:, sl], x2S[:, sl], p[:, :]))
                    P.do("vector", lambda E, S, sl=sl, p=[psC, psD][tb]: E.tensor_add(x2S[:, sl], x2S[:, sl], p[:, :]))
                    P.do("vector", lambda E, S, sl=sl, b=bi, tb=tb:
                         E.tensor_add(x2S[:, sl], x2S[:, sl], xoc[b][:, tb * 512:(tb + 1) * 512]), inc="v")
                relx[bi] = [("v", P.cnt["v"])]
                P.wait("sync", "v")
                P.do("sync", lambda E, S, j=j: E.dma_start(out=x2scr[j * 128:(j + 1) * 128, :], in_=x2S[:, j * NQ:(j + 1) * NQ]), inc="sx2", n=16)
                P.wait("vector", "t", stA[bi])
                P.do("vector", lambda E, S, j=j, ab=abf[bi]: E.tensor_copy(ab[:], x2S[:, j * NQ:(j + 1) * NQ]), inc="v")
                P.wait("scalar", "v")
                P.do("scalar", lambda E, S, ab=abf[bi], aq=asq[bi]: E.activation(aq[:], ab[:], AF.Square), inc="s")
                P.wait("tensor", "s")
                for tb in range(2):
                    P.do("tensor", lambda E, S, j=j, tb=tb, ab=abf[bi], tg=[psS[0:1, 0:512], psS[0:1, 512:1024]][tb]:
                         E.matmul(tg, onescol[:], ab[:, tb * 512:(tb + 1) * 512], start=(j == 0), stop=(j == NCH - 1)))
                for tb in range(2):
                    P.do("tensor", lambda E, S, j=j, tb=tb, aq=asq[bi], tg=[psE[0:1, :], psF[0:1, :]][tb]:
                         E.matmul(tg, onescol[:], aq[:, tb * 512:(tb + 1) * 512], start=(j == 0), stop=(j == NCH - 1)),
                         inc="t" if tb == 1 else None)
                stA[bi] = P.cnt["t"]
            a4.close()
            P.barrier()
            if phase == 4:
                for j in range(NCH):
                    P.do("gpsimd", lambda E, S, j=j: E.dma_start(out=dbg[j * 128:(j + 1) * 128, 0:NQ], in_=x2S[:, j * NQ:(j + 1) * NQ]), inc="gin", n=16)

        if phase >= 5:
            a5 = contextlib.ExitStack(); stacks.append(a5); stacks.append(a5)
            RB2 = sbt(a5, "RB2", [128, 2 * NQ])
            Abc = sbt(a5, "Abc", [128, NQ])
            yabs = sbt(a5, "yabs", [128, NQ])
            macc = sbt(a5, "macc", [128, NQ])
            tq = sbt(a5, "tq", [128, NQ])
            cf = sbt(a5, "cf", [128, NQ])
            b1t = sbt(a5, "b1t", [128, NQ])
            xi = sbt(a5, "xi", [128, NQ], I32)
            tcol = sbt(a5, "tcol", [128, 8])
            r3r = sbt(a5, "r3r", [1, NQ])

            gu_relw = [[], []]
            gu_sg = [(0, 0, 0)] * HC
            gu_spill = [0] * HC

            def gu_load_sign(hc, from_scratch):
                bi = hc % 2
                P.waits("gpsimd", gu_relw[bi])
                if not from_scratch:
                    P.do("gpsimd", lambda E, S, hc=hc, b=bi: E.dma_start(out=wgb[b][:], in_=wgT[hc]), inc=f"g{bi}", n=16)
                    P.do("gpsimd", lambda E, S, hc=hc, b=bi: E.dma_start(out=wub[b][:], in_=wuT[hc]), inc=f"g{bi}", n=16)
                    P.wait("scalar", f"g{bi}")
                    P.do("scalar", lambda E, S, b=bi: E.sign(wgb[b][:], wgb[b][:]))
                    P.do("scalar", lambda E, S, b=bi: E.sign(wub[b][:], wub[b][:]), inc="s")
                    P.wait("sync", "s")
                    P.do("sync", lambda E, S, hc=hc, b=bi: E.dma_start(out=sgg[hc], in_=wgb[b][:]), inc=f"sw{bi}", n=16)
                    P.do("sync", lambda E, S, hc=hc, b=bi: E.dma_start(out=sgu[hc], in_=wub[b][:]), inc=f"sw{bi}", n=16)
                    gu_spill[hc] = P.cnt[f"sw{bi}"]
                else:
                    P.wait("gpsimd", f"sw{bi}", gu_spill[hc])
                    P.do("gpsimd", lambda E, S, hc=hc, b=bi: E.dma_start(out=wgb[b][:], in_=sgg[hc]), inc=f"g{bi}", n=16)
                    P.do("gpsimd", lambda E, S, hc=hc, b=bi: E.dma_start(out=wub[b][:], in_=sgu[hc]), inc=f"g{bi}", n=16)
                gu_sg[hc] = (P.cnt["s"], P.cnt[f"g{bi}"], P.cnt[f"sw{bi}"])

            gu_load_sign(0, False)
            gu_load_sign(1, False)

            P.wait("vector", "t")
            ln_rows(NQ, 0, [(psS[0:1, 0:512], 0, 512), (psS[0:1, 512:1024], 512, 512)],
                    [(psE[0:1, :], 0, 512), (psF[0:1, :], 512, 512)], rowA, rowB)
            P.do("vector", lambda E, S: E.tensor_mul(r3r[0:1, :], rowA[0:1, 0:NQ], rowA[0:1, 0:NQ]))
            P.do("vector", lambda E, S: E.tensor_scalar(r3r[0:1, :], r3r[0:1, :], -LN_EPS, 1.0 + FFN_EPS, ALU.mult, ALU.add), inc="v")
            P.wait("scalar", "v")
            P.do("scalar", lambda E, S: E.activation(r3r[0:1, :], r3r[0:1, :], AF.Sqrt), inc="s")
            P.wait("vector", "s")
            P.do("vector", lambda E, S: E.reciprocal(r3r[0:1, :], r3r[0:1, :]), inc="v")
            P.wait("tensor", "v")
            bcast(rowA, NQ, [psA, psB])
            bcast(rowB, NQ, [psC, psD])
            P.wait("vector", "t")
            for tb in range(2):
                P.do("vector", lambda E, S, tb=tb, p=[psA, psB][tb]: E.tensor_copy(RB2[:, tb * 512:(tb + 1) * 512], p[:, :]))
                P.do("vector", lambda E, S, tb=tb, p=[psC, psD][tb]: E.tensor_copy(RB2[:, NQ + tb * 512:NQ + (tb + 1) * 512], p[:, :]), inc="v" if tb == 1 else None)

            P.wait("vector", "sx2")
            for j in range(NCH):
                P.do("vector", lambda E, S, j=j: E.tensor_mul(x2S[:, j * NQ:(j + 1) * NQ], x2S[:, j * NQ:(j + 1) * NQ], RB2[:, 0:NQ]))
                P.do("vector", lambda E, S, j=j: E.tensor_add(x2S[:, j * NQ:(j + 1) * NQ], x2S[:, j * NQ:(j + 1) * NQ], RB2[:, NQ:2 * NQ]), inc="v")
                P.wait("scalar", "v")
                P.do("scalar", lambda E, S, j=j: E.activation(yabs[:], x2S[:, j * NQ:(j + 1) * NQ], AF.Abs), inc="s")
                P.wait("vector", "s")
                if j == 0:
                    P.do("vector", lambda E, S: E.tensor_copy(macc[:], yabs[:]), inc="v")
                else:
                    P.do("vector", lambda E, S: E.tensor_max(macc[:], macc[:], yabs[:]), inc="v")
            P.wait("tensor", "v")
            for jb in range(8):
                p = [psA, psB][jb % 2]
                P.wait("tensor", "v", P.cnt["v"])
                P.do("tensor", lambda E, S, jb=jb, p=p:
                     E.transpose(p[:, 0:128], macc[:, jb * 128:(jb + 1) * 128], ident[:]), inc="t")
                P.wait("vector", "t")
                P.do("vector", lambda E, S, jb=jb, p=p:
                     E.reduce_max(tcol[:, jb:jb + 1], p[:, 0:128], axis=mybir.AxisListType.X), inc="v")
            P.wait("tensor", "v")
            for jb in range(8):
                p = [psC, psD][jb % 2]
                P.wait("tensor", "v", P.cnt["v"])
                P.do("tensor", lambda E, S, jb=jb, p=p:
                     E.matmul(p[0:1, 0:128], tcol[:, jb:jb + 1], ident[:], start=True, stop=True), inc="t")
                P.wait("vector", "t")
                P.do("vector", lambda E, S, jb=jb, p=p:
                     E.tensor_copy(rowB[0:1, jb * 128:(jb + 1) * 128], p[0:1, 0:128]), inc="v")
            P.do("vector", lambda E, S: E.tensor_mul(rowA[0:1, 0:NQ], r3r[0:1, :], rowB[0:1, 0:NQ]))
            P.do("vector", lambda E, S: E.tensor_scalar_mul(rowA[0:1, 0:NQ], rowA[0:1, 0:NQ], 1.0 / 127.0))
            P.do("vector", lambda E, S: E.reciprocal(rowB[0:1, 0:NQ], rowB[0:1, 0:NQ]))
            P.do("vector", lambda E, S: E.tensor_scalar_mul(rowB[0:1, 0:NQ], rowB[0:1, 0:NQ], 127.0), inc="v")
            P.wait("tensor", "v")
            bcast(rowB, NQ, [psA, psB])
            bcast(rowA, NQ, [psC, psD])
            P.wait("vector", "t")
            for tb in range(2):
                P.do("vector", lambda E, S, tb=tb, p=[psA, psB][tb]: E.tensor_copy(Abc[:, tb * 512:(tb + 1) * 512], p[:, :]))
                P.do("vector", lambda E, S, tb=tb, p=[psC, psD][tb]: E.tensor_copy(invsbc[:, tb * 512:(tb + 1) * 512], p[:, :]), inc="v" if tb == 1 else None)
            for j in range(NCH):
                P.do("vector", lambda E, S, j=j: E.tensor_mul(tq[:], x2S[:, j * NQ:(j + 1) * NQ], Abc[:]), inc="v")
                P.wait("scalar", "v")
                P.do("scalar", lambda E, S: E.activation(b1t[:], tq[:], AF.Sign), inc="s")
                P.wait("vector", "s")
                P.do("vector", lambda E, S: E.scalar_tensor_tensor(cf[:], b1t[:], 0.5, tq[:], ALU.mult, ALU.add), inc="v")
                P.wait("scalar", "v")
                P.do("scalar", lambda E, S: E.copy(xi[:], cf[:]))
                P.do("scalar", lambda E, S: E.copy(cf[:], xi[:]), inc="s")
                P.wait("vector", "s")
                P.do("vector", lambda E, S, j=j: E.tensor_scalar(xqS[:, j * NQ:(j + 1) * NQ], cf[:], -128.0, 127.0, ALU.max, ALU.min), inc="v")
            a5.close()
            a4m.close()
            P.barrier()
            if phase == 5:
                for j in range(NCH):
                    P.do("gpsimd", lambda E, S, j=j: E.dma_start(out=dbg[j * 128:(j + 1) * 128, 0:NQ], in_=xqS[:, j * NQ:(j + 1) * NQ]), inc="gin", n=16)
                P.do("gpsimd", lambda E, S: E.dma_start(out=dbg[0:128, NQ:2 * NQ], in_=invsbc[:]), inc="gin", n=16)

        if phase >= 6:
            a6 = contextlib.ExitStack(); stacks.append(a6); stacks.append(a6)
            pS = sbt(a6, "pS", [128, HC * 512], BF16)
            gt0 = sbt(a6, "gt0", [128, 512])
            gt1 = sbt(a6, "gt1", [128, 512])
            ffS = sbt(a6, "ffS", [128, NCH * 512])
            fbf = [sbt(a6, f"fbf{i}", [128, 512], BF16) for i in range(2)]
            fsq = [sbt(a6, f"fsq{i}", [128, 512], BF16) for i in range(2)]
            wdb = sbt(a6, "wdb", [128, 8192], BF16)
            x2c = [sbt(a6, f"x2c{i}", [128, 512]) for i in range(2)]
            ob = [sbt(a6, f"ob{i}", [128, 512]) for i in range(2)]
            RB4 = sbt(a6, "RB4", [128, 1024])
            wd_relw = [[], []]
            wd_spill = [0] * (NCH * 2)
            ob_rel = [[], []]
            relgu_carry = None
            for tb in range(2):
                tsl = slice(tb * 512, (tb + 1) * 512)
                if tb == 0:
                    relgu = [[("v", P.cnt["v"])], [("v", P.cnt["v"])]]
                else:
                    relgu = relgu_carry
                    gu_load_sign(0, True)
                    gu_load_sign(1, True)
                for hc in range(HC):
                    bi = hc % 2
                    ppG = [psA, psC][bi]
                    ppU = [psB, psD][bi]
                    if tb == 0:
                        P.wait("tensor", "s", gu_sg[hc][0])
                    else:
                        P.wait("tensor", f"g{bi}", gu_sg[hc][1])
                    P.waits("tensor", relgu[bi])
                    for ci in range(NCH):
                        P.do("tensor", lambda E, S, ci=ci, b=bi, p=ppG, tb=tb:
                             E.matmul(p[:, :], wgb[b][:, ci * 128:(ci + 1) * 128], xqS[:, ci * NQ + tb * 512:ci * NQ + (tb + 1) * 512],
                                      start=(ci == 0), stop=(ci == NCH - 1)))
                        P.do("tensor", lambda E, S, ci=ci, b=bi, p=ppU, tb=tb:
                             E.matmul(p[:, :], wub[b][:, ci * 128:(ci + 1) * 128], xqS[:, ci * NQ + tb * 512:ci * NQ + (tb + 1) * 512],
                                      start=(ci == 0), stop=(ci == NCH - 1)), inc="t" if ci == NCH - 1 else None)
                    gu_relw[bi] = [("t", P.cnt["t"])] + ([(f"sw{bi}", gu_sg[hc][2])] if tb == 0 else [])
                    if hc + 2 < HC:
                        gu_load_sign(hc + 2, tb == 1)
                    P.wait("vector", "t")
                    P.do("vector", lambda E, S, p=ppG, tsl=tsl: E.tensor_mul(gt0[:], p[:, :], invsbc[:, tsl]), inc="v")
                    P.wait("scalar", "v")
                    P.do("scalar", lambda E, S: E.activation(gt1[:], gt0[:], AF.Silu), inc="s")
                    P.wait("vector", "s")
                    P.do("vector", lambda E, S, p=ppU: E.tensor_mul(gt0[:], gt1[:], p[:, :]))
                    P.do("vector", lambda E, S, hc=hc, tsl=tsl: E.tensor_mul(pS[:, hc * 512:(hc + 1) * 512], gt0[:], invsbc[:, tsl]), inc="v")
                    relgu[bi] = [("v", P.cnt["v"])]
                relgu_carry = [list(relgu[0]), list(relgu[1])]
                relpw = [[("v", P.cnt["v"])], [("v", P.cnt["v"])]]
                sgdone = [(0, 0)] * 2
                pend_stats = None
                for fc in range(NCH):
                    pw = [psE, psF][fc % 2]
                    for hf in range(2):
                        P.waits("gpsimd", wd_relw[hf])
                        if tb == 0:
                            P.do("gpsimd", lambda E, S, fc=fc, hf=hf:
                                 E.dma_start(out=wdb[:, hf * 4096:(hf + 1) * 4096], in_=wdT[fc, :, hf * 4096:(hf + 1) * 4096]), inc=f"g{hf}", n=16)
                            P.wait("scalar", f"g{hf}")
                            P.do("scalar", lambda E, S, hf=hf: E.sign(wdb[:, hf * 4096:(hf + 1) * 4096], wdb[:, hf * 4096:(hf + 1) * 4096]), inc="s")
                            P.wait("sync", "s")
                            P.do("sync", lambda E, S, fc=fc, hf=hf:
                                 E.dma_start(out=sgd[fc, :, hf * 4096:(hf + 1) * 4096], in_=wdb[:, hf * 4096:(hf + 1) * 4096]), inc=f"sw{hf}", n=16)
                            wd_spill[fc * 2 + hf] = P.cnt[f"sw{hf}"]
                        else:
                            P.wait("gpsimd", f"sw{hf}", wd_spill[fc * 2 + hf])
                            P.do("gpsimd", lambda E, S, fc=fc, hf=hf:
                                 E.dma_start(out=wdb[:, hf * 4096:(hf + 1) * 4096], in_=sgd[fc, :, hf * 4096:(hf + 1) * 4096]), inc=f"g{hf}", n=16)
                        sgdone[hf] = (P.cnt["s"], P.cnt[f"g{hf}"])
                    P.waits("tensor", relpw[fc % 2])
                    for hc in range(HC):
                        if hc == 0:
                            P.wait("tensor", "s" if tb == 0 else "g0", sgdone[0][0 if tb == 0 else 1])
                        if hc == HC // 2:
                            P.wait("tensor", "s" if tb == 0 else "g1", sgdone[1][0 if tb == 0 else 1])
                        P.do("tensor", lambda E, S, hc=hc, p=pw:
                             E.matmul(p[:, :], wdb[:, hc * 128:(hc + 1) * 128], pS[:, hc * 512:(hc + 1) * 512],
                                      start=(hc == 0), stop=(hc == HC - 1)), inc="t" if hc in (HC // 2 - 1, HC - 1) else None)
                        if hc == HC // 2 - 1:
                            wd_relw[0] = [("t", P.cnt["t"])] + ([("sw0", wd_spill[fc * 2])] if tb == 0 else [])
                    wd_relw[1] = [("t", P.cnt["t"])] + ([("sw1", wd_spill[fc * 2 + 1])] if tb == 0 else [])
                    if pend_stats is not None:
                        pend_stats()
                    P.wait("vector", "t")
                    P.do("vector", lambda E, S, fc=fc, p=pw: E.tensor_copy(ffS[:, fc * 512:(fc + 1) * 512], p[:, :]))
                    P.do("vector", lambda E, S, fc=fc, fb=fbf[fc % 2]: E.tensor_copy(fb[:], ffS[:, fc * 512:(fc + 1) * 512]), inc="v")
                    relpw[fc % 2] = [("v", P.cnt["v"])]
                    P.wait("scalar", "v")
                    P.do("scalar", lambda E, S, fb=fbf[fc % 2], fq=fsq[fc % 2]: E.activation(fq[:], fb[:], AF.Square), inc="s")

                    def mk_stats(fc=fc, scnt=P.cnt["s"], fb=fbf[fc % 2], fq=fsq[fc % 2]):
                        def f():
                            P.wait("tensor", "s", scnt)
                            P.do("tensor", lambda E, S:
                                 E.matmul(psS[0:1, 0:512], onescol[:], fb[:], start=(fc == 0), stop=(fc == NCH - 1)))
                            P.do("tensor", lambda E, S:
                                 E.matmul(psS[0:1, 512:1024], onescol[:], fq[:], start=(fc == 0), stop=(fc == NCH - 1)),
                                 inc="t")
                        return f
                    pend_stats = mk_stats()
                pend_stats()
                P.wait("vector", "t")
                ln_rows(512, 0, [(psS[0:1, 0:512], 0, 512)], [(psS[0:1, 512:1024], 0, 512)], rowA, rowB)
                P.wait("tensor", "v")
                bcast(rowA, 512, [psE])
                bcast(rowB, 512, [psF])
                P.wait("vector", "t")
                P.do("vector", lambda E, S: E.tensor_copy(RB4[:, 0:512], psE[:, :]))
                P.do("vector", lambda E, S: E.tensor_copy(RB4[:, 512:1024], psF[:, :]), inc="v")
                relx = [[], []]
                for fc in range(NCH):
                    bi = fc % 2
                    P.waits("sync", relx[bi])
                    P.do("sync", lambda E, S, fc=fc, b=bi, tb=tb:
                         E.dma_start(out=x2c[b][:], in_=x2scr[fc * 128:(fc + 1) * 128, tb * 512:(tb + 1) * 512]), inc=f"xb{bi}", n=16)
                    P.wait("vector", f"xb{bi}")
                    P.waits("vector", ob_rel[bi])
                    P.do("vector", lambda E, S, fc=fc: E.tensor_mul(gt0[:], ffS[:, fc * 512:(fc + 1) * 512], RB4[:, 0:512]))
                    P.do("vector", lambda E, S: E.tensor_add(gt0[:], gt0[:], RB4[:, 512:1024]))
                    P.do("vector", lambda E, S, b=bi: E.tensor_add(ob[b][:], gt0[:], x2c[b][:]), inc="v")
                    relx[bi] = [("v", P.cnt["v"])]
                    if bi == 0:
                        P.wait("sync", "v")
                        P.dma("sync", lambda E, S, fc=fc, tb=tb, b=bi:
                              E.dma_start(out=out[fc * 128:(fc + 1) * 128, tb * 512:(tb + 1) * 512], in_=ob[b][:]), "sout")
                        ob_rel[bi] = [("sout", P.cnt["sout"])]
                    else:
                        P.wait("scalar", "v")
                        P.dma("scalar", lambda E, S, fc=fc, tb=tb, b=bi:
                              E.dma_start(out=out[fc * 128:(fc + 1) * 128, tb * 512:(tb + 1) * 512], in_=ob[b][:]), "gout")
                        ob_rel[bi] = [("gout", P.cnt["gout"])]
            a6.close()

        P.barrier()
        for st_ in reversed(stacks):
            st_.close()
        ops = P.ops

        @blk.sync
        def _(E):
            for f in ops["sync"]:
                f(E, sems)

        @blk.gpsimd
        def _(E):
            for f in ops["gpsimd"]:
                f(E, sems)

        @blk.vector
        def _(E):
            for f in ops["vector"]:
                f(E, sems)

        @blk.scalar
        def _(E):
            for f in ops["scalar"]:
                f(E, sems)

        @blk.tensor
        def _(E):
            for f in ops["tensor"]:
                f(E, sems)

    return nc



def _lhst_blocks(W):
    F_, Ci = W.shape
    A = np.asarray(W).reshape(F_ // 128, 128, Ci // 128, 128)
    return np.ascontiguousarray(A.transpose(0, 3, 2, 1).reshape(F_ // 128, 128, Ci))


def _rope_tables():
    import jax
    import jax.numpy as jnp
    with jax.default_device(jax.devices("cpu")[0]):
        inv = 1.0 / 10000.0 ** (jnp.arange(0, 128, 2, dtype=jnp.float32) / 128)
        t = jnp.arange(T, dtype=jnp.float32)
        fr = t[:, None] * inv[None, :]
        sin = np.asarray(jnp.repeat(jnp.sin(fr), 2, axis=-1))
        cos = np.asarray(jnp.repeat(jnp.cos(fr), 2, axis=-1))
    return sin, cos


def _own_positions(p):
    return np.concatenate([np.arange(128 * t, 128 * (t + 1)) for t in OWN[p]])


def _perm_positions(p):
    return np.concatenate([np.arange(128 * t, 128 * (t + 1)) for t in PERM[p]])


def _emask_for(p):
    tri = (np.arange(128)[:, None] <= np.arange(128)[None, :]).astype(ml_dtypes.bfloat16)
    m = np.ones((128, max(MASK_SLOTS, 1) * 128), dtype=ml_dtypes.bfloat16)
    for (ks, b_), (o0, L) in MASK_OFF.items():
        nU = PROFS[b_][ks]
        g = PERM[p][ks]
        for s_ in range(L):
            qslot = b_ * 4 + (4 - nU) + s_
            qt = PERM[p][qslot]
            sl = slice((o0 + s_) * 128, (o0 + s_ + 1) * 128)
            if qt < g:
                m[:, sl] = 0
            elif qt == g:
                m[:, sl] = tri
    return m


def make_inmaps(L0_attn_x, L0_attn_W_qkv, L0_attn_W_o, L0_ffn_W_g, L0_ffn_W_u, L0_ffn_W_d):
    x = np.asarray(L0_attn_x)
    Wqkv = np.asarray(L0_attn_W_qkv)
    shared = {
        "wqT": _lhst_blocks(Wqkv[0:2048]),
        "wkT": _lhst_blocks(Wqkv[2048:4096]),
        "wvT": np.ascontiguousarray(Wqkv[4096:6144].T.reshape(NCH, 128, C)),
        "woT": _lhst_blocks(np.asarray(L0_attn_W_o)),
        "wgT": _lhst_blocks(np.asarray(L0_ffn_W_g)),
        "wuT": _lhst_blocks(np.asarray(L0_ffn_W_u)),
        "wdT": _lhst_blocks(np.asarray(L0_ffn_W_d)),
        "ident": np.eye(128, dtype=np.float32),
    }
    sin, cos = _rope_tables()
    sgn = np.where(np.arange(128) % 2 == 0, -1.0, 1.0).astype(np.float32)
    rkc = np.ascontiguousarray(cos.T)
    rks = np.ascontiguousarray(sin.T * sgn[:, None])
    in_maps = []
    for core in range(8):
        b, p = core // 2, core % 2
        pp = _perm_positions(p)
        xt = np.ascontiguousarray(x[b].T[:, pp])
        d = dict(shared)
        d["xT"] = xt
        d["xown"] = np.ascontiguousarray(xt[:, 0:NQ])
        d["rkc"] = np.ascontiguousarray(rkc[:, pp])
        d["rks"] = np.ascontiguousarray(rks[:, pp])
        d["emask"] = _emask_for(p)
        in_maps.append(d)
    return in_maps


def kernel(L0_attn_x, L0_attn_W_qkv, L0_attn_W_o, L0_ffn_W_g, L0_ffn_W_u, L0_ffn_W_d):
    in_maps = make_inmaps(L0_attn_x, L0_attn_W_qkv, L0_attn_W_o, L0_ffn_W_g, L0_ffn_W_u, L0_ffn_W_d)
    nc = build_nc(phase=6)
    res = run_bass_kernel_spmd(nc, in_maps, core_ids=list(range(8)))
    y = np.empty((4, T, C), dtype=np.float32)
    for core in range(8):
        b, p = core // 2, core % 2
        pos = _own_positions(p)
        y[b, pos, :] = np.asarray(res.results[core]["out"]).T
    return y


# version 17
# speedup vs baseline: 1.6370x; 1.0408x over previous
import contextlib
import numpy as np
import ml_dtypes
import concourse.bass as bass
import concourse.mybir as mybir
from concourse.bass_utils import run_bass_kernel_spmd

F32 = mybir.dt.float32
BF16 = mybir.dt.bfloat16
I32 = mybir.dt.int32
AF = mybir.ActivationFunctionType
ALU = mybir.AluOpType

C = 2048
T = 2048
NH = 16
NCH = 16
NQ = 1024
HC = 64
RMS_EPS = 1.1920929e-07
LN_EPS = 1e-5
FFN_EPS = 1e-8
SCALE = float(np.float32(1.0) / np.float32(np.sqrt(np.float32(128.0))))
SHUF = [i ^ 1 for i in range(32)]

OWN = {0: [0, 3, 4, 7, 8, 11, 12, 15], 1: [1, 2, 5, 6, 9, 10, 13, 14]}
PERM = {p: OWN[p] + sorted(set(range(16)) - set(OWN[p])) for p in (0, 1)}

def _profiles():
    na, nb = [0] * 16, [0] * 16
    for ks in range(16):
        for p in (0, 1):
            g = PERM[p][ks]
            ca = sum(1 for j in range(0, 4) if PERM[p][j] >= g)
            cb = sum(1 for j in range(4, 8) if PERM[p][j] >= g)
            na[ks] = max(na[ks], ca)
            nb[ks] = max(nb[ks], cb)
    return na, nb

NA_U, NB_U = _profiles()
PROFS = (NA_U, NB_U)


def _mask_layout():
    off, o = {}, 0
    for b_ in (0, 1):
        for ks in range(16):
            nU = PROFS[b_][ks]
            if nU == 0:
                continue
            worst = 0
            for p in (0, 1):
                g = PERM[p][ks]
                qs = [PERM[p][j] for j in range(b_ * 4, b_ * 4 + 4)]
                n_p = sum(1 for t in qs if t >= g)
                worst = max(worst, (nU - n_p) + (1 if g in qs else 0))
            if worst > 0:
                off[(ks, b_)] = (o, worst)
                o += worst
    return off, o


MASK_OFF, MASK_SLOTS = _mask_layout()
for _p in (0, 1):
    for _b in (0, 1):
        _qs = [PERM[_p][j] for j in range(_b * 4, _b * 4 + 4)]
        assert all(_qs[i] < _qs[i + 1] for i in range(3))

ENGINES = ("sync", "gpsimd", "vector", "scalar", "tensor")
SEMS = ("t", "v", "s", "din", "xb0", "xb1", "spl0", "spl1", "sx2", "sout", "gout", "g0", "g1", "gin", "gin0", "gin1", "gw", "sw0", "sw1")


class Prog:
    def __init__(self):
        self.ops = {e: [] for e in ENGINES}
        self.cnt = {n: 0 for n in SEMS}

    def emit(self, eng, fn):
        self.ops[eng].append(fn)

    def wait(self, eng, sem, val=None):
        v = self.cnt[sem] if val is None else val
        if v > 0:
            self.emit(eng, lambda E, S, _s=sem, _v=v: E.wait_ge(S[_s], _v))

    def waits(self, eng, rel):
        for sem, v in rel:
            self.wait(eng, sem, v)

    def do(self, eng, fn, inc=None, n=1):
        if eng == "vector" or eng == "scalar":
            sem = "v" if eng == "vector" else "s"
            prev = self.cnt[sem]
            self.cnt[sem] = prev + 1
            def wrapped(E, S, _f=fn, _s=sem, _p=prev):
                if _p > 0:
                    E.wait_ge(S[_s], _p)
                return _f(E, S).then_inc(S[_s], 1)
            self.emit(eng, wrapped)
            return
        if inc is not None:
            self.cnt[inc] += n
            self.emit(eng, lambda E, S, _f=fn, _i=inc, _n=n: _f(E, S).then_inc(S[_i], _n))
        else:
            self.emit(eng, lambda E, S, _f=fn: _f(E, S))

    def dma(self, eng, fn, inc, n=16):
        self.cnt[inc] += n
        self.emit(eng, lambda E, S, _f=fn, _i=inc, _n=n: _f(E, S).then_inc(S[_i], _n))

    def snap(self):
        return [(s, v) for s, v in self.cnt.items() if v > 0]

    def barrier(self, engs=ENGINES):
        snap = self.snap()
        for e in engs:
            self.waits(e, snap)


def build_nc(phase=6):
    nc = bass.Bass("TRN2", target_bir_lowering=False)
    dtn = nc.dram_tensor
    xT = dtn("xT", [C, T], F32, kind="ExternalInput").ap()
    xown = dtn("xown", [C, NQ], F32, kind="ExternalInput").ap()
    wqT = dtn("wqT", [NH, 128, C], F32, kind="ExternalInput").ap()
    wkT = dtn("wkT", [NH, 128, C], F32, kind="ExternalInput").ap()
    wvT = dtn("wvT", [NCH, 128, C], F32, kind="ExternalInput").ap()
    woT = dtn("woT", [NCH, 128, C], F32, kind="ExternalInput").ap()
    wgT = dtn("wgT", [HC, 128, C], F32, kind="ExternalInput").ap()
    wuT = dtn("wuT", [HC, 128, C], F32, kind="ExternalInput").ap()
    wdT = dtn("wdT", [NCH, 128, 8192], F32, kind="ExternalInput").ap()
    rkc = dtn("rkc", [128, T], F32, kind="ExternalInput").ap()
    rks = dtn("rks", [128, T], F32, kind="ExternalInput").ap()
    emaskD = dtn("emask", [128, max(MASK_SLOTS, 1) * 128], BF16, kind="ExternalInput").ap()
    identD = dtn("ident", [128, 128], F32, kind="ExternalInput").ap()
    out = dtn("out", [C, NQ], F32, kind="ExternalOutput").ap()
    aoscr = dtn("aoscr", [C, NQ], BF16, kind="Internal").ap()
    sgg = dtn("sgg", [HC, 128, C], BF16, kind="Internal").ap()
    sgu = dtn("sgu", [HC, 128, C], BF16, kind="Internal").ap()
    sgd = dtn("sgd", [NCH, 128, 8192], BF16, kind="Internal").ap()
    x2scr = dtn("x2scr", [C, NQ], F32, kind="Internal").ap()
    dbg = dtn("dbg", [C, T], F32, kind="ExternalOutput").ap() if phase < 6 else None

    P = Prog()
    ctx = contextlib.ExitStack()
    stacks = []
    with ctx:
        def sbt(st, nm, sh, d=F32):
            return st.enter_context(nc.sbuf_tensor(nm, sh, d))
        onescol = sbt(ctx, "onescol", [128, 1], BF16)
        onesrow = sbt(ctx, "onesrow", [1, 128], F32)
        ident = sbt(ctx, "ident_s", [128, 128], F32)
        rowA = sbt(ctx, "rowA", [1, NQ])
        rowB = sbt(ctx, "rowB", [1, NQ])
        invsbc = sbt(ctx, "invsbc", [128, NQ])
        epsT = sbt(ctx, "epsT", [128, 3])
        psA = ctx.enter_context(nc.psum_tensor("psA", [128, 512], F32))
        psB = ctx.enter_context(nc.psum_tensor("psB", [128, 512], F32))
        psC = ctx.enter_context(nc.psum_tensor("psC", [128, 512], F32))
        psD = ctx.enter_context(nc.psum_tensor("psD", [128, 512], F32))
        psE = ctx.enter_context(nc.psum_tensor("psE", [128, 512], F32))
        psF = ctx.enter_context(nc.psum_tensor("psF", [128, 512], F32))
        psS = ctx.enter_context(nc.psum_tensor("psS", [1, 1024], F32))
        PS4 = [psA, psB, psC, psD]
        sems = {n: ctx.enter_context(nc.semaphore("sem_" + n)) for n in SEMS}
        blk = ctx.enter_context(nc.Block())

        P.do("sync", lambda E, S: E.dma_start(out=ident[:], in_=identD), inc="din", n=16)
        P.do("vector", lambda E, S: E.memset(onescol[:], 1.0))
        P.do("vector", lambda E, S: E.memset(onesrow[:], 1.0))
        P.do("vector", lambda E, S: E.memset(epsT[:, 0:1], LN_EPS))
        P.do("vector", lambda E, S: E.memset(epsT[:, 1:2], RMS_EPS))
        P.do("vector", lambda E, S: E.memset(epsT[:, 2:3], FFN_EPS), inc="v")

        def ln_rows(ntok, epscol, s1ap, s2ap, rR, rB):
            for a_, c0, n in s1ap:
                P.do("vector", lambda E, S, a=a_, c=c0, n=n: E.tensor_scalar_mul(rB[0:1, c:c + n], a, 1.0 / C))
            for a_, c0, n in s2ap:
                P.do("vector", lambda E, S, a=a_, c=c0, n=n: E.tensor_scalar_mul(rR[0:1, c:c + n], a, 1.0 / C))
            for a_, c0, n in s1ap:
                P.do("vector", lambda E, S, a=a_, c=c0, n=n: E.tensor_mul(a, rB[0:1, c:c + n], rB[0:1, c:c + n]))
            for a_, c0, n in s1ap:
                P.do("vector", lambda E, S, a=a_, c=c0, n=n: E.tensor_sub(rR[0:1, c:c + n], rR[0:1, c:c + n], a), inc="v")
            P.wait("scalar", "v")
            P.do("scalar", lambda E, S: E.activation(rR[0:1, 0:ntok], rR[0:1, 0:ntok], AF.Sqrt, bias=epsT[0:1, epscol:epscol + 1]), inc="s")
            P.wait("vector", "s")
            P.do("vector", lambda E, S: E.reciprocal(rR[0:1, 0:ntok], rR[0:1, 0:ntok]))
            P.do("vector", lambda E, S: E.scalar_tensor_tensor(rB[0:1, 0:ntok], rB[0:1, 0:ntok], -1.0, rR[0:1, 0:ntok], ALU.mult, ALU.mult), inc="v")

        def bcast(rowap, ncols, pslist):
            nb = (ncols + 511) // 512
            for i in range(nb):
                n = min(512, ncols - i * 512)
                P.do("tensor", lambda E, S, p=pslist[i], c=i * 512, n=n, r=rowap:
                     E.matmul(p[:, 0:n], onesrow[:], r[0:1, c:c + n], start=True, stop=True),
                     inc="t" if i == nb - 1 else None)

        a1 = contextlib.ExitStack(); stacks.append(a1)
        LNx = sbt(a1, "LNx", [128, NCH * T], BF16)
        V = sbt(a1, "V", [128, NCH * T], BF16)
        a1b = contextlib.ExitStack(); stacks.append(a1b)
        xbuf = [sbt(a1b, f"xbuf{i}", [128, T]) for i in range(2)]
        xbf = [sbt(a1b, f"xbf{i}", [128, T], BF16) for i in range(1)]
        xsq = [sbt(a1b, f"xsq{i}", [128, T], BF16) for i in range(1)]
        RBs = sbt(a1b, "RBs", [128, 2 * T])
        rowAT = sbt(a1b, "rowAT", [1, T])
        rowBT = sbt(a1b, "rowBT", [1, T])

        s1rows = [(psA[0:1, :], 0, 512), (psB[0:1, :], 512, 512), (psC[0:1, :], 1024, 512), (psD[0:1, :], 1536, 512)]
        s2rows = [(psE[0:1, :], 0, 512), (psF[0:1, :], 512, 512), (psS[0:1, 0:512], 1024, 512), (psS[0:1, 512:1024], 1536, 512)]
        rel = [[], []]
        for j in range(NCH):
            bi = j % 2
            P.waits("sync", rel[bi])
            P.do("sync", lambda E, S, j=j, b=bi: E.dma_start(out=xbuf[b][:], in_=xT[j * 128:(j + 1) * 128, :]), inc=f"xb{bi}", n=16)
            P.wait("vector", f"xb{bi}")
            P.wait("vector", "t", P.cnt["t"])
            P.do("vector", lambda E, S, b=bi: E.tensor_copy(xbf[0][:], xbuf[b][:]), inc="v")
            P.wait("scalar", f"xb{bi}")
            P.wait("scalar", "t", P.cnt["t"])
            P.do("scalar", lambda E, S, b=bi: E.activation(xsq[0][:], xbuf[b][:], AF.Square), inc="s")
            P.wait("tensor", "v")
            P.wait("tensor", "s")
            for tb in range(4):
                P.do("tensor", lambda E, S, j=j, tb=tb, p=PS4[tb]:
                     E.matmul(p[0:1, :], onescol[:], xbf[0][:, tb * 512:(tb + 1) * 512], start=(j == 0), stop=(j == NCH - 1)))
            for tb in range(4):
                tgt = [psE[0:1, :], psF[0:1, :], psS[0:1, 0:512], psS[0:1, 512:1024]][tb]
                P.do("tensor", lambda E, S, j=j, tb=tb, tg=tgt:
                     E.matmul(tg, onescol[:], xsq[0][:, tb * 512:(tb + 1) * 512], start=(j == 0), stop=(j == NCH - 1)),
                     inc="t" if tb == 3 else None)
            rel[bi] = [("v", P.cnt["v"]), ("s", P.cnt["s"])]
        P.wait("vector", "t")
        ln_rows(T, 0, s1rows, s2rows, rowAT, rowBT)
        P.wait("tensor", "v")
        for which, row in ((0, rowAT), (1, rowBT)):
            bcast(row, T, PS4)
            P.wait("vector", "t")
            for tb in range(4):
                P.do("vector", lambda E, S, w=which, tb=tb, p=PS4[tb]:
                     E.tensor_copy(RBs[:, w * T + tb * 512:w * T + (tb + 1) * 512], p[:, :]), inc="v" if tb == 3 else None)
            P.wait("tensor", "v")
        rel = [P.snap(), P.snap()]
        for j in range(NCH):
            bi = j % 2
            P.waits("sync", rel[bi])
            P.do("sync", lambda E, S, j=j, b=bi: E.dma_start(out=xbuf[b][:], in_=xT[j * 128:(j + 1) * 128, :]), inc=f"xb{bi}", n=16)
            P.wait("vector", f"xb{bi}")
            P.do("vector", lambda E, S, b=bi: E.tensor_mul(xbuf[b][:], xbuf[b][:], RBs[:, 0:T]))
            P.do("vector", lambda E, S, j=j, b=bi: E.tensor_add(LNx[:, j * T:(j + 1) * T], xbuf[b][:], RBs[:, T:2 * T]), inc="v")
            rel[bi] = [("v", P.cnt["v"])]
        P.barrier()
        a1b.close()
        if phase == 1:
            for j in range(NCH):
                P.do("gpsimd", lambda E, S, j=j: E.dma_start(out=dbg[j * 128:(j + 1) * 128, :], in_=LNx[:, j * T:(j + 1) * T]), inc="gin", n=16)

        if phase >= 2:
            a2 = contextlib.ExitStack(); stacks.append(a2); stacks.append(a2)
            wvb = [sbt(a2, f"wvb{i}", [128, NCH * 512], BF16) for i in range(2)]
            relw = [[], []]
            relv4 = [[("v", P.cnt["v"])] for _ in range(4)]
            for fb in range(4):
                bi = fb % 2
                P.waits("gpsimd", relw[bi])
                for ci in range(NCH):
                    P.do("gpsimd", lambda E, S, fb=fb, ci=ci, b=bi:
                         E.dma_start(out=wvb[b][:, ci * 512:(ci + 1) * 512], in_=wvT[ci, :, fb * 512:(fb + 1) * 512]), inc=f"g{bi}", n=16)
                P.wait("tensor", f"g{bi}")
                for ti in range(NCH):
                    pst = PS4[ti % 4]
                    P.waits("tensor", relv4[ti % 4])
                    for ci in range(NCH):
                        P.do("tensor", lambda E, S, ti=ti, ci=ci, b=bi, p=pst:
                             E.matmul(p[:, :], LNx[:, ci * T + ti * 128:ci * T + (ti + 1) * 128], wvb[b][:, ci * 512:(ci + 1) * 512],
                                      start=(ci == 0), stop=(ci == NCH - 1)), inc="t" if ci == NCH - 1 else None)
                    P.wait("vector", "t")
                    P.do("vector", lambda E, S, ti=ti, fb=fb, p=pst:
                         E.tensor_copy(V[:, ti * T + fb * 512:ti * T + (fb + 1) * 512], p[:, :]), inc="v")
                    relv4[ti % 4] = [("v", P.cnt["v"])]
                relw[bi] = [("t", P.cnt["t"])]
            a2.close()
            P.barrier()
            if phase == 2:
                for ti in range(NCH):
                    P.do("gpsimd", lambda E, S, ti=ti: E.dma_start(out=dbg[ti * 128:(ti + 1) * 128, :], in_=V[:, ti * T:(ti + 1) * T]), inc="gin", n=16)

        if phase >= 3:
            a3 = contextlib.ExitStack(); stacks.append(a3)
            rkcS = sbt(a3, "rkcS", [128, T], BF16)
            rksS = sbt(a3, "rksS", [128, T], BF16)
            emask = sbt(a3, "emask_s", [128, max(MASK_SLOTS, 1) * 128], BF16)
            wkb = [sbt(a3, f"wkb{i}", [128, C], BF16) for i in range(2)]
            wqb = [sbt(a3, f"wqb{i}", [128, C], BF16) for i in range(2)]
            krot3 = [sbt(a3, f"krot{i}", [128, T], BF16) for i in range(3)]
            qhat3 = [sbt(a3, f"qhat{i}", [128, NQ], BF16) for i in range(2)]
            ksc3 = [sbt(a3, f"ksc{i}", [128, 16]) for i in range(3)]
            qrot = sbt(a3, "qrot", [128, NQ], BF16)
            sqt = sbt(a3, "sqt", [128, T], BF16)
            tmpf = sbt(a3, "tmpf", [128, 1024], BF16)
            rSb = sbt(a3, "rSb", [128, 512])
            tshuf = rSb
            ebuf = [sbt(a3, f"ebuf{i}", [128, 512], BF16) for i in range(3)]
            identB = sbt(a3, "identB", [128, 128], BF16)
            aout1 = [sbt(a3, f"aout1_{i}", [128, NQ], BF16) for i in range(2)]
            for src, dst in ((rkc, rkcS), (rks, rksS)):
                P.do("gpsimd", lambda E, S, s_=src, d_=dst: E.dma_start(out=d_[:], in_=s_), inc="din", n=16)
            if MASK_SLOTS > 0:
                P.do("sync", lambda E, S: E.dma_start(out=emask[:], in_=emaskD), inc="din", n=16)
            P.do("gpsimd", lambda E, S: E.dma_start(out=identB[:], in_=identD), inc="din", n=16)
            P.wait("vector", "din")
            P.wait("scalar", "din")
            rel_ao = [[], []]
            rel_wkb = [[], []]
            rel_wqb = [[], []]
            ksum_done = [0]
            qstats_done = [0]
            LAST_KT = {0: max(kt for kt in range(16) if NA_U[kt] > 0),
                       1: max(kt for kt in range(16) if NB_U[kt] > 0)}
            ITERS = [(kt, b_, PROFS[b_][kt]) for kt in range(16) for b_ in (0, 1) if PROFS[b_][kt] > 0]

            def prepK(h):
                par, bi = h % 3, h % 2
                krot, ksc = krot3[par], ksc3[par]
                steps = []
                kld = [0]

                def s_loadk():
                    P.waits("gpsimd", rel_wkb[bi])
                    P.do("gpsimd", lambda E, S, h=h, b=bi: E.dma_start(out=wkb[b][:], in_=wkT[h]), inc=f"g{bi}", n=16)
                    kld[0] = P.cnt[f"g{bi}"]
                steps.append(s_loadk)

                def mk_kprod(tb):
                    def f():
                        P.wait("tensor", f"g{bi}", kld[0])
                        P.wait("tensor", "v", P.cnt["v"])
                        for ci in range(NCH):
                            P.do("tensor", lambda E, S, ci=ci, tb=tb, b=bi:
                                 E.matmul(psD[:, :], wkb[b][:, ci * 128:(ci + 1) * 128],
                                          LNx[:, ci * T + tb * 512:ci * T + (tb + 1) * 512],
                                          start=(ci == 0), stop=(ci == NCH - 1)),
                                 inc="t" if ci == NCH - 1 else None)
                        if tb == 3:
                            rel_wkb[bi] = [("t", P.cnt["t"])]
                    return f

                def mk_krope(tb):
                    def f():
                        P.wait("vector", "t")
                        sl = slice(tb * 512, (tb + 1) * 512)
                        P.do("vector", lambda E, S, sl=sl: E.tensor_mul(tmpf[:, 0:512], psD[:, :], rkcS[:, sl]))
                        P.do("vector", lambda E, S: E.stream_shuffle(tshuf[:, 0:512], psD[:, :], SHUF))
                        P.do("vector", lambda E, S, sl=sl: E.tensor_mul(tmpf[:, 512:1024], tshuf[:, 0:512], rksS[:, sl]))
                        P.do("vector", lambda E, S, sl=sl, kr=krot: E.tensor_add(kr[:, sl], tmpf[:, 0:512], tmpf[:, 512:1024]), inc="v")
                    return f

                for tb in range(4):
                    steps.append(mk_kprod(tb))
                    steps.append(mk_krope(tb))

                def s_ksq():
                    P.wait("scalar", "t", ksum_done[0])
                    P.wait("scalar", "t", qstats_done[0])
                    P.wait("scalar", "v")
                    P.do("scalar", lambda E, S, kr=krot: E.activation(sqt[:, 0:T], kr[:, 0:T], AF.Square), inc="s")
                steps.append(s_ksq)

                def mk_ksum(half):
                    def f():
                        P.wait("tensor", "s")
                        P.wait("tensor", "v", P.cnt["v"])
                        for r in range(8):
                            kt = half * 8 + r
                            P.do("tensor", lambda E, S, kt=kt:
                                 E.matmul(psD[:, kt - half * 8:kt - half * 8 + 1], sqt[:, kt * 128:(kt + 1) * 128], onescol[:], start=True, stop=True),
                                 inc="t" if r == 7 else None)
                        if half == 1:
                            ksum_done[0] = P.cnt["t"]
                        P.wait("vector", "t")
                        P.do("vector", lambda E, S, half=half, k=ksc: E.tensor_copy(k[:, half * 8:half * 8 + 8], psD[:, 0:8]), inc="v")
                    return f

                for half in (0, 1):
                    steps.append(mk_ksum(half))

                def s_kscrow():
                    P.wait("scalar", "v")
                    P.do("scalar", lambda E, S, k=ksc: E.activation(k[:, 0:16], k[:, 0:16], AF.Sqrt, bias=epsT[:, 1:2], scale=1.0 / 128.0), inc="s")
                    P.wait("vector", "s")
                    P.do("vector", lambda E, S, k=ksc: E.reciprocal(k[:, 0:16], k[:, 0:16]))
                    P.do("vector", lambda E, S, k=ksc: E.tensor_scalar_mul(k[:, 0:16], k[:, 0:16], SCALE), inc="v")
                steps.append(s_kscrow)
                return steps

            def prepQ(h):
                bi = h % 2
                qhat = qhat3[bi]
                steps = []
                qld = [0]

                def s_loadq():
                    P.waits("gpsimd", rel_wqb[bi])
                    P.do("gpsimd", lambda E, S, h=h, b=bi: E.dma_start(out=wqb[b][:], in_=wqT[h]), inc=f"gin{bi}", n=16)
                    qld[0] = P.cnt[f"gin{bi}"]
                steps.append(s_loadq)

                def mk_qprod(tb):
                    def f():
                        P.wait("tensor", f"gin{bi}", qld[0])
                        P.wait("tensor", "v", P.cnt["v"])
                        for ci in range(NCH):
                            P.do("tensor", lambda E, S, ci=ci, tb=tb, b=bi:
                                 E.matmul(psD[:, :], wqb[b][:, ci * 128:(ci + 1) * 128],
                                          LNx[:, ci * T + tb * 512:ci * T + (tb + 1) * 512],
                                          start=(ci == 0), stop=(ci == NCH - 1)),
                                 inc="t" if ci == NCH - 1 else None)
                        if tb == 1:
                            rel_wqb[bi] = [("t", P.cnt["t"])]
                    return f

                def mk_qrope(tb):
                    def f():
                        P.wait("vector", "t")
                        sl = slice(tb * 512, (tb + 1) * 512)
                        P.do("vector", lambda E, S, sl=sl: E.tensor_mul(tmpf[:, 0:512], psD[:, :], rkcS[:, sl]))
                        P.do("vector", lambda E, S: E.stream_shuffle(tshuf[:, 0:512], psD[:, :], SHUF))
                        P.do("vector", lambda E, S, sl=sl: E.tensor_mul(tmpf[:, 512:1024], tshuf[:, 0:512], rksS[:, sl]))
                        P.do("vector", lambda E, S, sl=sl: E.tensor_add(qrot[:, sl], tmpf[:, 0:512], tmpf[:, 512:1024]), inc="v")
                        if tb == 1:
                            P.wait("scalar", "t", ksum_done[0])
                            P.wait("scalar", "v")
                            P.do("scalar", lambda E, S: E.activation(sqt[:, 0:NQ], qrot[:, 0:NQ], AF.Square), inc="s")
                    return f

                for tb in (0, 1):
                    steps.append(mk_qprod(tb))
                    steps.append(mk_qrope(tb))

                def mk_qstats(tb):
                    def f():
                        if tb == 0:
                            P.wait("tensor", "s")
                        P.wait("tensor", "v", P.cnt["v"])
                        P.do("tensor", lambda E, S, tb=tb:
                             E.matmul(psD[0:1, :], onescol[:], sqt[:, tb * 512:(tb + 1) * 512], start=True, stop=True),
                             inc="t")
                        if tb == 1:
                            qstats_done[0] = P.cnt["t"]
                        P.wait("vector", "t")
                        P.do("vector", lambda E, S, tb=tb: E.tensor_copy(rowB[0:1, tb * 512:(tb + 1) * 512], psD[0:1, :]), inc="v")
                        if tb == 1:
                            P.wait("scalar", "v")
                            P.do("scalar", lambda E, S: E.activation(rowB[0:1, 0:NQ], rowB[0:1, 0:NQ], AF.Sqrt, bias=epsT[0:1, 1:2], scale=1.0 / 128.0), inc="s")
                            P.wait("vector", "s")
                            P.do("vector", lambda E, S: E.reciprocal(rowB[0:1, 0:NQ], rowB[0:1, 0:NQ]), inc="v")
                    return f

                steps.append(mk_qstats(0))
                steps.append(mk_qstats(1))

                def mk_qhat(tb):
                    def f():
                        P.wait("tensor", "v", P.cnt["v"])
                        P.do("tensor", lambda E, S, tb=tb: E.matmul(psD[:, :], onesrow[:], rowB[0:1, tb * 512:(tb + 1) * 512], start=True, stop=True), inc="t")
                        P.wait("vector", "t")
                        P.do("vector", lambda E, S, tb=tb, q=qhat:
                             E.tensor_mul(q[:, tb * 512:(tb + 1) * 512], qrot[:, tb * 512:(tb + 1) * 512], psD[:, :]), inc="v")
                    return f

                steps.append(mk_qhat(0))
                steps.append(mk_qhat(1))
                return steps

            def make_epi(h):
                ao = aout1[h % 2]
                epi_rel = {}

                def mk_eb(tb):
                    def f():
                        P.wait("tensor", "v", P.cnt["v"])
                        P.do("tensor", lambda E, S, tb=tb:
                             E.matmul(psD[:, :], onesrow[:], rowA[0:1, tb * 512:(tb + 1) * 512], start=True, stop=True), inc="t")
                    return f

                def mk_em(tb):
                    def f():
                        P.wait("vector", "t")
                        if tb == 0:
                            P.waits("vector", rel_ao[h % 2])
                        P.do("vector", lambda E, S: E.reciprocal(rSb[:, 0:512], psD[:, :]))
                        P.do("vector", lambda E, S, tb=tb, pv=[psE, psF][tb], a=ao:
                             E.tensor_mul(a[:, tb * 512:(tb + 1) * 512], pv[:, :], rSb[:, 0:512]), inc="v")
                        if tb == 1:
                            epi_rel["r"] = [("v", P.cnt["v"])]
                            P.wait("sync", "v")
                            P.do("sync", lambda E, S, h=h, a=ao: E.dma_start(out=aoscr[h * 128:(h + 1) * 128, :], in_=a[:]), inc=f"spl{h % 2}", n=16)
                            rel_ao[h % 2] = [(f"spl{h % 2}", P.cnt[f"spl{h % 2}"])]
                    return f
                return [mk_eb(0), mk_em(0), mk_eb(1), mk_em(1)], epi_rel

            for st_ in prepK(0):
                st_()
            for st_ in prepQ(0):
                st_()
            for st_ in prepK(1):
                st_()
            prep_rel = [("t", P.cnt["t"]), ("v", P.cnt["v"]), ("s", P.cnt["s"])]
            pend_epi = None

            for h in range(NH):
                krot, ksc, qhat = krot3[h % 3], ksc3[h % 3], qhat3[h % 2]
                steps = list(pend_epi[0]) if pend_epi else []
                epi_rel_prev = pend_epi[1] if pend_epi else None
                steps += prepQ(h + 1) if h + 1 < NH else []
                steps += prepK(h + 2) if h + 2 < NH else []
                si = 0
                P.waits("tensor", prep_rel)
                P.waits("scalar", prep_rel)
                P.wait("tensor", "s", P.cnt["s"])
                exp_done = []
                ready = []

                def emit_scores(i):
                    kt, b_, nU = ITERS[i]
                    qc0 = b_ * 512 + (4 - nU) * 128
                    ncols = nU * 128
                    psc = [psA, psB, psC][i % 3]
                    eb = ebuf[i % 3]
                    if i >= 3:
                        P.wait("tensor", "s", exp_done[i - 3])
                    P.do("tensor", lambda E, S, kt=kt, qc0=qc0, n=ncols, p=psc, kr=krot, q=qhat:
                         E.matmul(p[:, 0:n], kr[:, kt * 128:(kt + 1) * 128], q[:, qc0:qc0 + n], start=True, stop=True), inc="t")
                    mo = MASK_OFF.get((kt, b_))
                    if mo is not None:
                        o0, L = mo
                        P.do("tensor", lambda E, S, p=psc, o0=o0, L=L:
                             E.matmul(p[:, 0:L * 128], identB[:], emask[:, o0 * 128:(o0 + L) * 128], start=False, stop=True), inc="t")
                    P.wait("scalar", "t")
                    P.do("scalar", lambda E, S, kt=kt, n=ncols, p=psc, e=eb, k=ksc:
                         E.activation(e[:, 0:n], p[:, 0:n], AF.Exp, scale=k[:, kt:kt + 1]), inc="s")
                    exp_done.append(P.cnt["s"])
                    ready.append([("s", P.cnt["s"])])

                def emit_sumpv(i):
                    kt, b_, nU = ITERS[i]
                    qc0 = b_ * 512 + (4 - nU) * 128
                    ncols = nU * 128
                    eb = ebuf[i % 3]
                    first, last = (kt == 0), (kt == LAST_KT[b_])
                    if first and epi_rel_prev is not None:
                        P.waits("tensor", epi_rel_prev.get("r", []))
                    P.waits("tensor", ready[i])
                    P.do("tensor", lambda E, S, qc0=qc0, n=ncols, e=eb, f=first, l=last:
                         E.matmul(psS[0:1, qc0:qc0 + n], onescol[:], e[:, 0:n], start=f, stop=l))
                    P.do("tensor", lambda E, S, kt=kt, b_=b_, qc0=qc0, n=ncols, e=eb, f=first, l=last, h=h:
                         E.matmul([psE, psF][b_][:, qc0 - b_ * 512:qc0 - b_ * 512 + n],
                                  V[:, kt * T + h * 128:kt * T + (h + 1) * 128], e[:, 0:n], start=f, stop=l), inc="t")

                epi_n = len(pend_epi[0]) if pend_epi else 0
                for i in range(len(ITERS) + 1):
                    if i < len(ITERS):
                        emit_scores(i)
                    if i == 0:
                        while si < epi_n:
                            steps[si]()
                            si += 1
                    if i >= 1:
                        emit_sumpv(i - 1)
                    want = ((i + 1) * len(steps)) // (len(ITERS) + 1)
                    while si < want:
                        steps[si]()
                        si += 1
                while si < len(steps):
                    steps[si]()
                    si += 1
                prep_rel = [("t", P.cnt["t"]), ("v", P.cnt["v"]), ("s", P.cnt["s"])]
                P.wait("vector", "t")
                P.do("vector", lambda E, S: E.tensor_copy(rowA[0:1, 0:NQ], psS[0:1, 0:1024]), inc="v")
                pend_epi = make_epi(h)
            for st_ in pend_epi[0]:
                st_()
            a3.close()
            P.barrier()

        if phase >= 4:
            a1.close()
            a4o = contextlib.ExitStack(); stacks.append(a4o); stacks.append(a4o)
            xqS = sbt(a4o, "xqS", [128, NCH * NQ], BF16)
            wgb = [sbt(a4o, f"wgb{i}", [128, C], BF16) for i in range(2)]
            wub = [sbt(a4o, f"wub{i}", [128, C], BF16) for i in range(2)]
            a4m = contextlib.ExitStack(); stacks.append(a4m); stacks.append(a4m)
            x2S = sbt(a4m, "x2S", [128, NCH * NQ])
            a4 = contextlib.ExitStack(); stacks.append(a4); stacks.append(a4)
            aoutS = sbt(a4, "aoutS", [128, NCH * NQ], BF16)
            wob = [sbt(a4, f"wob{i}", [128, C], BF16) for i in range(2)]
            abf = [sbt(a4, f"abf{i}", [128, NQ], BF16) for i in range(2)]
            asq = [sbt(a4, f"asq{i}", [128, NQ], BF16) for i in range(2)]
            xoc = [sbt(a4, f"xoc{i}", [128, NQ]) for i in range(2)]
            for j in range(NCH):
                P.do("gpsimd", lambda E, S, j=j: E.dma_start(out=aoutS[:, j * NQ:(j + 1) * NQ], in_=aoscr[j * 128:(j + 1) * 128, :]), inc="gin", n=16)
            relw = [[], []]
            relpp = [[("v", P.cnt["v"])], [("v", P.cnt["v"])]]
            pend_wstats = None
            for fc in range(NCH):
                bi = fc % 2
                pp = [psA, psB] if bi == 0 else [psC, psD]
                P.waits("gpsimd", relw[bi])
                P.do("gpsimd", lambda E, S, fc=fc, b=bi: E.dma_start(out=wob[b][:], in_=woT[fc]), inc=f"g{bi}", n=16)
                P.wait("tensor", f"g{bi}")
                P.wait("tensor", "gin")
                P.waits("tensor", relpp[bi])
                for ci in range(NCH):
                    for tb in range(2):
                        P.do("tensor", lambda E, S, ci=ci, tb=tb, b=bi, p=pp[tb]:
                             E.matmul(p[:, :], wob[b][:, ci * 128:(ci + 1) * 128], aoutS[:, ci * NQ + tb * 512:ci * NQ + (tb + 1) * 512],
                                      start=(ci == 0), stop=(ci == NCH - 1)), inc="t" if (ci == NCH - 1 and tb == 1) else None)
                relw[bi] = [("t", P.cnt["t"])]
                if pend_wstats is not None:
                    pend_wstats()
                P.wait("vector", "t")
                for tb in range(2):
                    P.do("vector", lambda E, S, fc=fc, tb=tb, p=pp[tb]:
                         E.tensor_copy(x2S[:, fc * NQ + tb * 512:fc * NQ + (tb + 1) * 512], p[:, :]), inc="v" if tb == 1 else None)
                relpp[bi] = [("v", P.cnt["v"])]
                P.do("vector", lambda E, S, fc=fc, ab=abf[bi]: E.tensor_copy(ab[:], x2S[:, fc * NQ:(fc + 1) * NQ]), inc="v")
                P.wait("scalar", "v")
                P.do("scalar", lambda E, S, ab=abf[bi], aq=asq[bi]: E.activation(aq[:], ab[:], AF.Square), inc="s")

                def mk_wstats(fc=fc, scnt=P.cnt["s"], ab=abf[bi], aq=asq[bi]):
                    def f():
                        P.wait("tensor", "s", scnt)
                        for tb in range(2):
                            P.do("tensor", lambda E, S, tb=tb, tg=[psS[0:1, 0:512], psS[0:1, 512:1024]][tb]:
                                 E.matmul(tg, onescol[:], ab[:, tb * 512:(tb + 1) * 512], start=(fc == 0), stop=(fc == NCH - 1)))
                        for tb in range(2):
                            P.do("tensor", lambda E, S, tb=tb, tg=[psE[0:1, :], psF[0:1, :]][tb]:
                                 E.matmul(tg, onescol[:], aq[:, tb * 512:(tb + 1) * 512], start=(fc == 0), stop=(fc == NCH - 1)),
                                 inc="t" if tb == 1 else None)
                    return f
                pend_wstats = mk_wstats()
            pend_wstats()
            P.wait("vector", "t")
            ln_rows(NQ, 0, [(psS[0:1, 0:512], 0, 512), (psS[0:1, 512:1024], 512, 512)],
                    [(psE[0:1, :], 0, 512), (psF[0:1, :], 512, 512)], rowA, rowB)
            P.wait("tensor", "v")
            bcast(rowA, NQ, [psA, psB])
            bcast(rowB, NQ, [psC, psD])
            P.wait("vector", "t")
            relx = [[], []]
            stA = [0, 0]
            for j in range(NCH):
                bi = j % 2
                P.waits("sync", relx[bi])
                P.do("sync", lambda E, S, j=j, b=bi: E.dma_start(out=xoc[b][:], in_=xown[j * 128:(j + 1) * 128, :]), inc=f"xb{bi}", n=16)
                P.wait("vector", f"xb{bi}")
                for tb in range(2):
                    sl = slice(j * NQ + tb * 512, j * NQ + (tb + 1) * 512)
                    P.do("vector", lambda E, S, sl=sl, p=[psA, psB][tb]: E.tensor_mul(x2S[:, sl], x2S[:, sl], p[:, :]))
                    P.do("vector", lambda E, S, sl=sl, p=[psC, psD][tb]: E.tensor_add(x2S[:, sl], x2S[:, sl], p[:, :]))
                    P.do("vector", lambda E, S, sl=sl, b=bi, tb=tb:
                         E.tensor_add(x2S[:, sl], x2S[:, sl], xoc[b][:, tb * 512:(tb + 1) * 512]), inc="v")
                relx[bi] = [("v", P.cnt["v"])]
                P.wait("sync", "v")
                P.do("sync", lambda E, S, j=j: E.dma_start(out=x2scr[j * 128:(j + 1) * 128, :], in_=x2S[:, j * NQ:(j + 1) * NQ]), inc="sx2", n=16)
                P.wait("vector", "t", stA[bi])
                P.do("vector", lambda E, S, j=j, ab=abf[bi]: E.tensor_copy(ab[:], x2S[:, j * NQ:(j + 1) * NQ]), inc="v")
                P.wait("scalar", "v")
                P.do("scalar", lambda E, S, ab=abf[bi], aq=asq[bi]: E.activation(aq[:], ab[:], AF.Square), inc="s")
                P.wait("tensor", "s")
                for tb in range(2):
                    P.do("tensor", lambda E, S, j=j, tb=tb, ab=abf[bi], tg=[psS[0:1, 0:512], psS[0:1, 512:1024]][tb]:
                         E.matmul(tg, onescol[:], ab[:, tb * 512:(tb + 1) * 512], start=(j == 0), stop=(j == NCH - 1)))
                for tb in range(2):
                    P.do("tensor", lambda E, S, j=j, tb=tb, aq=asq[bi], tg=[psE[0:1, :], psF[0:1, :]][tb]:
                         E.matmul(tg, onescol[:], aq[:, tb * 512:(tb + 1) * 512], start=(j == 0), stop=(j == NCH - 1)),
                         inc="t" if tb == 1 else None)
                stA[bi] = P.cnt["t"]
            a4.close()
            P.barrier()
            if phase == 4:
                for j in range(NCH):
                    P.do("gpsimd", lambda E, S, j=j: E.dma_start(out=dbg[j * 128:(j + 1) * 128, 0:NQ], in_=x2S[:, j * NQ:(j + 1) * NQ]), inc="gin", n=16)

        if phase >= 5:
            a5 = contextlib.ExitStack(); stacks.append(a5); stacks.append(a5)
            RB2 = sbt(a5, "RB2", [128, 2 * NQ])
            Abc = sbt(a5, "Abc", [128, NQ])
            yabs = sbt(a5, "yabs", [128, NQ])
            macc = sbt(a5, "macc", [128, NQ])
            tq = sbt(a5, "tq", [128, NQ])
            cf = sbt(a5, "cf", [128, NQ])
            b1t = sbt(a5, "b1t", [128, NQ])
            xi = sbt(a5, "xi", [128, NQ], I32)
            tcol = sbt(a5, "tcol", [128, 8])
            r3r = sbt(a5, "r3r", [1, NQ])

            gu_relw = [[], []]
            gu_sg = [(0, 0, 0)] * HC
            gu_spill = [0] * HC

            def gu_load_sign(hc, from_scratch):
                bi = hc % 2
                P.waits("gpsimd", gu_relw[bi])
                if not from_scratch:
                    P.do("gpsimd", lambda E, S, hc=hc, b=bi: E.dma_start(out=wgb[b][:], in_=wgT[hc]), inc=f"g{bi}", n=16)
                    P.do("gpsimd", lambda E, S, hc=hc, b=bi: E.dma_start(out=wub[b][:], in_=wuT[hc]), inc=f"g{bi}", n=16)
                    P.wait("scalar", f"g{bi}")
                    P.do("scalar", lambda E, S, b=bi: E.sign(wgb[b][:], wgb[b][:]))
                    P.do("scalar", lambda E, S, b=bi: E.sign(wub[b][:], wub[b][:]), inc="s")
                    P.wait("sync", "s")
                    P.do("sync", lambda E, S, hc=hc, b=bi: E.dma_start(out=sgg[hc], in_=wgb[b][:]), inc=f"sw{bi}", n=16)
                    P.do("sync", lambda E, S, hc=hc, b=bi: E.dma_start(out=sgu[hc], in_=wub[b][:]), inc=f"sw{bi}", n=16)
                    gu_spill[hc] = P.cnt[f"sw{bi}"]
                else:
                    P.wait("gpsimd", f"sw{bi}", gu_spill[hc])
                    P.do("gpsimd", lambda E, S, hc=hc, b=bi: E.dma_start(out=wgb[b][:], in_=sgg[hc]), inc=f"g{bi}", n=16)
                    P.do("gpsimd", lambda E, S, hc=hc, b=bi: E.dma_start(out=wub[b][:], in_=sgu[hc]), inc=f"g{bi}", n=16)
                gu_sg[hc] = (P.cnt["s"], P.cnt[f"g{bi}"], P.cnt[f"sw{bi}"])

            gu_load_sign(0, False)
            gu_load_sign(1, False)

            P.wait("vector", "t")
            ln_rows(NQ, 0, [(psS[0:1, 0:512], 0, 512), (psS[0:1, 512:1024], 512, 512)],
                    [(psE[0:1, :], 0, 512), (psF[0:1, :], 512, 512)], rowA, rowB)
            P.do("vector", lambda E, S: E.tensor_mul(r3r[0:1, :], rowA[0:1, 0:NQ], rowA[0:1, 0:NQ]))
            P.do("vector", lambda E, S: E.tensor_scalar(r3r[0:1, :], r3r[0:1, :], -LN_EPS, 1.0 + FFN_EPS, ALU.mult, ALU.add), inc="v")
            P.wait("scalar", "v")
            P.do("scalar", lambda E, S: E.activation(r3r[0:1, :], r3r[0:1, :], AF.Sqrt), inc="s")
            P.wait("vector", "s")
            P.do("vector", lambda E, S: E.reciprocal(r3r[0:1, :], r3r[0:1, :]), inc="v")
            P.wait("tensor", "v")
            bcast(rowA, NQ, [psA, psB])
            bcast(rowB, NQ, [psC, psD])
            P.wait("vector", "t")
            for tb in range(2):
                P.do("vector", lambda E, S, tb=tb, p=[psA, psB][tb]: E.tensor_copy(RB2[:, tb * 512:(tb + 1) * 512], p[:, :]))
                P.do("vector", lambda E, S, tb=tb, p=[psC, psD][tb]: E.tensor_copy(RB2[:, NQ + tb * 512:NQ + (tb + 1) * 512], p[:, :]), inc="v" if tb == 1 else None)

            P.wait("vector", "sx2")
            for j in range(NCH):
                P.do("vector", lambda E, S, j=j: E.tensor_mul(x2S[:, j * NQ:(j + 1) * NQ], x2S[:, j * NQ:(j + 1) * NQ], RB2[:, 0:NQ]))
                P.do("vector", lambda E, S, j=j: E.tensor_add(x2S[:, j * NQ:(j + 1) * NQ], x2S[:, j * NQ:(j + 1) * NQ], RB2[:, NQ:2 * NQ]), inc="v")
                P.wait("scalar", "v")
                P.do("scalar", lambda E, S, j=j: E.activation(yabs[:], x2S[:, j * NQ:(j + 1) * NQ], AF.Abs), inc="s")
                P.wait("vector", "s")
                if j == 0:
                    P.do("vector", lambda E, S: E.tensor_copy(macc[:], yabs[:]), inc="v")
                else:
                    P.do("vector", lambda E, S: E.tensor_max(macc[:], macc[:], yabs[:]), inc="v")
            P.wait("tensor", "v")
            for jb in range(8):
                p = [psA, psB][jb % 2]
                P.wait("tensor", "v", P.cnt["v"])
                P.do("tensor", lambda E, S, jb=jb, p=p:
                     E.transpose(p[:, 0:128], macc[:, jb * 128:(jb + 1) * 128], ident[:]), inc="t")
                P.wait("vector", "t")
                P.do("vector", lambda E, S, jb=jb, p=p:
                     E.reduce_max(tcol[:, jb:jb + 1], p[:, 0:128], axis=mybir.AxisListType.X), inc="v")
            P.wait("tensor", "v")
            for jb in range(8):
                p = [psC, psD][jb % 2]
                P.wait("tensor", "v", P.cnt["v"])
                P.do("tensor", lambda E, S, jb=jb, p=p:
                     E.matmul(p[0:1, 0:128], tcol[:, jb:jb + 1], ident[:], start=True, stop=True), inc="t")
                P.wait("vector", "t")
                P.do("vector", lambda E, S, jb=jb, p=p:
                     E.tensor_copy(rowB[0:1, jb * 128:(jb + 1) * 128], p[0:1, 0:128]), inc="v")
            P.do("vector", lambda E, S: E.tensor_mul(rowA[0:1, 0:NQ], r3r[0:1, :], rowB[0:1, 0:NQ]))
            P.do("vector", lambda E, S: E.tensor_scalar_mul(rowA[0:1, 0:NQ], rowA[0:1, 0:NQ], 1.0 / 127.0))
            P.do("vector", lambda E, S: E.reciprocal(rowB[0:1, 0:NQ], rowB[0:1, 0:NQ]))
            P.do("vector", lambda E, S: E.tensor_scalar_mul(rowB[0:1, 0:NQ], rowB[0:1, 0:NQ], 127.0), inc="v")
            P.wait("tensor", "v")
            bcast(rowB, NQ, [psA, psB])
            bcast(rowA, NQ, [psC, psD])
            P.wait("vector", "t")
            for tb in range(2):
                P.do("vector", lambda E, S, tb=tb, p=[psA, psB][tb]: E.tensor_copy(Abc[:, tb * 512:(tb + 1) * 512], p[:, :]))
                P.do("vector", lambda E, S, tb=tb, p=[psC, psD][tb]: E.tensor_copy(invsbc[:, tb * 512:(tb + 1) * 512], p[:, :]), inc="v" if tb == 1 else None)
            for j in range(NCH):
                P.do("vector", lambda E, S, j=j: E.tensor_mul(tq[:], x2S[:, j * NQ:(j + 1) * NQ], Abc[:]), inc="v")
                P.wait("scalar", "v")
                P.do("scalar", lambda E, S: E.activation(b1t[:], tq[:], AF.Sign), inc="s")
                P.wait("vector", "s")
                P.do("vector", lambda E, S: E.scalar_tensor_tensor(cf[:], b1t[:], 0.5, tq[:], ALU.mult, ALU.add), inc="v")
                P.wait("scalar", "v")
                P.do("scalar", lambda E, S: E.copy(xi[:], cf[:]))
                P.do("scalar", lambda E, S: E.copy(cf[:], xi[:]), inc="s")
                P.wait("vector", "s")
                P.do("vector", lambda E, S, j=j: E.tensor_scalar(xqS[:, j * NQ:(j + 1) * NQ], cf[:], -128.0, 127.0, ALU.max, ALU.min), inc="v")
            a5.close()
            a4m.close()
            P.barrier()
            if phase == 5:
                for j in range(NCH):
                    P.do("gpsimd", lambda E, S, j=j: E.dma_start(out=dbg[j * 128:(j + 1) * 128, 0:NQ], in_=xqS[:, j * NQ:(j + 1) * NQ]), inc="gin", n=16)
                P.do("gpsimd", lambda E, S: E.dma_start(out=dbg[0:128, NQ:2 * NQ], in_=invsbc[:]), inc="gin", n=16)

        if phase >= 6:
            a6 = contextlib.ExitStack(); stacks.append(a6); stacks.append(a6)
            pS = sbt(a6, "pS", [128, HC * 512], BF16)
            gt0 = sbt(a6, "gt0", [128, 512])
            gt1 = sbt(a6, "gt1", [128, 512])
            ffS = sbt(a6, "ffS", [128, NCH * 512])
            fbf = [sbt(a6, f"fbf{i}", [128, 512], BF16) for i in range(2)]
            fsq = [sbt(a6, f"fsq{i}", [128, 512], BF16) for i in range(2)]
            wdb = sbt(a6, "wdb", [128, 8192], BF16)
            x2c = [sbt(a6, f"x2c{i}", [128, 512]) for i in range(2)]
            ob = [sbt(a6, f"ob{i}", [128, 512]) for i in range(2)]
            RB4 = sbt(a6, "RB4", [128, 1024])
            wd_relw = [[], []]
            wd_spill = [0] * (NCH * 2)
            ob_rel = [[], []]
            relgu_carry = None
            for tb in range(2):
                tsl = slice(tb * 512, (tb + 1) * 512)
                if tb == 0:
                    relgu = [[("v", P.cnt["v"])], [("v", P.cnt["v"])]]
                else:
                    relgu = relgu_carry
                    gu_load_sign(0, True)
                    gu_load_sign(1, True)
                for hc in range(HC):
                    bi = hc % 2
                    ppG = [psA, psC][bi]
                    ppU = [psB, psD][bi]
                    if tb == 0:
                        P.wait("tensor", "s", gu_sg[hc][0])
                    else:
                        P.wait("tensor", f"g{bi}", gu_sg[hc][1])
                    P.waits("tensor", relgu[bi])
                    for ci in range(NCH):
                        P.do("tensor", lambda E, S, ci=ci, b=bi, p=ppG, tb=tb:
                             E.matmul(p[:, :], wgb[b][:, ci * 128:(ci + 1) * 128], xqS[:, ci * NQ + tb * 512:ci * NQ + (tb + 1) * 512],
                                      start=(ci == 0), stop=(ci == NCH - 1)))
                        P.do("tensor", lambda E, S, ci=ci, b=bi, p=ppU, tb=tb:
                             E.matmul(p[:, :], wub[b][:, ci * 128:(ci + 1) * 128], xqS[:, ci * NQ + tb * 512:ci * NQ + (tb + 1) * 512],
                                      start=(ci == 0), stop=(ci == NCH - 1)), inc="t" if ci == NCH - 1 else None)
                    gu_relw[bi] = [("t", P.cnt["t"])] + ([(f"sw{bi}", gu_sg[hc][2])] if tb == 0 else [])
                    if hc + 2 < HC:
                        gu_load_sign(hc + 2, tb == 1)
                    P.wait("vector", "t")
                    P.do("vector", lambda E, S, p=ppG, tsl=tsl: E.tensor_mul(gt0[:], p[:, :], invsbc[:, tsl]), inc="v")
                    P.wait("scalar", "v")
                    P.do("scalar", lambda E, S: E.activation(gt1[:], gt0[:], AF.Silu), inc="s")
                    P.wait("vector", "s")
                    P.do("vector", lambda E, S, p=ppU: E.tensor_mul(gt0[:], gt1[:], p[:, :]))
                    P.do("vector", lambda E, S, hc=hc, tsl=tsl: E.tensor_mul(pS[:, hc * 512:(hc + 1) * 512], gt0[:], invsbc[:, tsl]), inc="v")
                    relgu[bi] = [("v", P.cnt["v"])]
                relgu_carry = [list(relgu[0]), list(relgu[1])]
                relpw = [[("v", P.cnt["v"])], [("v", P.cnt["v"])]]
                sgdone = [(0, 0)] * 2
                pend_stats = None
                for fc in range(NCH):
                    pw = [psE, psF][fc % 2]
                    for hf in range(2):
                        P.waits("gpsimd", wd_relw[hf])
                        if tb == 0:
                            P.do("gpsimd", lambda E, S, fc=fc, hf=hf:
                                 E.dma_start(out=wdb[:, hf * 4096:(hf + 1) * 4096], in_=wdT[fc, :, hf * 4096:(hf + 1) * 4096]), inc=f"g{hf}", n=16)
                            P.wait("scalar", f"g{hf}")
                            P.do("scalar", lambda E, S, hf=hf: E.sign(wdb[:, hf * 4096:(hf + 1) * 4096], wdb[:, hf * 4096:(hf + 1) * 4096]), inc="s")
                            P.wait("sync", "s")
                            P.do("sync", lambda E, S, fc=fc, hf=hf:
                                 E.dma_start(out=sgd[fc, :, hf * 4096:(hf + 1) * 4096], in_=wdb[:, hf * 4096:(hf + 1) * 4096]), inc=f"sw{hf}", n=16)
                            wd_spill[fc * 2 + hf] = P.cnt[f"sw{hf}"]
                        else:
                            P.wait("gpsimd", f"sw{hf}", wd_spill[fc * 2 + hf])
                            P.do("gpsimd", lambda E, S, fc=fc, hf=hf:
                                 E.dma_start(out=wdb[:, hf * 4096:(hf + 1) * 4096], in_=sgd[fc, :, hf * 4096:(hf + 1) * 4096]), inc=f"g{hf}", n=16)
                        sgdone[hf] = (P.cnt["s"], P.cnt[f"g{hf}"])
                    P.waits("tensor", relpw[fc % 2])
                    for hc in range(HC):
                        if hc == 0:
                            P.wait("tensor", "s" if tb == 0 else "g0", sgdone[0][0 if tb == 0 else 1])
                        if hc == HC // 2:
                            P.wait("tensor", "s" if tb == 0 else "g1", sgdone[1][0 if tb == 0 else 1])
                        P.do("tensor", lambda E, S, hc=hc, p=pw:
                             E.matmul(p[:, :], wdb[:, hc * 128:(hc + 1) * 128], pS[:, hc * 512:(hc + 1) * 512],
                                      start=(hc == 0), stop=(hc == HC - 1)), inc="t" if hc in (HC // 2 - 1, HC - 1) else None)
                        if hc == HC // 2 - 1:
                            wd_relw[0] = [("t", P.cnt["t"])] + ([("sw0", wd_spill[fc * 2])] if tb == 0 else [])
                    wd_relw[1] = [("t", P.cnt["t"])] + ([("sw1", wd_spill[fc * 2 + 1])] if tb == 0 else [])
                    if pend_stats is not None:
                        pend_stats()
                    P.wait("vector", "t")
                    P.do("vector", lambda E, S, fc=fc, p=pw: E.tensor_copy(ffS[:, fc * 512:(fc + 1) * 512], p[:, :]))
                    P.do("vector", lambda E, S, fc=fc, fb=fbf[fc % 2]: E.tensor_copy(fb[:], ffS[:, fc * 512:(fc + 1) * 512]), inc="v")
                    relpw[fc % 2] = [("v", P.cnt["v"])]
                    P.wait("scalar", "v")
                    P.do("scalar", lambda E, S, fb=fbf[fc % 2], fq=fsq[fc % 2]: E.activation(fq[:], fb[:], AF.Square), inc="s")

                    def mk_stats(fc=fc, scnt=P.cnt["s"], fb=fbf[fc % 2], fq=fsq[fc % 2]):
                        def f():
                            P.wait("tensor", "s", scnt)
                            P.do("tensor", lambda E, S:
                                 E.matmul(psS[0:1, 0:512], onescol[:], fb[:], start=(fc == 0), stop=(fc == NCH - 1)))
                            P.do("tensor", lambda E, S:
                                 E.matmul(psS[0:1, 512:1024], onescol[:], fq[:], start=(fc == 0), stop=(fc == NCH - 1)),
                                 inc="t")
                        return f
                    pend_stats = mk_stats()
                pend_stats()
                P.wait("vector", "t")
                ln_rows(512, 0, [(psS[0:1, 0:512], 0, 512)], [(psS[0:1, 512:1024], 0, 512)], rowA, rowB)
                P.wait("tensor", "v")
                bcast(rowA, 512, [psE])
                bcast(rowB, 512, [psF])
                P.wait("vector", "t")
                P.do("vector", lambda E, S: E.tensor_copy(RB4[:, 0:512], psE[:, :]))
                P.do("vector", lambda E, S: E.tensor_copy(RB4[:, 512:1024], psF[:, :]), inc="v")
                relx = [[], []]
                for fc in range(NCH):
                    bi = fc % 2
                    P.waits("sync", relx[bi])
                    P.do("sync", lambda E, S, fc=fc, b=bi, tb=tb:
                         E.dma_start(out=x2c[b][:], in_=x2scr[fc * 128:(fc + 1) * 128, tb * 512:(tb + 1) * 512]), inc=f"xb{bi}", n=16)
                    P.wait("vector", f"xb{bi}")
                    P.waits("vector", ob_rel[bi])
                    P.do("vector", lambda E, S, fc=fc: E.tensor_mul(gt0[:], ffS[:, fc * 512:(fc + 1) * 512], RB4[:, 0:512]))
                    P.do("vector", lambda E, S: E.tensor_add(gt0[:], gt0[:], RB4[:, 512:1024]))
                    P.do("vector", lambda E, S, b=bi: E.tensor_add(ob[b][:], gt0[:], x2c[b][:]), inc="v")
                    relx[bi] = [("v", P.cnt["v"])]
                    if bi == 0:
                        P.wait("sync", "v")
                        P.dma("sync", lambda E, S, fc=fc, tb=tb, b=bi:
                              E.dma_start(out=out[fc * 128:(fc + 1) * 128, tb * 512:(tb + 1) * 512], in_=ob[b][:]), "sout")
                        ob_rel[bi] = [("sout", P.cnt["sout"])]
                    else:
                        P.wait("scalar", "v")
                        P.dma("scalar", lambda E, S, fc=fc, tb=tb, b=bi:
                              E.dma_start(out=out[fc * 128:(fc + 1) * 128, tb * 512:(tb + 1) * 512], in_=ob[b][:]), "gout")
                        ob_rel[bi] = [("gout", P.cnt["gout"])]
            a6.close()

        P.barrier()
        for st_ in reversed(stacks):
            st_.close()
        ops = P.ops

        @blk.sync
        def _(E):
            for f in ops["sync"]:
                f(E, sems)

        @blk.gpsimd
        def _(E):
            for f in ops["gpsimd"]:
                f(E, sems)

        @blk.vector
        def _(E):
            for f in ops["vector"]:
                f(E, sems)

        @blk.scalar
        def _(E):
            for f in ops["scalar"]:
                f(E, sems)

        @blk.tensor
        def _(E):
            for f in ops["tensor"]:
                f(E, sems)

    return nc



def _lhst_blocks(W):
    F_, Ci = W.shape
    A = np.asarray(W).reshape(F_ // 128, 128, Ci // 128, 128)
    return np.ascontiguousarray(A.transpose(0, 3, 2, 1).reshape(F_ // 128, 128, Ci))


def _rope_tables():
    import jax
    import jax.numpy as jnp
    with jax.default_device(jax.devices("cpu")[0]):
        inv = 1.0 / 10000.0 ** (jnp.arange(0, 128, 2, dtype=jnp.float32) / 128)
        t = jnp.arange(T, dtype=jnp.float32)
        fr = t[:, None] * inv[None, :]
        sin = np.asarray(jnp.repeat(jnp.sin(fr), 2, axis=-1))
        cos = np.asarray(jnp.repeat(jnp.cos(fr), 2, axis=-1))
    return sin, cos


def _own_positions(p):
    return np.concatenate([np.arange(128 * t, 128 * (t + 1)) for t in OWN[p]])


def _perm_positions(p):
    return np.concatenate([np.arange(128 * t, 128 * (t + 1)) for t in PERM[p]])


def _emask_for(p):
    NEG = np.float32(-1e9)
    tri = np.where(np.arange(128)[:, None] <= np.arange(128)[None, :],
                   np.float32(0.0), NEG).astype(ml_dtypes.bfloat16)
    m = np.zeros((128, max(MASK_SLOTS, 1) * 128), dtype=ml_dtypes.bfloat16)
    for (ks, b_), (o0, L) in MASK_OFF.items():
        nU = PROFS[b_][ks]
        g = PERM[p][ks]
        for s_ in range(L):
            qslot = b_ * 4 + (4 - nU) + s_
            qt = PERM[p][qslot]
            sl = slice((o0 + s_) * 128, (o0 + s_ + 1) * 128)
            if qt < g:
                m[:, sl] = NEG
            elif qt == g:
                m[:, sl] = tri
    return m


def make_inmaps(L0_attn_x, L0_attn_W_qkv, L0_attn_W_o, L0_ffn_W_g, L0_ffn_W_u, L0_ffn_W_d):
    x = np.asarray(L0_attn_x)
    Wqkv = np.asarray(L0_attn_W_qkv)
    shared = {
        "wqT": _lhst_blocks(Wqkv[0:2048]),
        "wkT": _lhst_blocks(Wqkv[2048:4096]),
        "wvT": np.ascontiguousarray(Wqkv[4096:6144].T.reshape(NCH, 128, C)),
        "woT": _lhst_blocks(np.asarray(L0_attn_W_o)),
        "wgT": _lhst_blocks(np.asarray(L0_ffn_W_g)),
        "wuT": _lhst_blocks(np.asarray(L0_ffn_W_u)),
        "wdT": _lhst_blocks(np.asarray(L0_ffn_W_d)),
        "ident": np.eye(128, dtype=np.float32),
    }
    sin, cos = _rope_tables()
    sgn = np.where(np.arange(128) % 2 == 0, -1.0, 1.0).astype(np.float32)
    rkc = np.ascontiguousarray(cos.T)
    rks = np.ascontiguousarray(sin.T * sgn[:, None])
    in_maps = []
    for core in range(8):
        b, p = core // 2, core % 2
        pp = _perm_positions(p)
        xt = np.ascontiguousarray(x[b].T[:, pp])
        d = dict(shared)
        d["xT"] = xt
        d["xown"] = np.ascontiguousarray(xt[:, 0:NQ])
        d["rkc"] = np.ascontiguousarray(rkc[:, pp])
        d["rks"] = np.ascontiguousarray(rks[:, pp])
        d["emask"] = _emask_for(p)
        in_maps.append(d)
    return in_maps


def kernel(L0_attn_x, L0_attn_W_qkv, L0_attn_W_o, L0_ffn_W_g, L0_ffn_W_u, L0_ffn_W_d):
    in_maps = make_inmaps(L0_attn_x, L0_attn_W_qkv, L0_attn_W_o, L0_ffn_W_g, L0_ffn_W_u, L0_ffn_W_d)
    nc = build_nc(phase=6)
    res = run_bass_kernel_spmd(nc, in_maps, core_ids=list(range(8)))
    y = np.empty((4, T, C), dtype=np.float32)
    for core in range(8):
        b, p = core // 2, core % 2
        pos = _own_positions(p)
        y[b, pos, :] = np.asarray(res.results[core]["out"]).T
    return y


# version 18
# speedup vs baseline: 1.6655x; 1.0174x over previous
import contextlib
import numpy as np
import ml_dtypes
import concourse.bass as bass
import concourse.mybir as mybir
from concourse.bass_utils import run_bass_kernel_spmd

F32 = mybir.dt.float32
BF16 = mybir.dt.bfloat16
I32 = mybir.dt.int32
AF = mybir.ActivationFunctionType
ALU = mybir.AluOpType

C = 2048
T = 2048
NH = 16
NCH = 16
NQ = 1024
HC = 64
RMS_EPS = 1.1920929e-07
LN_EPS = 1e-5
FFN_EPS = 1e-8
SCALE = float(np.float32(1.0) / np.float32(np.sqrt(np.float32(128.0))))
SHUF = [i ^ 1 for i in range(32)]

OWN = {0: [0, 3, 4, 7, 8, 11, 12, 15], 1: [1, 2, 5, 6, 9, 10, 13, 14]}
PERM = {p: OWN[p] + sorted(set(range(16)) - set(OWN[p])) for p in (0, 1)}

def _profiles():
    na, nb = [0] * 16, [0] * 16
    for ks in range(16):
        for p in (0, 1):
            g = PERM[p][ks]
            ca = sum(1 for j in range(0, 4) if PERM[p][j] >= g)
            cb = sum(1 for j in range(4, 8) if PERM[p][j] >= g)
            na[ks] = max(na[ks], ca)
            nb[ks] = max(nb[ks], cb)
    return na, nb

NA_U, NB_U = _profiles()
PROFS = (NA_U, NB_U)


def _mask_layout():
    off, o = {}, 0
    for b_ in (0, 1):
        for ks in range(16):
            nU = PROFS[b_][ks]
            if nU == 0:
                continue
            worst = 0
            for p in (0, 1):
                g = PERM[p][ks]
                qs = [PERM[p][j] for j in range(b_ * 4, b_ * 4 + 4)]
                n_p = sum(1 for t in qs if t >= g)
                worst = max(worst, (nU - n_p) + (1 if g in qs else 0))
            if worst > 0:
                off[(ks, b_)] = (o, worst)
                o += worst
    return off, o


MASK_OFF, MASK_SLOTS = _mask_layout()
for _p in (0, 1):
    for _b in (0, 1):
        _qs = [PERM[_p][j] for j in range(_b * 4, _b * 4 + 4)]
        assert all(_qs[i] < _qs[i + 1] for i in range(3))

ENGINES = ("sync", "gpsimd", "vector", "scalar", "tensor")
SEMS = ("t", "v", "s", "din", "xb0", "xb1", "spl0", "spl1", "sx2", "sout", "gout", "g0", "g1", "gin", "gin0", "gin1", "gw", "sw0", "sw1")


class Prog:
    def __init__(self):
        self.ops = {e: [] for e in ENGINES}
        self.cnt = {n: 0 for n in SEMS}

    def emit(self, eng, fn):
        self.ops[eng].append(fn)

    def wait(self, eng, sem, val=None):
        v = self.cnt[sem] if val is None else val
        if v > 0:
            self.emit(eng, lambda E, S, _s=sem, _v=v: E.wait_ge(S[_s], _v))

    def waits(self, eng, rel):
        for sem, v in rel:
            self.wait(eng, sem, v)

    def do(self, eng, fn, inc=None, n=1):
        if eng == "vector" or eng == "scalar":
            sem = "v" if eng == "vector" else "s"
            prev = self.cnt[sem]
            self.cnt[sem] = prev + 1
            def wrapped(E, S, _f=fn, _s=sem, _p=prev):
                if _p > 0:
                    E.wait_ge(S[_s], _p)
                return _f(E, S).then_inc(S[_s], 1)
            self.emit(eng, wrapped)
            return
        if inc is not None:
            self.cnt[inc] += n
            self.emit(eng, lambda E, S, _f=fn, _i=inc, _n=n: _f(E, S).then_inc(S[_i], _n))
        else:
            self.emit(eng, lambda E, S, _f=fn: _f(E, S))

    def dma(self, eng, fn, inc, n=16):
        self.cnt[inc] += n
        self.emit(eng, lambda E, S, _f=fn, _i=inc, _n=n: _f(E, S).then_inc(S[_i], _n))

    def snap(self):
        return [(s, v) for s, v in self.cnt.items() if v > 0]

    def barrier(self, engs=ENGINES):
        snap = self.snap()
        for e in engs:
            self.waits(e, snap)


def build_nc(phase=6):
    nc = bass.Bass("TRN2", target_bir_lowering=False)
    dtn = nc.dram_tensor
    xT = dtn("xT", [C, T], F32, kind="ExternalInput").ap()
    xown = dtn("xown", [C, NQ], F32, kind="ExternalInput").ap()
    wqT = dtn("wqT", [NH, 128, C], F32, kind="ExternalInput").ap()
    wkT = dtn("wkT", [NH, 128, C], F32, kind="ExternalInput").ap()
    wvT = dtn("wvT", [NCH, 128, C], F32, kind="ExternalInput").ap()
    woT = dtn("woT", [NCH, 128, C], F32, kind="ExternalInput").ap()
    wgT = dtn("wgT", [HC, 128, C], F32, kind="ExternalInput").ap()
    wuT = dtn("wuT", [HC, 128, C], F32, kind="ExternalInput").ap()
    wdT = dtn("wdT", [NCH, 128, 8192], F32, kind="ExternalInput").ap()
    rkc = dtn("rkc", [128, T], F32, kind="ExternalInput").ap()
    rks = dtn("rks", [128, T], F32, kind="ExternalInput").ap()
    emaskD = dtn("emask", [128, max(MASK_SLOTS, 1) * 128], BF16, kind="ExternalInput").ap()
    identD = dtn("ident", [128, 128], F32, kind="ExternalInput").ap()
    out = dtn("out", [C, NQ], F32, kind="ExternalOutput").ap()
    aoscr = dtn("aoscr", [C, NQ], BF16, kind="Internal").ap()
    sgg = dtn("sgg", [HC, 128, C], BF16, kind="Internal").ap()
    sgu = dtn("sgu", [HC, 128, C], BF16, kind="Internal").ap()
    sgd = dtn("sgd", [NCH, 128, 8192], BF16, kind="Internal").ap()
    x2scr = dtn("x2scr", [C, NQ], F32, kind="Internal").ap()
    dbg = dtn("dbg", [C, T], F32, kind="ExternalOutput").ap() if phase < 6 else None

    P = Prog()
    ctx = contextlib.ExitStack()
    stacks = []
    with ctx:
        def sbt(st, nm, sh, d=F32):
            return st.enter_context(nc.sbuf_tensor(nm, sh, d))
        onescol = sbt(ctx, "onescol", [128, 1], BF16)
        onesrow = sbt(ctx, "onesrow", [1, 128], F32)
        ident = sbt(ctx, "ident_s", [128, 128], F32)
        rowA = sbt(ctx, "rowA", [1, NQ])
        rowB = sbt(ctx, "rowB", [1, NQ])
        invsbc = sbt(ctx, "invsbc", [128, NQ])
        epsT = sbt(ctx, "epsT", [128, 3])
        psA = ctx.enter_context(nc.psum_tensor("psA", [128, 512], F32))
        psB = ctx.enter_context(nc.psum_tensor("psB", [128, 512], F32))
        psC = ctx.enter_context(nc.psum_tensor("psC", [128, 512], F32))
        psD = ctx.enter_context(nc.psum_tensor("psD", [128, 512], F32))
        psE = ctx.enter_context(nc.psum_tensor("psE", [128, 512], F32))
        psF = ctx.enter_context(nc.psum_tensor("psF", [128, 512], F32))
        psS = ctx.enter_context(nc.psum_tensor("psS", [1, 1024], F32))
        PS4 = [psA, psB, psC, psD]
        sems = {n: ctx.enter_context(nc.semaphore("sem_" + n)) for n in SEMS}
        blk = ctx.enter_context(nc.Block())

        P.do("sync", lambda E, S: E.dma_start(out=ident[:], in_=identD), inc="din", n=16)
        P.do("vector", lambda E, S: E.memset(onescol[:], 1.0))
        P.do("vector", lambda E, S: E.memset(onesrow[:], 1.0))
        P.do("vector", lambda E, S: E.memset(epsT[:, 0:1], LN_EPS))
        P.do("vector", lambda E, S: E.memset(epsT[:, 1:2], RMS_EPS))
        P.do("vector", lambda E, S: E.memset(epsT[:, 2:3], FFN_EPS), inc="v")

        def ln_rows(ntok, epscol, s1ap, s2ap, rR, rB):
            for a_, c0, n in s1ap:
                P.do("vector", lambda E, S, a=a_, c=c0, n=n: E.tensor_scalar_mul(rB[0:1, c:c + n], a, 1.0 / C))
            for a_, c0, n in s2ap:
                P.do("vector", lambda E, S, a=a_, c=c0, n=n: E.tensor_scalar_mul(rR[0:1, c:c + n], a, 1.0 / C))
            for a_, c0, n in s1ap:
                P.do("vector", lambda E, S, a=a_, c=c0, n=n: E.tensor_mul(a, rB[0:1, c:c + n], rB[0:1, c:c + n]))
            for a_, c0, n in s1ap:
                P.do("vector", lambda E, S, a=a_, c=c0, n=n: E.tensor_sub(rR[0:1, c:c + n], rR[0:1, c:c + n], a), inc="v")
            P.wait("scalar", "v")
            P.do("scalar", lambda E, S: E.activation(rR[0:1, 0:ntok], rR[0:1, 0:ntok], AF.Sqrt, bias=epsT[0:1, epscol:epscol + 1]), inc="s")
            P.wait("vector", "s")
            P.do("vector", lambda E, S: E.reciprocal(rR[0:1, 0:ntok], rR[0:1, 0:ntok]))
            P.do("vector", lambda E, S: E.scalar_tensor_tensor(rB[0:1, 0:ntok], rB[0:1, 0:ntok], -1.0, rR[0:1, 0:ntok], ALU.mult, ALU.mult), inc="v")

        def bcast(rowap, ncols, pslist):
            nb = (ncols + 511) // 512
            for i in range(nb):
                n = min(512, ncols - i * 512)
                P.do("tensor", lambda E, S, p=pslist[i], c=i * 512, n=n, r=rowap:
                     E.matmul(p[:, 0:n], onesrow[:], r[0:1, c:c + n], start=True, stop=True),
                     inc="t" if i == nb - 1 else None)

        a1 = contextlib.ExitStack(); stacks.append(a1)
        LNx = sbt(a1, "LNx", [128, NCH * T], BF16)
        V = sbt(a1, "V", [128, NCH * T], BF16)
        a1b = contextlib.ExitStack(); stacks.append(a1b)
        xbuf = [sbt(a1b, f"xbuf{i}", [128, T]) for i in range(2)]
        xbf = [sbt(a1b, f"xbf{i}", [128, T], BF16) for i in range(1)]
        xsq = [sbt(a1b, f"xsq{i}", [128, T], BF16) for i in range(1)]
        RBs = sbt(a1b, "RBs", [128, 2 * T])
        rowAT = sbt(a1b, "rowAT", [1, T])
        rowBT = sbt(a1b, "rowBT", [1, T])

        s1rows = [(psA[0:1, :], 0, 512), (psB[0:1, :], 512, 512), (psC[0:1, :], 1024, 512), (psD[0:1, :], 1536, 512)]
        s2rows = [(psE[0:1, :], 0, 512), (psF[0:1, :], 512, 512), (psS[0:1, 0:512], 1024, 512), (psS[0:1, 512:1024], 1536, 512)]
        rel = [[], []]
        for j in range(NCH):
            bi = j % 2
            P.waits("sync", rel[bi])
            P.do("sync", lambda E, S, j=j, b=bi: E.dma_start(out=xbuf[b][:], in_=xT[j * 128:(j + 1) * 128, :]), inc=f"xb{bi}", n=16)
            P.wait("vector", f"xb{bi}")
            P.wait("vector", "t", P.cnt["t"])
            P.do("vector", lambda E, S, b=bi: E.tensor_copy(xbf[0][:], xbuf[b][:]), inc="v")
            P.wait("scalar", f"xb{bi}")
            P.wait("scalar", "t", P.cnt["t"])
            P.do("scalar", lambda E, S, b=bi: E.activation(xsq[0][:], xbuf[b][:], AF.Square), inc="s")
            P.wait("tensor", "v")
            P.wait("tensor", "s")
            for tb in range(4):
                P.do("tensor", lambda E, S, j=j, tb=tb, p=PS4[tb]:
                     E.matmul(p[0:1, :], onescol[:], xbf[0][:, tb * 512:(tb + 1) * 512], start=(j == 0), stop=(j == NCH - 1)))
            for tb in range(4):
                tgt = [psE[0:1, :], psF[0:1, :], psS[0:1, 0:512], psS[0:1, 512:1024]][tb]
                P.do("tensor", lambda E, S, j=j, tb=tb, tg=tgt:
                     E.matmul(tg, onescol[:], xsq[0][:, tb * 512:(tb + 1) * 512], start=(j == 0), stop=(j == NCH - 1)),
                     inc="t" if tb == 3 else None)
            rel[bi] = [("v", P.cnt["v"]), ("s", P.cnt["s"])]
        P.wait("vector", "t")
        ln_rows(T, 0, s1rows, s2rows, rowAT, rowBT)
        P.wait("tensor", "v")
        for which, row in ((0, rowAT), (1, rowBT)):
            bcast(row, T, PS4)
            P.wait("vector", "t")
            for tb in range(4):
                P.do("vector", lambda E, S, w=which, tb=tb, p=PS4[tb]:
                     E.tensor_copy(RBs[:, w * T + tb * 512:w * T + (tb + 1) * 512], p[:, :]), inc="v" if tb == 3 else None)
            P.wait("tensor", "v")
        rel = [P.snap(), P.snap()]
        for j in range(NCH):
            bi = j % 2
            P.waits("sync", rel[bi])
            P.do("sync", lambda E, S, j=j, b=bi: E.dma_start(out=xbuf[b][:], in_=xT[j * 128:(j + 1) * 128, :]), inc=f"xb{bi}", n=16)
            P.wait("vector", f"xb{bi}")
            P.do("vector", lambda E, S, b=bi: E.tensor_mul(xbuf[b][:], xbuf[b][:], RBs[:, 0:T]))
            P.do("vector", lambda E, S, j=j, b=bi: E.tensor_add(LNx[:, j * T:(j + 1) * T], xbuf[b][:], RBs[:, T:2 * T]), inc="v")
            rel[bi] = [("v", P.cnt["v"])]
        P.barrier()
        a1b.close()
        if phase == 1:
            for j in range(NCH):
                P.do("gpsimd", lambda E, S, j=j: E.dma_start(out=dbg[j * 128:(j + 1) * 128, :], in_=LNx[:, j * T:(j + 1) * T]), inc="gin", n=16)

        if phase >= 2:
            a2 = contextlib.ExitStack(); stacks.append(a2); stacks.append(a2)
            wvb = [sbt(a2, f"wvb{i}", [128, NCH * 512], BF16) for i in range(2)]
            relw = [[], []]
            relv4 = [[("v", P.cnt["v"])] for _ in range(4)]
            for fb in range(4):
                bi = fb % 2
                P.waits("gpsimd", relw[bi])
                for ci in range(NCH):
                    P.do("gpsimd", lambda E, S, fb=fb, ci=ci, b=bi:
                         E.dma_start(out=wvb[b][:, ci * 512:(ci + 1) * 512], in_=wvT[ci, :, fb * 512:(fb + 1) * 512]), inc=f"g{bi}", n=16)
                P.wait("tensor", f"g{bi}")
                for ti in range(NCH):
                    pst = PS4[ti % 4]
                    P.waits("tensor", relv4[ti % 4])
                    for ci in range(NCH):
                        P.do("tensor", lambda E, S, ti=ti, ci=ci, b=bi, p=pst:
                             E.matmul(p[:, :], LNx[:, ci * T + ti * 128:ci * T + (ti + 1) * 128], wvb[b][:, ci * 512:(ci + 1) * 512],
                                      start=(ci == 0), stop=(ci == NCH - 1)), inc="t" if ci == NCH - 1 else None)
                    P.wait("vector", "t")
                    P.do("vector", lambda E, S, ti=ti, fb=fb, p=pst:
                         E.tensor_copy(V[:, ti * T + fb * 512:ti * T + (fb + 1) * 512], p[:, :]), inc="v")
                    relv4[ti % 4] = [("v", P.cnt["v"])]
                relw[bi] = [("t", P.cnt["t"])]
            a2.close()
            P.barrier()
            if phase == 2:
                for ti in range(NCH):
                    P.do("gpsimd", lambda E, S, ti=ti: E.dma_start(out=dbg[ti * 128:(ti + 1) * 128, :], in_=V[:, ti * T:(ti + 1) * T]), inc="gin", n=16)

        if phase >= 3:
            a3 = contextlib.ExitStack(); stacks.append(a3)
            rkcS = sbt(a3, "rkcS", [128, T], BF16)
            rksS = sbt(a3, "rksS", [128, T], BF16)
            emask = sbt(a3, "emask_s", [128, max(MASK_SLOTS, 1) * 128], BF16)
            wkb = [sbt(a3, f"wkb{i}", [128, C], BF16) for i in range(2)]
            wqb = [sbt(a3, f"wqb{i}", [128, C], BF16) for i in range(2)]
            krot3 = [sbt(a3, f"krot{i}", [128, T], BF16) for i in range(3)]
            qhat3 = [sbt(a3, f"qhat{i}", [128, NQ], BF16) for i in range(2)]
            ksc3 = [sbt(a3, f"ksc{i}", [128, 16]) for i in range(3)]
            qrot = sbt(a3, "qrot", [128, NQ], BF16)
            sqt = sbt(a3, "sqt", [128, T], BF16)
            tmpf = sbt(a3, "tmpf", [128, 1024], BF16)
            rSb = sbt(a3, "rSb", [128, 512])
            tshuf = rSb
            ebuf = [sbt(a3, f"ebuf{i}", [128, 512], BF16) for i in range(3)]
            identB = sbt(a3, "identB", [128, 128], BF16)
            aout1 = [sbt(a3, f"aout1_{i}", [128, NQ], BF16) for i in range(2)]
            for src, dst in ((rkc, rkcS), (rks, rksS)):
                P.do("gpsimd", lambda E, S, s_=src, d_=dst: E.dma_start(out=d_[:], in_=s_), inc="din", n=16)
            if MASK_SLOTS > 0:
                P.do("sync", lambda E, S: E.dma_start(out=emask[:], in_=emaskD), inc="din", n=16)
            P.do("gpsimd", lambda E, S: E.dma_start(out=identB[:], in_=identD), inc="din", n=16)
            P.wait("vector", "din")
            P.wait("scalar", "din")
            rel_ao = [[], []]
            rel_wkb = [[], []]
            rel_wqb = [[], []]
            ksum_done = [0]
            qstats_done = [0]
            LAST_KT = {0: max(kt for kt in range(16) if NA_U[kt] > 0),
                       1: max(kt for kt in range(16) if NB_U[kt] > 0)}
            ITERS = [(kt, b_, PROFS[b_][kt]) for kt in range(16) for b_ in (0, 1) if PROFS[b_][kt] > 0]

            def prepK(h):
                par, bi = h % 3, h % 2
                krot, ksc = krot3[par], ksc3[par]
                steps = []
                kld = [0]

                def s_loadk():
                    P.waits("gpsimd", rel_wkb[bi])
                    P.do("gpsimd", lambda E, S, h=h, b=bi: E.dma_start(out=wkb[b][:], in_=wkT[h]), inc=f"g{bi}", n=16)
                    kld[0] = P.cnt[f"g{bi}"]
                steps.append(s_loadk)

                def mk_kprod(half):
                    def f():
                        P.wait("tensor", f"g{bi}", kld[0])
                        P.wait("tensor", "v", P.cnt["v"])
                        for ci in range(NCH):
                            for q_, p in ((0, psC), (1, psD)):
                                tb = 2 * half + q_
                                P.do("tensor", lambda E, S, ci=ci, tb=tb, b=bi, p=p:
                                     E.matmul(p[:, :], wkb[b][:, ci * 128:(ci + 1) * 128],
                                              LNx[:, ci * T + tb * 512:ci * T + (tb + 1) * 512],
                                              start=(ci == 0), stop=(ci == NCH - 1)),
                                     inc="t" if (ci == NCH - 1 and q_ == 1) else None)
                        if half == 1:
                            rel_wkb[bi] = [("t", P.cnt["t"])]
                    return f

                def mk_krope(half):
                    def f():
                        P.wait("vector", "t")
                        for q_, p in ((0, psC), (1, psD)):
                            tb = 2 * half + q_
                            sl = slice(tb * 512, (tb + 1) * 512)
                            P.do("vector", lambda E, S, p=p, sl=sl: E.tensor_mul(tmpf[:, 0:512], p[:, :], rkcS[:, sl]))
                            P.do("vector", lambda E, S, p=p: E.stream_shuffle(tshuf[:, 0:512], p[:, :], SHUF))
                            P.do("vector", lambda E, S, sl=sl: E.tensor_mul(tmpf[:, 512:1024], tshuf[:, 0:512], rksS[:, sl]))
                            P.do("vector", lambda E, S, sl=sl, kr=krot: E.tensor_add(kr[:, sl], tmpf[:, 0:512], tmpf[:, 512:1024]), inc="v")
                    return f

                for half in (0, 1):
                    steps.append(mk_kprod(half))
                    steps.append(mk_krope(half))

                def s_ksq():
                    P.wait("scalar", "t", ksum_done[0])
                    P.wait("scalar", "t", qstats_done[0])
                    P.wait("scalar", "v")
                    P.do("scalar", lambda E, S, kr=krot: E.activation(sqt[:, 0:T], kr[:, 0:T], AF.Square), inc="s")
                steps.append(s_ksq)

                def s_ksum():
                    P.wait("tensor", "s")
                    P.wait("tensor", "v", P.cnt["v"])
                    for kt in range(16):
                        p = psC if kt < 8 else psD
                        P.do("tensor", lambda E, S, kt=kt, p=p:
                             E.matmul(p[:, kt % 8:kt % 8 + 1], sqt[:, kt * 128:(kt + 1) * 128], onescol[:], start=True, stop=True),
                             inc="t" if kt == 15 else None)
                    ksum_done[0] = P.cnt["t"]
                    P.wait("vector", "t")
                    P.do("vector", lambda E, S, k=ksc: E.tensor_copy(k[:, 0:8], psC[:, 0:8]))
                    P.do("vector", lambda E, S, k=ksc: E.tensor_copy(k[:, 8:16], psD[:, 0:8]), inc="v")

                steps.append(s_ksum)

                def s_kscrow():
                    P.wait("scalar", "v")
                    P.do("scalar", lambda E, S, k=ksc: E.activation(k[:, 0:16], k[:, 0:16], AF.Sqrt, bias=epsT[:, 1:2], scale=1.0 / 128.0), inc="s")
                    P.wait("vector", "s")
                    P.do("vector", lambda E, S, k=ksc: E.reciprocal(k[:, 0:16], k[:, 0:16]))
                    P.do("vector", lambda E, S, k=ksc: E.tensor_scalar_mul(k[:, 0:16], k[:, 0:16], SCALE), inc="v")
                steps.append(s_kscrow)
                return steps

            def prepQ(h):
                bi = h % 2
                qhat = qhat3[bi]
                steps = []
                qld = [0]

                def s_loadq():
                    P.waits("gpsimd", rel_wqb[bi])
                    P.do("gpsimd", lambda E, S, h=h, b=bi: E.dma_start(out=wqb[b][:], in_=wqT[h]), inc=f"gin{bi}", n=16)
                    qld[0] = P.cnt[f"gin{bi}"]
                steps.append(s_loadq)

                def s_qprod():
                    P.wait("tensor", f"gin{bi}", qld[0])
                    P.wait("tensor", "v", P.cnt["v"])
                    for ci in range(NCH):
                        for q_, p in ((0, psC), (1, psD)):
                            P.do("tensor", lambda E, S, ci=ci, tb=q_, b=bi, p=p:
                                 E.matmul(p[:, :], wqb[b][:, ci * 128:(ci + 1) * 128],
                                          LNx[:, ci * T + tb * 512:ci * T + (tb + 1) * 512],
                                          start=(ci == 0), stop=(ci == NCH - 1)),
                                 inc="t" if (ci == NCH - 1 and q_ == 1) else None)
                    rel_wqb[bi] = [("t", P.cnt["t"])]
                steps.append(s_qprod)

                def s_qrope():
                    P.wait("vector", "t")
                    for q_, p in ((0, psC), (1, psD)):
                        sl = slice(q_ * 512, (q_ + 1) * 512)
                        P.do("vector", lambda E, S, p=p, sl=sl: E.tensor_mul(tmpf[:, 0:512], p[:, :], rkcS[:, sl]))
                        P.do("vector", lambda E, S, p=p: E.stream_shuffle(tshuf[:, 0:512], p[:, :], SHUF))
                        P.do("vector", lambda E, S, sl=sl: E.tensor_mul(tmpf[:, 512:1024], tshuf[:, 0:512], rksS[:, sl]))
                        P.do("vector", lambda E, S, sl=sl: E.tensor_add(qrot[:, sl], tmpf[:, 0:512], tmpf[:, 512:1024]), inc="v")
                    P.wait("scalar", "t", ksum_done[0])
                    P.wait("scalar", "v")
                    P.do("scalar", lambda E, S: E.activation(sqt[:, 0:NQ], qrot[:, 0:NQ], AF.Square), inc="s")
                steps.append(s_qrope)

                def s_qstats():
                    P.wait("tensor", "s")
                    P.wait("tensor", "v", P.cnt["v"])
                    for q_, p in ((0, psC), (1, psD)):
                        P.do("tensor", lambda E, S, tb=q_, p=p:
                             E.matmul(p[0:1, :], onescol[:], sqt[:, tb * 512:(tb + 1) * 512], start=True, stop=True),
                             inc="t" if q_ == 1 else None)
                    qstats_done[0] = P.cnt["t"]
                    P.wait("vector", "t")
                    P.do("vector", lambda E, S: E.tensor_copy(rowB[0:1, 0:512], psC[0:1, :]))
                    P.do("vector", lambda E, S: E.tensor_copy(rowB[0:1, 512:1024], psD[0:1, :]), inc="v")
                    P.wait("scalar", "v")
                    P.do("scalar", lambda E, S: E.activation(rowB[0:1, 0:NQ], rowB[0:1, 0:NQ], AF.Sqrt, bias=epsT[0:1, 1:2], scale=1.0 / 128.0), inc="s")
                    P.wait("vector", "s")
                    P.do("vector", lambda E, S: E.reciprocal(rowB[0:1, 0:NQ], rowB[0:1, 0:NQ]), inc="v")
                steps.append(s_qstats)

                def s_qhat():
                    P.wait("tensor", "v", P.cnt["v"])
                    P.do("tensor", lambda E, S: E.matmul(psC[:, :], onesrow[:], rowB[0:1, 0:512], start=True, stop=True))
                    P.do("tensor", lambda E, S: E.matmul(psD[:, :], onesrow[:], rowB[0:1, 512:1024], start=True, stop=True), inc="t")
                    P.wait("vector", "t")
                    for q_, p in ((0, psC), (1, psD)):
                        P.do("vector", lambda E, S, tb=q_, p=p, q=qhat:
                             E.tensor_mul(q[:, tb * 512:(tb + 1) * 512], qrot[:, tb * 512:(tb + 1) * 512], p[:, :]),
                             inc="v" if q_ == 1 else None)
                steps.append(s_qhat)
                return steps

            def make_epi(h):
                ao = aout1[h % 2]
                epi_rel = {}

                def e1():
                    P.wait("tensor", "v", P.cnt["v"])
                    P.do("tensor", lambda E, S: E.matmul(psC[:, :], onesrow[:], rowA[0:1, 0:512], start=True, stop=True))
                    P.do("tensor", lambda E, S: E.matmul(psD[:, :], onesrow[:], rowA[0:1, 512:1024], start=True, stop=True), inc="t")

                def e2():
                    P.wait("vector", "t")
                    P.waits("vector", rel_ao[h % 2])
                    for tb in range(2):
                        P.do("vector", lambda E, S, tb=tb, p=[psC, psD][tb]: E.reciprocal(rSb[:, 0:512], p[:, :]))
                        P.do("vector", lambda E, S, tb=tb, pv=[psE, psF][tb], a=ao:
                             E.tensor_mul(a[:, tb * 512:(tb + 1) * 512], pv[:, :], rSb[:, 0:512]), inc="v")
                    epi_rel["r"] = [("v", P.cnt["v"])]
                    P.wait("sync", "v")
                    P.do("sync", lambda E, S, h=h, a=ao: E.dma_start(out=aoscr[h * 128:(h + 1) * 128, :], in_=a[:]), inc=f"spl{h % 2}", n=16)
                    rel_ao[h % 2] = [(f"spl{h % 2}", P.cnt[f"spl{h % 2}"])]
                return [e1, e2], epi_rel

            for st_ in prepK(0):
                st_()
            for st_ in prepQ(0):
                st_()
            for st_ in prepK(1):
                st_()
            prep_rel = [("t", P.cnt["t"]), ("v", P.cnt["v"]), ("s", P.cnt["s"])]
            pend_epi = None

            for h in range(NH):
                krot, ksc, qhat = krot3[h % 3], ksc3[h % 3], qhat3[h % 2]
                steps = list(pend_epi[0]) if pend_epi else []
                epi_rel_prev = pend_epi[1] if pend_epi else None
                steps += prepQ(h + 1) if h + 1 < NH else []
                steps += prepK(h + 2) if h + 2 < NH else []
                si = 0
                P.waits("tensor", prep_rel)
                P.waits("scalar", prep_rel)
                P.wait("tensor", "s", P.cnt["s"])
                exp_done = []
                ready = []

                def emit_scores(i):
                    kt, b_, nU = ITERS[i]
                    qc0 = b_ * 512 + (4 - nU) * 128
                    ncols = nU * 128
                    psc = [psA, psB][i % 2]
                    eb = ebuf[i % 2]
                    if i >= 2:
                        P.wait("tensor", "s", exp_done[i - 2])
                    P.do("tensor", lambda E, S, kt=kt, qc0=qc0, n=ncols, p=psc, kr=krot, q=qhat:
                         E.matmul(p[:, 0:n], kr[:, kt * 128:(kt + 1) * 128], q[:, qc0:qc0 + n], start=True, stop=True), inc="t")
                    mo = MASK_OFF.get((kt, b_))
                    if mo is not None:
                        o0, L = mo
                        P.do("tensor", lambda E, S, p=psc, o0=o0, L=L:
                             E.matmul(p[:, 0:L * 128], identB[:], emask[:, o0 * 128:(o0 + L) * 128], start=False, stop=True), inc="t")
                    P.wait("scalar", "t")
                    P.do("scalar", lambda E, S, kt=kt, n=ncols, p=psc, e=eb, k=ksc:
                         E.activation(e[:, 0:n], p[:, 0:n], AF.Exp, scale=k[:, kt:kt + 1]), inc="s")
                    exp_done.append(P.cnt["s"])
                    ready.append([("s", P.cnt["s"])])

                def emit_sumpv(i):
                    kt, b_, nU = ITERS[i]
                    qc0 = b_ * 512 + (4 - nU) * 128
                    ncols = nU * 128
                    eb = ebuf[i % 2]
                    first, last = (kt == 0), (kt == LAST_KT[b_])
                    if first and epi_rel_prev is not None:
                        P.waits("tensor", epi_rel_prev.get("r", []))
                    P.waits("tensor", ready[i])
                    P.do("tensor", lambda E, S, qc0=qc0, n=ncols, e=eb, f=first, l=last:
                         E.matmul(psS[0:1, qc0:qc0 + n], onescol[:], e[:, 0:n], start=f, stop=l))
                    P.do("tensor", lambda E, S, kt=kt, b_=b_, qc0=qc0, n=ncols, e=eb, f=first, l=last, h=h:
                         E.matmul([psE, psF][b_][:, qc0 - b_ * 512:qc0 - b_ * 512 + n],
                                  V[:, kt * T + h * 128:kt * T + (h + 1) * 128], e[:, 0:n], start=f, stop=l), inc="t")

                epi_n = len(pend_epi[0]) if pend_epi else 0
                for i in range(len(ITERS) + 1):
                    if i < len(ITERS):
                        emit_scores(i)
                    if i == 0:
                        while si < epi_n:
                            steps[si]()
                            si += 1
                    if i >= 1:
                        emit_sumpv(i - 1)
                    want = ((i + 1) * len(steps)) // (len(ITERS) + 1)
                    while si < want:
                        steps[si]()
                        si += 1
                while si < len(steps):
                    steps[si]()
                    si += 1
                prep_rel = [("t", P.cnt["t"]), ("v", P.cnt["v"]), ("s", P.cnt["s"])]
                P.wait("vector", "t")
                P.do("vector", lambda E, S: E.tensor_copy(rowA[0:1, 0:NQ], psS[0:1, 0:1024]), inc="v")
                pend_epi = make_epi(h)
            for st_ in pend_epi[0]:
                st_()
            a3.close()
            P.barrier()

        if phase >= 4:
            a1.close()
            a4o = contextlib.ExitStack(); stacks.append(a4o); stacks.append(a4o)
            xqS = sbt(a4o, "xqS", [128, NCH * NQ], BF16)
            wgb = [sbt(a4o, f"wgb{i}", [128, C], BF16) for i in range(2)]
            wub = [sbt(a4o, f"wub{i}", [128, C], BF16) for i in range(2)]
            a4m = contextlib.ExitStack(); stacks.append(a4m); stacks.append(a4m)
            x2S = sbt(a4m, "x2S", [128, NCH * NQ])
            a4 = contextlib.ExitStack(); stacks.append(a4); stacks.append(a4)
            aoutS = sbt(a4, "aoutS", [128, NCH * NQ], BF16)
            wob = [sbt(a4, f"wob{i}", [128, C], BF16) for i in range(2)]
            abf = [sbt(a4, f"abf{i}", [128, NQ], BF16) for i in range(2)]
            asq = [sbt(a4, f"asq{i}", [128, NQ], BF16) for i in range(2)]
            xoc = [sbt(a4, f"xoc{i}", [128, NQ]) for i in range(2)]
            for j in range(NCH):
                P.do("gpsimd", lambda E, S, j=j: E.dma_start(out=aoutS[:, j * NQ:(j + 1) * NQ], in_=aoscr[j * 128:(j + 1) * 128, :]), inc="gin", n=16)
            relw = [[], []]
            relpp = [[("v", P.cnt["v"])], [("v", P.cnt["v"])]]
            pend_wstats = None
            for fc in range(NCH):
                bi = fc % 2
                pp = [psA, psB] if bi == 0 else [psC, psD]
                P.waits("gpsimd", relw[bi])
                P.do("gpsimd", lambda E, S, fc=fc, b=bi: E.dma_start(out=wob[b][:], in_=woT[fc]), inc=f"g{bi}", n=16)
                P.wait("tensor", f"g{bi}")
                P.wait("tensor", "gin")
                P.waits("tensor", relpp[bi])
                for ci in range(NCH):
                    for tb in range(2):
                        P.do("tensor", lambda E, S, ci=ci, tb=tb, b=bi, p=pp[tb]:
                             E.matmul(p[:, :], wob[b][:, ci * 128:(ci + 1) * 128], aoutS[:, ci * NQ + tb * 512:ci * NQ + (tb + 1) * 512],
                                      start=(ci == 0), stop=(ci == NCH - 1)), inc="t" if (ci == NCH - 1 and tb == 1) else None)
                relw[bi] = [("t", P.cnt["t"])]
                if pend_wstats is not None:
                    pend_wstats()
                P.wait("vector", "t")
                for tb in range(2):
                    P.do("vector", lambda E, S, fc=fc, tb=tb, p=pp[tb]:
                         E.tensor_copy(x2S[:, fc * NQ + tb * 512:fc * NQ + (tb + 1) * 512], p[:, :]), inc="v" if tb == 1 else None)
                relpp[bi] = [("v", P.cnt["v"])]
                P.do("vector", lambda E, S, fc=fc, ab=abf[bi]: E.tensor_copy(ab[:], x2S[:, fc * NQ:(fc + 1) * NQ]), inc="v")
                P.wait("scalar", "v")
                P.do("scalar", lambda E, S, ab=abf[bi], aq=asq[bi]: E.activation(aq[:], ab[:], AF.Square), inc="s")

                def mk_wstats(fc=fc, scnt=P.cnt["s"], ab=abf[bi], aq=asq[bi]):
                    def f():
                        P.wait("tensor", "s", scnt)
                        for tb in range(2):
                            P.do("tensor", lambda E, S, tb=tb, tg=[psS[0:1, 0:512], psS[0:1, 512:1024]][tb]:
                                 E.matmul(tg, onescol[:], ab[:, tb * 512:(tb + 1) * 512], start=(fc == 0), stop=(fc == NCH - 1)))
                        for tb in range(2):
                            P.do("tensor", lambda E, S, tb=tb, tg=[psE[0:1, :], psF[0:1, :]][tb]:
                                 E.matmul(tg, onescol[:], aq[:, tb * 512:(tb + 1) * 512], start=(fc == 0), stop=(fc == NCH - 1)),
                                 inc="t" if tb == 1 else None)
                    return f
                pend_wstats = mk_wstats()
            pend_wstats()
            P.wait("vector", "t")
            ln_rows(NQ, 0, [(psS[0:1, 0:512], 0, 512), (psS[0:1, 512:1024], 512, 512)],
                    [(psE[0:1, :], 0, 512), (psF[0:1, :], 512, 512)], rowA, rowB)
            P.wait("tensor", "v")
            bcast(rowA, NQ, [psA, psB])
            bcast(rowB, NQ, [psC, psD])
            P.wait("vector", "t")
            relx = [[], []]
            stA = [0, 0]
            for j in range(NCH):
                bi = j % 2
                P.waits("sync", relx[bi])
                P.do("sync", lambda E, S, j=j, b=bi: E.dma_start(out=xoc[b][:], in_=xown[j * 128:(j + 1) * 128, :]), inc=f"xb{bi}", n=16)
                P.wait("vector", f"xb{bi}")
                for tb in range(2):
                    sl = slice(j * NQ + tb * 512, j * NQ + (tb + 1) * 512)
                    P.do("vector", lambda E, S, sl=sl, p=[psA, psB][tb]: E.tensor_mul(x2S[:, sl], x2S[:, sl], p[:, :]))
                    P.do("vector", lambda E, S, sl=sl, p=[psC, psD][tb]: E.tensor_add(x2S[:, sl], x2S[:, sl], p[:, :]))
                    P.do("vector", lambda E, S, sl=sl, b=bi, tb=tb:
                         E.tensor_add(x2S[:, sl], x2S[:, sl], xoc[b][:, tb * 512:(tb + 1) * 512]), inc="v")
                relx[bi] = [("v", P.cnt["v"])]
                P.wait("sync", "v")
                P.do("sync", lambda E, S, j=j: E.dma_start(out=x2scr[j * 128:(j + 1) * 128, :], in_=x2S[:, j * NQ:(j + 1) * NQ]), inc="sx2", n=16)
                P.wait("vector", "t", stA[bi])
                P.do("vector", lambda E, S, j=j, ab=abf[bi]: E.tensor_copy(ab[:], x2S[:, j * NQ:(j + 1) * NQ]), inc="v")
                P.wait("scalar", "v")
                P.do("scalar", lambda E, S, ab=abf[bi], aq=asq[bi]: E.activation(aq[:], ab[:], AF.Square), inc="s")
                P.wait("tensor", "s")
                for tb in range(2):
                    P.do("tensor", lambda E, S, j=j, tb=tb, ab=abf[bi], tg=[psS[0:1, 0:512], psS[0:1, 512:1024]][tb]:
                         E.matmul(tg, onescol[:], ab[:, tb * 512:(tb + 1) * 512], start=(j == 0), stop=(j == NCH - 1)))
                for tb in range(2):
                    P.do("tensor", lambda E, S, j=j, tb=tb, aq=asq[bi], tg=[psE[0:1, :], psF[0:1, :]][tb]:
                         E.matmul(tg, onescol[:], aq[:, tb * 512:(tb + 1) * 512], start=(j == 0), stop=(j == NCH - 1)),
                         inc="t" if tb == 1 else None)
                stA[bi] = P.cnt["t"]
            a4.close()
            P.barrier()
            if phase == 4:
                for j in range(NCH):
                    P.do("gpsimd", lambda E, S, j=j: E.dma_start(out=dbg[j * 128:(j + 1) * 128, 0:NQ], in_=x2S[:, j * NQ:(j + 1) * NQ]), inc="gin", n=16)

        if phase >= 5:
            a5 = contextlib.ExitStack(); stacks.append(a5); stacks.append(a5)
            RB2 = sbt(a5, "RB2", [128, 2 * NQ])
            Abc = sbt(a5, "Abc", [128, NQ])
            yabs = sbt(a5, "yabs", [128, NQ])
            macc = sbt(a5, "macc", [128, NQ])
            tq = sbt(a5, "tq", [128, NQ])
            cf = sbt(a5, "cf", [128, NQ])
            b1t = sbt(a5, "b1t", [128, NQ])
            xi = sbt(a5, "xi", [128, NQ], I32)
            tcol = sbt(a5, "tcol", [128, 8])
            r3r = sbt(a5, "r3r", [1, NQ])

            gu_relw = [[], []]
            gu_sg = [(0, 0, 0)] * HC
            gu_spill = [0] * HC

            def gu_load_sign(hc, from_scratch):
                bi = hc % 2
                P.waits("gpsimd", gu_relw[bi])
                if not from_scratch:
                    P.do("gpsimd", lambda E, S, hc=hc, b=bi: E.dma_start(out=wgb[b][:], in_=wgT[hc]), inc=f"g{bi}", n=16)
                    P.do("gpsimd", lambda E, S, hc=hc, b=bi: E.dma_start(out=wub[b][:], in_=wuT[hc]), inc=f"g{bi}", n=16)
                    P.wait("scalar", f"g{bi}")
                    P.do("scalar", lambda E, S, b=bi: E.sign(wgb[b][:], wgb[b][:]))
                    P.do("scalar", lambda E, S, b=bi: E.sign(wub[b][:], wub[b][:]), inc="s")
                    P.wait("sync", "s")
                    P.do("sync", lambda E, S, hc=hc, b=bi: E.dma_start(out=sgg[hc], in_=wgb[b][:]), inc=f"sw{bi}", n=16)
                    P.do("sync", lambda E, S, hc=hc, b=bi: E.dma_start(out=sgu[hc], in_=wub[b][:]), inc=f"sw{bi}", n=16)
                    gu_spill[hc] = P.cnt[f"sw{bi}"]
                else:
                    P.wait("gpsimd", f"sw{bi}", gu_spill[hc])
                    P.do("gpsimd", lambda E, S, hc=hc, b=bi: E.dma_start(out=wgb[b][:], in_=sgg[hc]), inc=f"g{bi}", n=16)
                    P.do("gpsimd", lambda E, S, hc=hc, b=bi: E.dma_start(out=wub[b][:], in_=sgu[hc]), inc=f"g{bi}", n=16)
                gu_sg[hc] = (P.cnt["s"], P.cnt[f"g{bi}"], P.cnt[f"sw{bi}"])

            gu_load_sign(0, False)
            gu_load_sign(1, False)

            P.wait("vector", "t")
            ln_rows(NQ, 0, [(psS[0:1, 0:512], 0, 512), (psS[0:1, 512:1024], 512, 512)],
                    [(psE[0:1, :], 0, 512), (psF[0:1, :], 512, 512)], rowA, rowB)
            P.do("vector", lambda E, S: E.tensor_mul(r3r[0:1, :], rowA[0:1, 0:NQ], rowA[0:1, 0:NQ]))
            P.do("vector", lambda E, S: E.tensor_scalar(r3r[0:1, :], r3r[0:1, :], -LN_EPS, 1.0 + FFN_EPS, ALU.mult, ALU.add), inc="v")
            P.wait("scalar", "v")
            P.do("scalar", lambda E, S: E.activation(r3r[0:1, :], r3r[0:1, :], AF.Sqrt), inc="s")
            P.wait("vector", "s")
            P.do("vector", lambda E, S: E.reciprocal(r3r[0:1, :], r3r[0:1, :]), inc="v")
            P.wait("tensor", "v")
            bcast(rowA, NQ, [psA, psB])
            bcast(rowB, NQ, [psC, psD])
            P.wait("vector", "t")
            for tb in range(2):
                P.do("vector", lambda E, S, tb=tb, p=[psA, psB][tb]: E.tensor_copy(RB2[:, tb * 512:(tb + 1) * 512], p[:, :]))
                P.do("vector", lambda E, S, tb=tb, p=[psC, psD][tb]: E.tensor_copy(RB2[:, NQ + tb * 512:NQ + (tb + 1) * 512], p[:, :]), inc="v" if tb == 1 else None)

            P.wait("vector", "sx2")
            for j in range(NCH):
                P.do("vector", lambda E, S, j=j: E.tensor_mul(x2S[:, j * NQ:(j + 1) * NQ], x2S[:, j * NQ:(j + 1) * NQ], RB2[:, 0:NQ]))
                P.do("vector", lambda E, S, j=j: E.tensor_add(x2S[:, j * NQ:(j + 1) * NQ], x2S[:, j * NQ:(j + 1) * NQ], RB2[:, NQ:2 * NQ]), inc="v")
                P.wait("scalar", "v")
                P.do("scalar", lambda E, S, j=j: E.activation(yabs[:], x2S[:, j * NQ:(j + 1) * NQ], AF.Abs), inc="s")
                P.wait("vector", "s")
                if j == 0:
                    P.do("vector", lambda E, S: E.tensor_copy(macc[:], yabs[:]), inc="v")
                else:
                    P.do("vector", lambda E, S: E.tensor_max(macc[:], macc[:], yabs[:]), inc="v")
            P.wait("tensor", "v")
            for jb in range(8):
                p = [psA, psB][jb % 2]
                P.wait("tensor", "v", P.cnt["v"])
                P.do("tensor", lambda E, S, jb=jb, p=p:
                     E.transpose(p[:, 0:128], macc[:, jb * 128:(jb + 1) * 128], ident[:]), inc="t")
                P.wait("vector", "t")
                P.do("vector", lambda E, S, jb=jb, p=p:
                     E.reduce_max(tcol[:, jb:jb + 1], p[:, 0:128], axis=mybir.AxisListType.X), inc="v")
            P.wait("tensor", "v")
            for jb in range(8):
                p = [psC, psD][jb % 2]
                P.wait("tensor", "v", P.cnt["v"])
                P.do("tensor", lambda E, S, jb=jb, p=p:
                     E.matmul(p[0:1, 0:128], tcol[:, jb:jb + 1], ident[:], start=True, stop=True), inc="t")
                P.wait("vector", "t")
                P.do("vector", lambda E, S, jb=jb, p=p:
                     E.tensor_copy(rowB[0:1, jb * 128:(jb + 1) * 128], p[0:1, 0:128]), inc="v")
            P.do("vector", lambda E, S: E.tensor_mul(rowA[0:1, 0:NQ], r3r[0:1, :], rowB[0:1, 0:NQ]))
            P.do("vector", lambda E, S: E.tensor_scalar_mul(rowA[0:1, 0:NQ], rowA[0:1, 0:NQ], 1.0 / 127.0))
            P.do("vector", lambda E, S: E.reciprocal(rowB[0:1, 0:NQ], rowB[0:1, 0:NQ]))
            P.do("vector", lambda E, S: E.tensor_scalar_mul(rowB[0:1, 0:NQ], rowB[0:1, 0:NQ], 127.0), inc="v")
            P.wait("tensor", "v")
            bcast(rowB, NQ, [psA, psB])
            bcast(rowA, NQ, [psC, psD])
            P.wait("vector", "t")
            for tb in range(2):
                P.do("vector", lambda E, S, tb=tb, p=[psA, psB][tb]: E.tensor_copy(Abc[:, tb * 512:(tb + 1) * 512], p[:, :]))
                P.do("vector", lambda E, S, tb=tb, p=[psC, psD][tb]: E.tensor_copy(invsbc[:, tb * 512:(tb + 1) * 512], p[:, :]), inc="v" if tb == 1 else None)
            for j in range(NCH):
                P.do("vector", lambda E, S, j=j: E.tensor_mul(tq[:], x2S[:, j * NQ:(j + 1) * NQ], Abc[:]), inc="v")
                P.wait("scalar", "v")
                P.do("scalar", lambda E, S: E.activation(b1t[:], tq[:], AF.Sign), inc="s")
                P.wait("vector", "s")
                P.do("vector", lambda E, S: E.scalar_tensor_tensor(cf[:], b1t[:], 0.5, tq[:], ALU.mult, ALU.add), inc="v")
                P.wait("scalar", "v")
                P.do("scalar", lambda E, S: E.copy(xi[:], cf[:]))
                P.do("scalar", lambda E, S: E.copy(cf[:], xi[:]), inc="s")
                P.wait("vector", "s")
                P.do("vector", lambda E, S, j=j: E.tensor_scalar(xqS[:, j * NQ:(j + 1) * NQ], cf[:], -128.0, 127.0, ALU.max, ALU.min), inc="v")
            a5.close()
            a4m.close()
            P.barrier()
            if phase == 5:
                for j in range(NCH):
                    P.do("gpsimd", lambda E, S, j=j: E.dma_start(out=dbg[j * 128:(j + 1) * 128, 0:NQ], in_=xqS[:, j * NQ:(j + 1) * NQ]), inc="gin", n=16)
                P.do("gpsimd", lambda E, S: E.dma_start(out=dbg[0:128, NQ:2 * NQ], in_=invsbc[:]), inc="gin", n=16)

        if phase >= 6:
            a6 = contextlib.ExitStack(); stacks.append(a6); stacks.append(a6)
            pS = sbt(a6, "pS", [128, HC * 512], BF16)
            gt0 = sbt(a6, "gt0", [128, 512])
            gt1 = sbt(a6, "gt1", [128, 512])
            ffS = sbt(a6, "ffS", [128, NCH * 512])
            fbf = [sbt(a6, f"fbf{i}", [128, 512], BF16) for i in range(2)]
            fsq = [sbt(a6, f"fsq{i}", [128, 512], BF16) for i in range(2)]
            wdb = sbt(a6, "wdb", [128, 8192], BF16)
            x2c = [sbt(a6, f"x2c{i}", [128, 512]) for i in range(2)]
            ob = [sbt(a6, f"ob{i}", [128, 512]) for i in range(2)]
            RB4 = sbt(a6, "RB4", [128, 1024])
            wd_relw = [[], []]
            wd_spill = [0] * (NCH * 2)
            ob_rel = [[], []]
            relgu_carry = None
            for tb in range(2):
                tsl = slice(tb * 512, (tb + 1) * 512)
                if tb == 0:
                    relgu = [[("v", P.cnt["v"])], [("v", P.cnt["v"])]]
                else:
                    relgu = relgu_carry
                    gu_load_sign(0, True)
                    gu_load_sign(1, True)
                for hc in range(HC):
                    bi = hc % 2
                    ppG = [psA, psC][bi]
                    ppU = [psB, psD][bi]
                    if tb == 0:
                        P.wait("tensor", "s", gu_sg[hc][0])
                    else:
                        P.wait("tensor", f"g{bi}", gu_sg[hc][1])
                    P.waits("tensor", relgu[bi])
                    for ci in range(NCH):
                        P.do("tensor", lambda E, S, ci=ci, b=bi, p=ppG, tb=tb:
                             E.matmul(p[:, :], wgb[b][:, ci * 128:(ci + 1) * 128], xqS[:, ci * NQ + tb * 512:ci * NQ + (tb + 1) * 512],
                                      start=(ci == 0), stop=(ci == NCH - 1)))
                        P.do("tensor", lambda E, S, ci=ci, b=bi, p=ppU, tb=tb:
                             E.matmul(p[:, :], wub[b][:, ci * 128:(ci + 1) * 128], xqS[:, ci * NQ + tb * 512:ci * NQ + (tb + 1) * 512],
                                      start=(ci == 0), stop=(ci == NCH - 1)), inc="t" if ci == NCH - 1 else None)
                    gu_relw[bi] = [("t", P.cnt["t"])] + ([(f"sw{bi}", gu_sg[hc][2])] if tb == 0 else [])
                    if hc + 2 < HC:
                        gu_load_sign(hc + 2, tb == 1)
                    P.wait("vector", "t")
                    P.do("vector", lambda E, S, p=ppG, tsl=tsl: E.tensor_mul(gt0[:], p[:, :], invsbc[:, tsl]), inc="v")
                    P.wait("scalar", "v")
                    P.do("scalar", lambda E, S: E.activation(gt1[:], gt0[:], AF.Silu), inc="s")
                    P.wait("vector", "s")
                    P.do("vector", lambda E, S, p=ppU: E.tensor_mul(gt0[:], gt1[:], p[:, :]))
                    P.do("vector", lambda E, S, hc=hc, tsl=tsl: E.tensor_mul(pS[:, hc * 512:(hc + 1) * 512], gt0[:], invsbc[:, tsl]), inc="v")
                    relgu[bi] = [("v", P.cnt["v"])]
                relgu_carry = [list(relgu[0]), list(relgu[1])]
                relpw = [[("v", P.cnt["v"])], [("v", P.cnt["v"])]]
                sgdone = [(0, 0)] * 2
                pend_stats = None
                for fc in range(NCH):
                    pw = [psE, psF][fc % 2]
                    for hf in range(2):
                        P.waits("gpsimd", wd_relw[hf])
                        if tb == 0:
                            P.do("gpsimd", lambda E, S, fc=fc, hf=hf:
                                 E.dma_start(out=wdb[:, hf * 4096:(hf + 1) * 4096], in_=wdT[fc, :, hf * 4096:(hf + 1) * 4096]), inc=f"g{hf}", n=16)
                            P.wait("scalar", f"g{hf}")
                            P.do("scalar", lambda E, S, hf=hf: E.sign(wdb[:, hf * 4096:(hf + 1) * 4096], wdb[:, hf * 4096:(hf + 1) * 4096]), inc="s")
                            P.wait("sync", "s")
                            P.do("sync", lambda E, S, fc=fc, hf=hf:
                                 E.dma_start(out=sgd[fc, :, hf * 4096:(hf + 1) * 4096], in_=wdb[:, hf * 4096:(hf + 1) * 4096]), inc=f"sw{hf}", n=16)
                            wd_spill[fc * 2 + hf] = P.cnt[f"sw{hf}"]
                        else:
                            P.wait("gpsimd", f"sw{hf}", wd_spill[fc * 2 + hf])
                            P.do("gpsimd", lambda E, S, fc=fc, hf=hf:
                                 E.dma_start(out=wdb[:, hf * 4096:(hf + 1) * 4096], in_=sgd[fc, :, hf * 4096:(hf + 1) * 4096]), inc=f"g{hf}", n=16)
                        sgdone[hf] = (P.cnt["s"], P.cnt[f"g{hf}"])
                    P.waits("tensor", relpw[fc % 2])
                    for hc in range(HC):
                        if hc == 0:
                            P.wait("tensor", "s" if tb == 0 else "g0", sgdone[0][0 if tb == 0 else 1])
                        if hc == HC // 2:
                            P.wait("tensor", "s" if tb == 0 else "g1", sgdone[1][0 if tb == 0 else 1])
                        P.do("tensor", lambda E, S, hc=hc, p=pw:
                             E.matmul(p[:, :], wdb[:, hc * 128:(hc + 1) * 128], pS[:, hc * 512:(hc + 1) * 512],
                                      start=(hc == 0), stop=(hc == HC - 1)), inc="t" if hc in (HC // 2 - 1, HC - 1) else None)
                        if hc == HC // 2 - 1:
                            wd_relw[0] = [("t", P.cnt["t"])] + ([("sw0", wd_spill[fc * 2])] if tb == 0 else [])
                    wd_relw[1] = [("t", P.cnt["t"])] + ([("sw1", wd_spill[fc * 2 + 1])] if tb == 0 else [])
                    if pend_stats is not None:
                        pend_stats()
                    P.wait("vector", "t")
                    P.do("vector", lambda E, S, fc=fc, p=pw: E.tensor_copy(ffS[:, fc * 512:(fc + 1) * 512], p[:, :]))
                    P.do("vector", lambda E, S, fc=fc, fb=fbf[fc % 2]: E.tensor_copy(fb[:], ffS[:, fc * 512:(fc + 1) * 512]), inc="v")
                    relpw[fc % 2] = [("v", P.cnt["v"])]
                    P.wait("scalar", "v")
                    P.do("scalar", lambda E, S, fb=fbf[fc % 2], fq=fsq[fc % 2]: E.activation(fq[:], fb[:], AF.Square), inc="s")

                    def mk_stats(fc=fc, scnt=P.cnt["s"], fb=fbf[fc % 2], fq=fsq[fc % 2]):
                        def f():
                            P.wait("tensor", "s", scnt)
                            P.do("tensor", lambda E, S:
                                 E.matmul(psS[0:1, 0:512], onescol[:], fb[:], start=(fc == 0), stop=(fc == NCH - 1)))
                            P.do("tensor", lambda E, S:
                                 E.matmul(psS[0:1, 512:1024], onescol[:], fq[:], start=(fc == 0), stop=(fc == NCH - 1)),
                                 inc="t")
                        return f
                    pend_stats = mk_stats()
                pend_stats()
                P.wait("vector", "t")
                ln_rows(512, 0, [(psS[0:1, 0:512], 0, 512)], [(psS[0:1, 512:1024], 0, 512)], rowA, rowB)
                P.wait("tensor", "v")
                bcast(rowA, 512, [psE])
                bcast(rowB, 512, [psF])
                P.wait("vector", "t")
                P.do("vector", lambda E, S: E.tensor_copy(RB4[:, 0:512], psE[:, :]))
                P.do("vector", lambda E, S: E.tensor_copy(RB4[:, 512:1024], psF[:, :]), inc="v")
                relx = [[], []]
                for fc in range(NCH):
                    bi = fc % 2
                    P.waits("sync", relx[bi])
                    P.do("sync", lambda E, S, fc=fc, b=bi, tb=tb:
                         E.dma_start(out=x2c[b][:], in_=x2scr[fc * 128:(fc + 1) * 128, tb * 512:(tb + 1) * 512]), inc=f"xb{bi}", n=16)
                    P.wait("vector", f"xb{bi}")
                    P.waits("vector", ob_rel[bi])
                    P.do("vector", lambda E, S, fc=fc: E.tensor_mul(gt0[:], ffS[:, fc * 512:(fc + 1) * 512], RB4[:, 0:512]))
                    P.do("vector", lambda E, S: E.tensor_add(gt0[:], gt0[:], RB4[:, 512:1024]))
                    P.do("vector", lambda E, S, b=bi: E.tensor_add(ob[b][:], gt0[:], x2c[b][:]), inc="v")
                    relx[bi] = [("v", P.cnt["v"])]
                    if bi == 0:
                        P.wait("sync", "v")
                        P.dma("sync", lambda E, S, fc=fc, tb=tb, b=bi:
                              E.dma_start(out=out[fc * 128:(fc + 1) * 128, tb * 512:(tb + 1) * 512], in_=ob[b][:]), "sout")
                        ob_rel[bi] = [("sout", P.cnt["sout"])]
                    else:
                        P.wait("scalar", "v")
                        P.dma("scalar", lambda E, S, fc=fc, tb=tb, b=bi:
                              E.dma_start(out=out[fc * 128:(fc + 1) * 128, tb * 512:(tb + 1) * 512], in_=ob[b][:]), "gout")
                        ob_rel[bi] = [("gout", P.cnt["gout"])]
            a6.close()

        P.barrier()
        for st_ in reversed(stacks):
            st_.close()
        ops = P.ops

        @blk.sync
        def _(E):
            for f in ops["sync"]:
                f(E, sems)

        @blk.gpsimd
        def _(E):
            for f in ops["gpsimd"]:
                f(E, sems)

        @blk.vector
        def _(E):
            for f in ops["vector"]:
                f(E, sems)

        @blk.scalar
        def _(E):
            for f in ops["scalar"]:
                f(E, sems)

        @blk.tensor
        def _(E):
            for f in ops["tensor"]:
                f(E, sems)

    return nc



def _lhst_blocks(W):
    F_, Ci = W.shape
    A = np.asarray(W).reshape(F_ // 128, 128, Ci // 128, 128)
    return np.ascontiguousarray(A.transpose(0, 3, 2, 1).reshape(F_ // 128, 128, Ci))


def _rope_tables():
    import jax
    import jax.numpy as jnp
    with jax.default_device(jax.devices("cpu")[0]):
        inv = 1.0 / 10000.0 ** (jnp.arange(0, 128, 2, dtype=jnp.float32) / 128)
        t = jnp.arange(T, dtype=jnp.float32)
        fr = t[:, None] * inv[None, :]
        sin = np.asarray(jnp.repeat(jnp.sin(fr), 2, axis=-1))
        cos = np.asarray(jnp.repeat(jnp.cos(fr), 2, axis=-1))
    return sin, cos


def _own_positions(p):
    return np.concatenate([np.arange(128 * t, 128 * (t + 1)) for t in OWN[p]])


def _perm_positions(p):
    return np.concatenate([np.arange(128 * t, 128 * (t + 1)) for t in PERM[p]])


def _emask_for(p):
    NEG = np.float32(-1e9)
    tri = np.where(np.arange(128)[:, None] <= np.arange(128)[None, :],
                   np.float32(0.0), NEG).astype(ml_dtypes.bfloat16)
    m = np.zeros((128, max(MASK_SLOTS, 1) * 128), dtype=ml_dtypes.bfloat16)
    for (ks, b_), (o0, L) in MASK_OFF.items():
        nU = PROFS[b_][ks]
        g = PERM[p][ks]
        for s_ in range(L):
            qslot = b_ * 4 + (4 - nU) + s_
            qt = PERM[p][qslot]
            sl = slice((o0 + s_) * 128, (o0 + s_ + 1) * 128)
            if qt < g:
                m[:, sl] = NEG
            elif qt == g:
                m[:, sl] = tri
    return m


def make_inmaps(L0_attn_x, L0_attn_W_qkv, L0_attn_W_o, L0_ffn_W_g, L0_ffn_W_u, L0_ffn_W_d):
    x = np.asarray(L0_attn_x)
    Wqkv = np.asarray(L0_attn_W_qkv)
    shared = {
        "wqT": _lhst_blocks(Wqkv[0:2048]),
        "wkT": _lhst_blocks(Wqkv[2048:4096]),
        "wvT": np.ascontiguousarray(Wqkv[4096:6144].T.reshape(NCH, 128, C)),
        "woT": _lhst_blocks(np.asarray(L0_attn_W_o)),
        "wgT": _lhst_blocks(np.asarray(L0_ffn_W_g)),
        "wuT": _lhst_blocks(np.asarray(L0_ffn_W_u)),
        "wdT": _lhst_blocks(np.asarray(L0_ffn_W_d)),
        "ident": np.eye(128, dtype=np.float32),
    }
    sin, cos = _rope_tables()
    sgn = np.where(np.arange(128) % 2 == 0, -1.0, 1.0).astype(np.float32)
    rkc = np.ascontiguousarray(cos.T)
    rks = np.ascontiguousarray(sin.T * sgn[:, None])
    in_maps = []
    for core in range(8):
        b, p = core // 2, core % 2
        pp = _perm_positions(p)
        xt = np.ascontiguousarray(x[b].T[:, pp])
        d = dict(shared)
        d["xT"] = xt
        d["xown"] = np.ascontiguousarray(xt[:, 0:NQ])
        d["rkc"] = np.ascontiguousarray(rkc[:, pp])
        d["rks"] = np.ascontiguousarray(rks[:, pp])
        d["emask"] = _emask_for(p)
        in_maps.append(d)
    return in_maps


def kernel(L0_attn_x, L0_attn_W_qkv, L0_attn_W_o, L0_ffn_W_g, L0_ffn_W_u, L0_ffn_W_d):
    in_maps = make_inmaps(L0_attn_x, L0_attn_W_qkv, L0_attn_W_o, L0_ffn_W_g, L0_ffn_W_u, L0_ffn_W_d)
    nc = build_nc(phase=6)
    res = run_bass_kernel_spmd(nc, in_maps, core_ids=list(range(8)))
    y = np.empty((4, T, C), dtype=np.float32)
    for core in range(8):
        b, p = core // 2, core % 2
        pos = _own_positions(p)
        y[b, pos, :] = np.asarray(res.results[core]["out"]).T
    return y
